# version 20
# baseline (speedup 1.0000x reference)
import contextlib
import math

import numpy as np
import ml_dtypes

import concourse.bass as bass
import concourse.mybir as mybir
from concourse.bass_utils import run_bass_kernel_spmd

F32 = mybir.dt.float32
BF16 = mybir.dt.bfloat16
I32 = mybir.dt.int32
U8 = mybir.dt.uint8
AF = mybir.ActivationFunctionType
ALU = mybir.AluOpType

TOK = 2048
D = 2048
NT = TOK // 128
DC = D // 128
H = 4
DK, DV = 512, 256
CH = 128
NCH = TOK // CH
SCALE_GLA = float(DK ** -0.5)
HB = 16
HD = 128
FFD = 5632
FC = FFD // 128
ATT_SHIFT = -20.0
EPS = 1e-5


_CHAINED_OPS = {
    "memset", "memzero", "tensor_copy", "tensor_tensor", "tensor_add",
    "tensor_sub", "tensor_mul", "tensor_max", "tensor_scalar",
    "tensor_single_scalar", "tensor_scalar_add", "tensor_scalar_sub",
    "tensor_scalar_mul", "tensor_scalar_max", "tensor_scalar_min",
    "tensor_relu", "tensor_reduce", "reduce_max", "reduce_sum",
    "scalar_tensor_tensor", "tensor_tensor_scan", "reciprocal", "select",
    "copy_predicated", "transpose", "bn_stats", "bn_aggr", "activation",
    "square", "sqrt", "mul", "copy", "add", "sign", "stream_shuffle",
    "pool", "pool_avg", "pool_max", "affine_then_add", "affine_select",
    "tensor_tensor_reduce", "affine_mul_reduce",
}


class _Chained:
    def __init__(self, p, eng, sem):
        self._p, self._eng, self._sem = p, eng, sem

    def __getattr__(self, name):
        f = getattr(self._eng, name)
        if name not in _CHAINED_OPS:
            return f
        p, eng, sem = self._p, self._eng, self._sem

        def wrap(*a, **k):
            p.wait(eng, sem)
            i = f(*a, **k)
            p.inc(i, sem)
            p._chain_last = (i, sem)
            return i
        return wrap


class RInstr:
    __slots__ = ("post",)

    def __init__(self):
        self.post = []

    def then_inc(self, sem, k=1):
        self.post.append((sem, k))
        return self


class REng:

    def __init__(self):
        self.ops = []

    def __getattr__(self, name):
        def call(*a, **k):
            tok = RInstr()
            self.ops.append((name, a, k, tok))
            return tok
        return call


class WBlock:

    _ENGS = ("sync", "tensor", "vector", "scalar", "gpsimd")

    def __init__(self, p, block):
        self._p, self._b = p, block
        self._rec = {e: REng() for e in self._ENGS}

    def _section(self, ename, f, chain_sem):
        rec = self._rec[ename]
        if chain_sem is None:
            f(rec)
        else:
            f(_Chained(self._p, rec, chain_sem))

    def sync(self, f):
        self._section("sync", f, None)

    def tensor(self, f):
        self._section("tensor", f, None)

    def vector(self, f):
        self._section("vector", f, "ve")

    def scalar(self, f):
        self._section("scalar", f, "ac")

    def gpsimd(self, f):
        self._section("gpsimd", f, "ge")

    def flush(self):
        for ename in self._ENGS:
            rec = self._rec[ename]
            if not rec.ops:
                continue
            ops, rec.ops = rec.ops, []
            dec = getattr(self._b, ename)

            def body(eng, ops=ops):
                for name, a, k, tok in ops:
                    i = getattr(eng, name)(*a, **k)
                    for sem, kk in tok.post:
                        i.then_inc(sem, kk)
            dec(body)


class Prog:

    def __init__(self, nc, ctx, sem_names):
        self.nc = nc
        self.ctx = ctx
        self.sem = {}
        self.cnt = {}
        self.mark = {}
        self._chain_last = None
        for n in sem_names:
            self.sem[n] = ctx.enter_context(nc.semaphore(n))
            self.cnt[n] = 0

    def _ensure(self, name):
        if name not in self.sem:
            self.sem[name] = self.ctx.enter_context(self.nc.semaphore(name))
            self.cnt[name] = 0

    def inc(self, instr, name, k=1):
        if (self._chain_last is not None and self._chain_last[0] is instr
                and self._chain_last[1] == name):
            return self.cnt[name]
        self._ensure(name)
        instr.then_inc(self.sem[name], k)
        self.cnt[name] += k
        return self.cnt[name]

    def dma(self, instr, name):
        return self.inc(instr, name, 16)

    def set_mark(self, key, name=None):
        self.mark[key] = (name, self.cnt[name])

    def wait(self, eng, name, val=None):
        self._ensure(name)
        if val is None:
            val = self.cnt[name]
        if val > 0:
            eng.wait_ge(self.sem[name], val)

    def wait_mark(self, eng, key):
        name, val = self.mark[key]
        if val > 0:
            eng.wait_ge(self.sem[name], val)


def barrier(p, block, engs=("sync", "tensor", "vector", "scalar", "gpsimd")):
    full = len(engs) == 5
    dma_streams = [n for n in p.sem
                   if n.split("_")[0] in ("ld", "st", "dg", "ds")] if full else []
    snap = {}
    for e in engs:
        sname = "ph_" + e
        snap[e] = p.cnt[sname] + 1
    own = {"vector": "ve", "scalar": "ac", "gpsimd": "ge", "tensor": "pe"}
    for e in engs:
        dec = getattr(block, e)

        def body(eng, e=e):
            if e == "sync":
                for n in dma_streams:
                    p.wait(eng, n)
            if e in own:
                p.wait(eng, own[e])
            eng.nop().then_inc(p.sem["ph_" + e], 1)
            for o in engs:
                if o != e:
                    eng.wait_ge(p.sem["ph_" + o], snap[o])
        dec(body)
    for e in engs:
        p.cnt["ph_" + e] = snap[e]
    block.flush()


SEMS = [
    "dg", "ds", "pe", "ve", "ac", "ge", "os",
    "ph_sync", "ph_tensor", "ph_vector", "ph_scalar", "ph_gpsimd",
]



def emit_setup_consts(p, block, sb):
    @block.gpsimd
    def _(g):
        i = g.memset(sb["cb"][:, 0:1], EPS)
        p.inc(i, "ge")
        i = g.memset(sb["cb"][:, 1:2], 1.0)
        p.inc(i, "ge")
        i = g.memset(sb["ident_f"][:], 0.0)
        p.inc(i, "ge")
        p.wait(g, "ge")
        i = g.affine_select(out=sb["ident_f"][:], in_=sb["ident_f"][:],
                            compare_op=ALU.not_equal, fill=1.0, base=0,
                            pattern=[[-1, 128]], channel_multiplier=1)
        p.inc(i, "ge")
        i = g.memset(sb["ident_b"][:], 0.0)
        p.inc(i, "ge")
        p.wait(g, "ge")
        i = g.affine_select(out=sb["ident_b"][:], in_=sb["ident_b"][:],
                            compare_op=ALU.not_equal, fill=1.0, base=0,
                            pattern=[[-1, 128]], channel_multiplier=1)
        p.inc(i, "ge")
        i = g.memset(sb["mask"][:], 0.0)
        p.inc(i, "ge")
        p.wait(g, "ge")
        i = g.affine_select(out=sb["mask"][:], in_=sb["mask"][:],
                            compare_op=ALU.is_gt, fill=1.0, base=0,
                            pattern=[[-1, 128]], channel_multiplier=1)
        p.inc(i, "ge")


def emit_feature_vec(p, block, sb, name, src_ap, cols, scale=1.0):
    @block.sync
    def _(s):
        p.wait(s, "ac")
        i = s.dma_start(out=sb["fvstage"][0:cols, :], in_=src_ap)
        p.dma(i, "ds_fv")

    @block.tensor
    def _(t):
        p.wait(t, "ds_fv")
        p.wait(t, "ge")
        p.wait(t, "ac")
        i = t.transpose(sb["fvpsum"][:, 0:cols], sb["fvstage"][0:cols, :],
                        sb["ident_f"][0:cols, 0:cols])
        p.inc(i, "pe")

    @block.scalar
    def _(sc):
        p.wait(sc, "pe")
        i = sc.activation(sb[name][:], sb["fvpsum"][:, 0:cols], AF.Copy,
                          scale=float(scale))
        p.inc(i, "ac")


def emit_ln_phase(p, block, sb, x_src, xn_dst, ntiles=NT, dep=None, queue="sync"):
    qdec = getattr(block, queue)
    for t in range(ntiles):
        buf = t % 2
        @qdec
        def _(s, t=t, buf=buf):
            if dep is not None:
                p.wait_mark(s, dep(t))
            if t >= 2:
                p.wait_mark(s, ("lnxn", t - 2))
            i = s.dma_start(out=sb["lnx"][buf][:], in_=x_src[t * 128:(t + 1) * 128, :])
            p.dma(i, "ld_lnx%d" % buf)
            p.set_mark(("lnld", t), "ld_lnx%d" % buf)

        @block.scalar
        def _(sc, t=t, buf=buf):
            p.wait_mark(sc, ("lnld", t))
            if t >= 2:
                p.wait_mark(sc, ("lnvar", t - 2))
            i = sc.activation(sb["lnscr"][:], sb["lnx"][buf][:], AF.Square,
                              accum_out=sb["lnss"][buf][:])
            p.inc(i, "ac")
            p.set_mark(("lnsq", t), "ac")

        @block.vector
        def _(v, t=t, buf=buf):
            p.wait_mark(v, ("lnld", t))
            v.reduce_sum(sb["lnsum"][buf][:], sb["lnx"][buf][:],
                         axis=mybir.AxisListType.X)
            v.tensor_scalar_mul(sb["lnm"][buf][:], sb["lnsum"][buf][:], 1.0 / D)
            v.tensor_mul(sb["lnm2"][buf][:], sb["lnm"][buf][:], sb["lnm"][buf][:])
            p.wait_mark(v, ("lnsq", t))
            i = v.tensor_scalar(sb["lnvar"][buf][:], sb["lnss"][buf][:],
                                1.0 / D, None, ALU.mult)
            i = v.tensor_sub(sb["lnvar"][buf][:], sb["lnvar"][buf][:], sb["lnm2"][buf][:])
            p.inc(i, "ve")
            p.set_mark(("lnvar", t), "ve")

        @block.scalar
        def _(sc, t=t, buf=buf):
            p.wait_mark(sc, ("lnvar", t))
            if t >= 2:
                p.wait_mark(sc, ("lnxn", t - 2))
            sc.activation(sb["lnrs"][buf][:], sb["lnvar"][buf][:], AF.Ln, bias=sb["cb"][:, 0:1])
            i = sc.activation(sb["lnrs"][buf][:], sb["lnrs"][buf][:], AF.Exp, scale=-0.5)
            p.inc(i, "ac")
            p.set_mark(("lnrs", t), "ac")

        @block.vector
        def _(v, t=t, buf=buf):
            p.wait_mark(v, ("lnrs", t))
            if t >= 2:
                p.wait_mark(v, ("lnst", t - 2))
            i = v.tensor_scalar(sb["lnxn"][buf][:], sb["lnx"][buf][:],
                                sb["lnm"][buf][:], sb["lnrs"][buf][:],
                                ALU.subtract, ALU.mult)
            p.inc(i, "ve")
            p.set_mark(("lnxn", t), "ve")

        @qdec
        def _(s, t=t, buf=buf):
            p.wait_mark(s, ("lnxn", t))
            i = s.dma_start(out=xn_dst[t * 128:(t + 1) * 128, :], in_=sb["lnxn"][buf][:])
            p.dma(i, "st_lnxn%d" % buf)
            p.set_mark(("lnst", t), "st_lnxn%d" % buf)


def emit_load_xnT(p, block, sb, xn_dram, dst_key="xnT"):
    @block.scalar
    def _(s):
        if ("lnst", NT // 2 - 2) in p.mark:
            p.wait_mark(s, ("lnst", NT // 2 - 2))
            p.wait_mark(s, ("lnst", NT // 2 - 1))
        for dc in range(DC):
            i = s.dma_start_transpose(
                sb[dst_key][:, dc, 0:TOK // 2],
                xn_dram[0:TOK // 2, dc * 128:(dc + 1) * 128])
            p.dma(i, "ds_xnt")
        p.wait(s, "st_lnxn0")
        p.wait(s, "st_lnxn1")
        for dc in range(DC):
            i = s.dma_start_transpose(
                sb[dst_key][:, dc, TOK // 2:TOK],
                xn_dram[TOK // 2:TOK, dc * 128:(dc + 1) * 128])
            p.dma(i, "ds_xnt")
    p.set_mark("xnT_ready", "ds_xnt")



def build_launch1():
    nc = bass.Bass("TRN2", target_bir_lowering=False)
    xsh = nc.dram_tensor("xsh", [TOK, D], F32, kind="ExternalInput").ap()
    Wq = nc.dram_tensor("Wq", [16, 128, 16, 128], F32, kind="ExternalInput").ap()
    Wk = nc.dram_tensor("Wk", [16, 128, 16, 128], F32, kind="ExternalInput").ap()
    Wv = nc.dram_tensor("Wv", [128, 16, 1024], F32, kind="ExternalInput").ap()
    Wg = nc.dram_tensor("Wg", [128, 16, 1024], F32, kind="ExternalInput").ap()
    Wgk1 = nc.dram_tensor("Wgk1", [128, 16, 16], F32, kind="ExternalInput").ap()
    Wgk2 = nc.dram_tensor("Wgk2", [16, 2048], F32, kind="ExternalInput").ap()
    bgk2r = nc.dram_tensor("bgk2r", [16, 128], F32, kind="ExternalInput").ap()

    oacc_o = nc.dram_tensor("oacc", [TOK, 1024], F32, kind="ExternalOutput").ap()
    qgt_o = nc.dram_tensor("qgt", [2048, TOK], BF16, kind="ExternalOutput").ap()
    gtok_o = nc.dram_tensor("gtok", [TOK, 1024], BF16, kind="ExternalOutput").ap()
    sloc_o = nc.dram_tensor("sloc", [16, 128, 256], F32, kind="ExternalOutput").ap()
    dtot_o = nc.dram_tensor("dtot", [128, 16], F32, kind="ExternalOutput").ap()

    xn_d = nc.dram_tensor("xn_d", [TOK, D], BF16).ap()
    qet_d = nc.dram_tensor("qet_d", [2048, TOK], BF16).ap()
    kdt_d = nc.dram_tensor("kdt_d", [2048, TOK], BF16).ap()
    kbt_d = nc.dram_tensor("kbt_d", [2048, TOK], BF16).ap()
    v_d = nc.dram_tensor("v_d", [TOK, 1024], BF16).ap()

    with contextlib.ExitStack() as ctx:
        E = ctx.enter_context
        p = Prog(nc, ctx, SEMS)
        sb = {}
        sb["cb"] = E(nc.sbuf_tensor("cb", [128, 2], F32))
        sb["ident_f"] = E(nc.sbuf_tensor("ident_f", [128, 128], F32))
        sb["ident_b"] = E(nc.sbuf_tensor("ident_b", [128, 128], BF16))
        sb["mask"] = E(nc.sbuf_tensor("mask", [128, 128], BF16))
        sb["fvstage"] = E(nc.sbuf_tensor("fvstage", [128, 128], F32))
        sb["negb"] = E(nc.sbuf_tensor("negb", [128, 16], F32))
        sb["DecL"] = E(nc.sbuf_tensor("DecL", [128, 16, NCH], F32))
        sb["Dtot"] = E(nc.sbuf_tensor("Dtot", [128, 16], F32))

        block = WBlock(p, E(nc.Block()))

        emit_setup_consts(p, block, sb)
        with nc.psum_tensor("fvpsum", [128, 128], F32) as fvps:
            sb["fvpsum"] = fvps
            emit_feature_vec(p, block, sb, "negb", bgk2r, 16, scale=-1.0)
            barrier(p, block)

            with contextlib.ExitStack() as lctx:
                L = lctx.enter_context
                sb["lnx"] = [L(nc.sbuf_tensor(f"lnx{i}", [128, D], F32)) for i in range(2)]
                sb["lnxn"] = [L(nc.sbuf_tensor(f"lnxn{i}", [128, D], BF16)) for i in range(2)]
                sb["lnscr"] = L(nc.sbuf_tensor("lnscr", [128, D], F32))
                for nm in ("lnss", "lnsum", "lnm", "lnm2", "lnvar", "lnrs"):
                    sb[nm] = [L(nc.sbuf_tensor(f"{nm}{i}", [128, 1], F32)) for i in range(2)]
                emit_ln_phase(p, block, sb, xsh, xn_d)
                barrier(p, block)

            sb["xnT"] = ctx2 = nc.sbuf_tensor("xnT", [128, DC, TOK], BF16)
            sb["xnT"] = sb["xnT"].__enter__()
            emit_load_xnT(p, block, sb, xn_d)
            barrier(p, block)

        with contextlib.ExitStack() as lctx:
            L = lctx.enter_context
            wgk1b = L(nc.sbuf_tensor("wgk1b", [128, DC, 16], BF16))
            wgk2b = L(nc.sbuf_tensor("wgk2b", [16, 2048], BF16))
            gk1t = L(nc.sbuf_tensor("gk1t", [16, TOK], BF16))
            wq_t = [L(nc.sbuf_tensor(f"wq_t{i}", [128, DC, 128], BF16)) for i in range(2)]
            wk_t = [L(nc.sbuf_tensor(f"wk_t{i}", [128, DC, 128], BF16)) for i in range(2)]
            tA = L(nc.sbuf_tensor("tA", [128, TOK], F32))
            Pb = L(nc.sbuf_tensor("Pb", [128, TOK], F32))
            Plast = L(nc.sbuf_tensor("Plast", [128, NCH], F32))
            PlastS = L(nc.sbuf_tensor("PlastS", [128, NCH], F32))
            Gx = L(nc.sbuf_tensor("Gx", [128, NCH + 1], F32))
            EG = L(nc.sbuf_tensor("EG", [128, NCH], F32))
            Ptot = L(nc.sbuf_tensor("Ptot", [128, 1], F32))
            E1 = L(nc.sbuf_tensor("E1", [128, TOK], BF16))
            E2 = L(nc.sbuf_tensor("E2", [128, TOK], BF16))
            E3 = L(nc.sbuf_tensor("E3", [128, TOK], BF16))
            qet_s = [L(nc.sbuf_tensor(f"qet_s{i}", [128, TOK], BF16)) for i in range(2)]
            qgt_s = [L(nc.sbuf_tensor(f"qgt_s{i}", [128, TOK], BF16)) for i in range(2)]
            kdt_s = [L(nc.sbuf_tensor(f"kdt_s{i}", [128, TOK], BF16)) for i in range(2)]
            kbt_s = [L(nc.sbuf_tensor(f"kbt_s{i}", [128, TOK], BF16)) for i in range(2)]
            with nc.psum_tensor("psGK", [16, TOK], F32) as psGK:
                @block.gpsimd
                def _(g):
                    i = g.dma_start(out=wgk1b[:], in_=Wgk1)
                    p.dma(i, "dg_gkw")
                    i = g.dma_start(out=wgk2b[:], in_=Wgk2)
                    p.dma(i, "dg_gkw")
                    p.set_mark("gkw", "dg_gkw")

                @block.tensor
                def _(t):
                    p.wait_mark(t, "gkw")
                    p.wait_mark(t, "xnT_ready")
                    for tt in range(4):
                        for dc in range(DC):
                            i = t.matmul(psGK[:, tt * 512:(tt + 1) * 512],
                                         wgk1b[:, dc, :],
                                         sb["xnT"][:, dc, tt * 512:(tt + 1) * 512],
                                         start=(dc == 0), stop=(dc == DC - 1))
                    p.inc(i, "pe")
                    p.set_mark("gk1t_mm", "pe")

                @block.vector
                def _(v):
                    p.wait_mark(v, "gk1t_mm")
                    i = v.tensor_copy(gk1t[:], psGK[:])
                    p.inc(i, "ve")
                    p.set_mark("gk1t", "ve")
                barrier(p, block, engs=("tensor", "vector"))

            psA = L(nc.psum_tensor("psA", [128, TOK], F32))
            psB = L(nc.psum_tensor("psB", [128, TOK], F32))

            for fc in range(16):
                buf = fc % 2
                @block.gpsimd
                def _(g, fc=fc, buf=buf):
                    if fc >= 2:
                        p.wait_mark(g, ("qk_mm", fc - 2))
                    i = g.dma_start(out=wq_t[buf][:], in_=Wq[fc])
                    p.dma(i, "dg_wqk%d" % buf)
                    i = g.dma_start(out=wk_t[buf][:], in_=Wk[fc])
                    p.dma(i, "dg_wqk%d" % buf)
                    p.set_mark(("wqk", fc), "dg_wqk%d" % buf)

                @block.tensor
                def _(t, fc=fc):
                    p.wait_mark(t, "gk1t")
                    if fc >= 1:
                        p.wait_mark(t, ("kproj_done_v", fc - 1))
                    for tt in range(4):
                        i = t.matmul(psA[:, tt * 512:(tt + 1) * 512],
                                     wgk2b[:, fc * 128:(fc + 1) * 128],
                                     gk1t[:, tt * 512:(tt + 1) * 512],
                                     start=True, stop=True)
                    p.inc(i, "pe")
                    p.set_mark(("gkpre", fc), "pe")

                @block.scalar
                def _(sc, fc=fc):
                    p.wait_mark(sc, ("gkpre", fc))
                    if fc >= 1:
                        p.wait_mark(sc, ("cum", fc - 1))
                    sc.activation(tA[:], psA[:], AF.Exp, scale=-1.0,
                                  bias=sb["negb"][:, fc:fc + 1])
                    i = sc.activation(tA[:], tA[:], AF.Ln, bias=sb["cb"][:, 1:2])
                    p.inc(i, "ac")
                    p.set_mark(("sp", fc), "ac")

                @block.vector
                def _(v, fc=fc):
                    p.wait_mark(v, ("sp", fc))
                    if fc >= 1:
                        p.wait_mark(v, ("emul", fc - 1))
                    for c in range(NCH):
                        v.tensor_tensor_scan(Pb[:, c * CH:(c + 1) * CH],
                                             tA[:, c * CH:(c + 1) * CH],
                                             tA[:, c * CH:(c + 1) * CH],
                                             0.0, ALU.add, ALU.bypass)
                    v.tensor_copy(Plast[:], Pb[:, CH - 1::CH])
                    v.tensor_scalar_mul(PlastS[:], Plast[:], -1.0 / 16.0)
                    v.memset(Gx[:, 0:1], 0.0)
                    v.tensor_tensor_scan(Gx[:, 1:NCH + 1], Plast[:], Plast[:],
                                         0.0, ALU.add, ALU.bypass)
                    i = v.reduce_sum(Ptot[:], Plast[:], axis=mybir.AxisListType.X)
                    p.inc(i, "ve")
                    p.set_mark(("cum", fc), "ve")

                @block.scalar
                def _(sc, fc=fc):
                    p.wait_mark(sc, ("cum", fc))
                    if fc >= 1:
                        p.wait_mark(sc, ("qgt_done", fc - 1))
                        p.wait_mark(sc, ("kbt_done", fc - 1))
                    sc.activation(E1[:], Pb[:], AF.Exp, scale=-1.0 / 16.0)
                    sc.activation(E2[:], Pb[:], AF.Exp, scale=1.0 / 16.0)
                    for c in range(NCH):
                        sc.activation(E3[:, c * CH:(c + 1) * CH],
                                      Pb[:, c * CH:(c + 1) * CH], AF.Exp,
                                      scale=1.0 / 16.0, bias=PlastS[:, c:c + 1])
                    sc.activation(EG[:], Gx[:, 0:NCH], AF.Exp, scale=-1.0 / 16.0)
                    sc.activation(sb["DecL"][:, fc, :], Plast[:], AF.Exp, scale=-1.0 / 16.0)
                    i = sc.activation(sb["Dtot"][:, fc:fc + 1], Ptot[:], AF.Exp, scale=-1.0 / 16.0)
                    p.inc(i, "ac")
                    p.set_mark(("emul", fc), "ac")

                @block.tensor
                def _(t, fc=fc, buf=buf):
                    p.wait_mark(t, ("wqk", fc))
                    if fc >= 1:
                        p.wait_mark(t, ("qet_done", fc - 1))
                    for dc in range(DC):
                        for tt in range(4):
                            i = t.matmul(psB[:, tt * 512:(tt + 1) * 512],
                                         wq_t[buf][:, dc, :],
                                         sb["xnT"][:, dc, tt * 512:(tt + 1) * 512],
                                         start=(dc == 0), stop=(dc == DC - 1))
                    p.inc(i, "pe")
                    p.set_mark(("qproj", fc), "pe")

                @block.vector
                def _(v, fc=fc, buf=buf):
                    p.wait_mark(v, ("emul", fc))
                    p.wait_mark(v, ("qproj", fc))
                    if fc >= 2:
                        p.wait_mark(v, ("qst", fc - 2))
                    for tt in range(4):
                        i = v.scalar_tensor_tensor(
                            qet_s[buf][:, tt * 512:(tt + 1) * 512],
                            psB[:, tt * 512:(tt + 1) * 512], SCALE_GLA,
                            E1[:, tt * 512:(tt + 1) * 512], ALU.mult, ALU.mult)
                    p.inc(i, "ve")
                    p.set_mark(("qet_done", fc), "ve")
                    for c in range(NCH):
                        i = v.tensor_scalar_mul(
                            qgt_s[buf][:, c * CH:(c + 1) * CH],
                            qet_s[buf][:, c * CH:(c + 1) * CH],
                            EG[:, c:c + 1])
                    p.inc(i, "ve")
                    p.set_mark(("qgt_done", fc), "ve")

                @block.tensor
                def _(t, fc=fc, buf=buf):
                    p.wait_mark(t, ("sp", fc))
                    p.wait_mark(t, ("wqk", fc))
                    if fc >= 1:
                        p.wait_mark(t, ("kbt_done", fc - 1))
                    for dc in range(DC):
                        for tt in range(4):
                            i = t.matmul(psA[:, tt * 512:(tt + 1) * 512],
                                         wk_t[buf][:, dc, :],
                                         sb["xnT"][:, dc, tt * 512:(tt + 1) * 512],
                                         start=(dc == 0), stop=(dc == DC - 1))
                    p.inc(i, "pe")
                    p.set_mark(("kproj", fc), "pe")

                @block.vector
                def _(v, fc=fc, buf=buf):
                    p.wait_mark(v, ("kproj", fc))
                    if fc >= 2:
                        p.wait_mark(v, ("qst", fc - 2))
                    for tt in range(4):
                        v.tensor_mul(kdt_s[buf][:, tt * 512:(tt + 1) * 512],
                                     psA[:, tt * 512:(tt + 1) * 512],
                                     E2[:, tt * 512:(tt + 1) * 512])
                    for tt in range(4):
                        i = v.tensor_mul(kbt_s[buf][:, tt * 512:(tt + 1) * 512],
                                         psA[:, tt * 512:(tt + 1) * 512],
                                         E3[:, tt * 512:(tt + 1) * 512])
                    p.inc(i, "ve")
                    p.set_mark(("kbt_done", fc), "ve")
                    p.set_mark(("kproj_done_v", fc), "ve")

                @block.sync
                def _(s, fc=fc, buf=buf):
                    p.wait_mark(s, ("qgt_done", fc))
                    i = s.dma_start(out=qet_d[fc * 128:(fc + 1) * 128, :], in_=qet_s[buf][:])
                    p.dma(i, "st_q%d" % buf)
                    i = s.dma_start(out=qgt_o[fc * 128:(fc + 1) * 128, :], in_=qgt_s[buf][:])
                    p.dma(i, "st_q%d" % buf)
                    p.wait_mark(s, ("kbt_done", fc))
                    i = s.dma_start(out=kdt_d[fc * 128:(fc + 1) * 128, :], in_=kdt_s[buf][:])
                    p.dma(i, "st_q%d" % buf)
                    i = s.dma_start(out=kbt_d[fc * 128:(fc + 1) * 128, :], in_=kbt_s[buf][:])
                    p.dma(i, "st_q%d" % buf)
                    p.set_mark(("qst", fc), "st_q%d" % buf)
                    p.set_mark(("qk_mm", fc), "st_q%d" % buf)

            barrier(p, block)

        with contextlib.ExitStack() as lctx:
            L = lctx.enter_context
            wv_b = L(nc.sbuf_tensor("wv_b", [128, DC, 1024], BF16))
            wg_b = L(nc.sbuf_tensor("wg_b", [128, DC, 1024], BF16))
            v_s = [L(nc.sbuf_tensor(f"v_s{i}", [128, 1024], BF16)) for i in range(2)]
            g_s = [L(nc.sbuf_tensor(f"g_s{i}", [128, 1024], BF16)) for i in range(2)]
            psV = [L(nc.psum_tensor(f"psV{i}", [128, 1024], F32)) for i in range(2)]
            psG = [L(nc.psum_tensor(f"psG{i}", [128, 1024], F32)) for i in range(2)]

            @block.gpsimd
            def _(g):
                i = g.dma_start(out=wv_b[:], in_=Wv)
                p.dma(i, "dg_wvg")
                i = g.dma_start(out=wg_b[:], in_=Wg)
                p.dma(i, "dg_wvg")
                p.set_mark("wvg", "dg_wvg")

            for tt in range(NT):
                buf = tt % 2
                @block.tensor
                def _(t, tt=tt, buf=buf):
                    p.wait_mark(t, "wvg")
                    if tt >= 2:
                        p.wait_mark(t, ("vg_evac", tt - 2))
                    for half in range(2):
                        for dc in range(DC):
                            t.matmul(psV[buf][:, half * 512:(half + 1) * 512],
                                     sb["xnT"][:, dc, tt * 128:(tt + 1) * 128],
                                     wv_b[:, dc, half * 512:(half + 1) * 512],
                                     start=(dc == 0), stop=(dc == DC - 1))
                    for half in range(2):
                        for dc in range(DC):
                            i = t.matmul(psG[buf][:, half * 512:(half + 1) * 512],
                                         sb["xnT"][:, dc, tt * 128:(tt + 1) * 128],
                                         wg_b[:, dc, half * 512:(half + 1) * 512],
                                         start=(dc == 0), stop=(dc == DC - 1))
                    p.inc(i, "pe")
                    p.set_mark(("vg_mm", tt), "pe")

                @block.vector
                def _(v, tt=tt, buf=buf):
                    p.wait_mark(v, ("vg_mm", tt))
                    if tt >= 2:
                        p.wait_mark(v, ("vg_st", tt - 2))
                    v.tensor_copy(v_s[buf][:], psV[buf][:])
                    i = v.tensor_copy(g_s[buf][:], psG[buf][:])
                    p.inc(i, "ve")
                    p.set_mark(("vg_evac", tt), "ve")

                @block.sync
                def _(s, tt=tt, buf=buf):
                    p.wait_mark(s, ("vg_evac", tt))
                    i = s.dma_start(out=v_d[tt * 128:(tt + 1) * 128, :], in_=v_s[buf][:])
                    p.dma(i, "st_vg%d" % buf)
                    i = s.dma_start(out=gtok_o[tt * 128:(tt + 1) * 128, :], in_=g_s[buf][:])
                    p.dma(i, "st_vg%d" % buf)
                    p.set_mark(("vg_st", tt), "st_vg%d" % buf)
            barrier(p, block)

        ctx2.__exit__(None, None, None)

        with contextlib.ExitStack() as lctx:
            L = lctx.enter_context
            S = L(nc.sbuf_tensor("S", [128, 16, 256], F32))
            S_bf = L(nc.sbuf_tensor("S_bf", [128, 16, 256], BF16))
            qet_bl = [L(nc.sbuf_tensor(f"qet_bl{i}", [128, 16, 512], BF16)) for i in range(2)]
            kdt_bl = [L(nc.sbuf_tensor(f"kdt_bl{i}", [128, 16, 512], BF16)) for i in range(2)]
            kbt_bl = [L(nc.sbuf_tensor(f"kbt_bl{i}", [128, 16, 512], BF16)) for i in range(2)]
            v_bl = [L(nc.sbuf_tensor(f"v_bl{i}", [128, 4, 1024], BF16)) for i in range(2)]
            oacc_bl = [L(nc.sbuf_tensor(f"oacc_bl{i}", [128, 4, 1024], F32)) for i in range(2)]
            Am = [L(nc.sbuf_tensor(f"Am{i}", [128, 128], BF16)) for i in range(2)]
            kbts = [L(nc.sbuf_tensor(f"kbts{i}", [128, 4, 128], BF16)) for i in range(2)]
            pa = [L(nc.psum_tensor(f"pa{i}", [128, 128], F32)) for i in range(2)]
            po = [L(nc.psum_tensor(f"po{i}", [128, 256], F32)) for i in range(2)]
            pt = [L(nc.psum_tensor(f"pt{i}", [128, 4, 128], BF16)) for i in range(2)]
            pu = L(nc.psum_tensor("pu", [128, 4, 256], F32))

            @block.vector
            def _(v):
                v.memset(S[:], 0.0)
                i = v.memset(S_bf[:], 0.0)
                p.inc(i, "ve")
                p.set_mark("S0", "ve")

            NBL = 4
            NITER = NBL * 4 * H

            def emit_blk_load(bl):
                bbuf = bl % 2
                @block.sync
                def _(s, bl=bl, bbuf=bbuf):
                    if bl < 2:
                        for nm in ("st_q0", "st_q1", "st_vg0", "st_vg1"):
                            p.wait(s, nm)
                    else:
                        p.wait_mark(s, ("blk_done", bl - 2))
                        p.wait_mark(s, ("supd", (bl - 2) * 4 * H + 4 * H - 1))
                    sl = slice(bl * 512, (bl + 1) * 512)
                    i = s.dma_start(out=qet_bl[bbuf][:],
                                    in_=qet_d.rearrange("(f p) t -> p f t", p=128)[:, :, sl])
                    p.dma(i, "ld_blk%d" % bbuf)
                    i = s.dma_start(out=kdt_bl[bbuf][:],
                                    in_=kdt_d.rearrange("(f p) t -> p f t", p=128)[:, :, sl])
                    p.dma(i, "ld_blk%d" % bbuf)
                    i = s.dma_start(out=kbt_bl[bbuf][:],
                                    in_=kbt_d.rearrange("(f p) t -> p f t", p=128)[:, :, sl])
                    p.dma(i, "ld_blk%d" % bbuf)
                    i = s.dma_start(out=v_bl[bbuf][:],
                                    in_=v_d[bl * 512:(bl + 1) * 512, :].rearrange("(c p) f -> p c f", p=128))
                    p.dma(i, "ld_blk%d" % bbuf)
                    p.set_mark(("blk_ld", bl), "ld_blk%d" % bbuf)

            def emit_A(it):
                bl, ci, h = it // 16, (it // H) % 4, it % H
                bbuf, ib = bl % 2, it % 2
                @block.tensor
                def _(t, bl=bl, bbuf=bbuf, ci=ci, h=h, it=it, ib=ib):
                    p.wait_mark(t, ("blk_ld", bl))
                    if it == 0:
                        p.wait_mark(t, "S0")
                        p.wait(t, "ge")
                    if it >= 2:
                        p.wait_mark(t, ("mask", it - 2))
                    for dkc in range(4):
                        i = t.matmul(pa[ib][:],
                                     kdt_bl[bbuf][:, h * 4 + dkc, ci * 128:(ci + 1) * 128],
                                     qet_bl[bbuf][:, h * 4 + dkc, ci * 128:(ci + 1) * 128],
                                     start=(dkc == 0), stop=(dkc == 3))
                    p.inc(i, "pe")
                    p.set_mark(("A", it), "pe")

                @block.vector
                def _(v, it=it, ib=ib):
                    p.wait_mark(v, ("A", it))
                    i = v.tensor_mul(Am[ib][:], pa[ib][:], sb["mask"][:])
                    p.inc(i, "ve")
                    p.set_mark(("mask", it), "ve")

            def emit_oT(it):
                bl, ci, h = it // 16, (it // H) % 4, it % H
                bbuf, ib = bl % 2, it % 2
                @block.tensor
                def _(t, bl=bl, bbuf=bbuf, ci=ci, h=h, it=it, ib=ib):
                    p.wait_mark(t, ("mask", it))
                    if it >= H:
                        p.wait_mark(t, ("supd", it - H))
                    if it >= 2:
                        p.wait_mark(t, ("oev", it - 2))
                    i = t.matmul(po[ib][:], Am[ib][:],
                                 v_bl[bbuf][:, ci, h * 256:(h + 1) * 256],
                                 start=True, stop=False)
                    for dkc in range(4):
                        i = t.matmul(po[ib][:],
                                     qet_bl[bbuf][:, h * 4 + dkc, ci * 128:(ci + 1) * 128],
                                     S_bf[:, h * 4 + dkc, :],
                                     start=False, stop=(dkc == 3))
                    p.inc(i, "pe")
                    p.set_mark(("o", it), "pe")
                    if it >= 2:
                        p.wait_mark(t, ("tev", it - 2))
                    for dkc in range(4):
                        i = t.transpose(pt[ib][:, dkc, :],
                                        kbt_bl[bbuf][:, h * 4 + dkc, ci * 128:(ci + 1) * 128],
                                        sb["ident_b"][:])
                    p.inc(i, "pe")
                    p.set_mark(("T", it), "pe")

                @block.vector
                def _(v, it=it, ib=ib):
                    p.wait_mark(v, ("T", it))
                    i = v.tensor_copy(kbts[ib][:], pt[ib][:])
                    p.inc(i, "ve")
                    p.set_mark(("tev", it), "ve")

            def emit_U(it):
                bl, ci, h = it // 16, (it // H) % 4, it % H
                bbuf, ib = bl % 2, it % 2
                c_glob = it // H
                @block.tensor
                def _(t, bl=bl, bbuf=bbuf, ci=ci, h=h, it=it, ib=ib):
                    p.wait_mark(t, ("tev", it))
                    if it >= 1:
                        p.wait_mark(t, ("supd", it - 1))
                    for dkc in range(4):
                        i = t.matmul(pu[:, dkc, :], kbts[ib][:, dkc, :],
                                     v_bl[bbuf][:, ci, h * 256:(h + 1) * 256],
                                     start=True, stop=True)
                    p.inc(i, "pe")
                    p.set_mark(("U", it), "pe")

                @block.vector
                def _(v, bl=bl, ci=ci, h=h, it=it, ib=ib, c_glob=c_glob, bbuf=bbuf):
                    p.wait_mark(v, ("U", it))
                    for dkc in range(4):
                        v.scalar_tensor_tensor(
                            S[:, h * 4 + dkc, :], S[:, h * 4 + dkc, :],
                            sb["DecL"][:, h * 4 + dkc, c_glob:c_glob + 1],
                            pu[:, dkc, :], ALU.mult, ALU.add)
                    i = v.tensor_copy(S_bf[:, h * 4:h * 4 + 4, :], S[:, h * 4:h * 4 + 4, :])
                    p.inc(i, "ve")
                    p.set_mark(("supd", it), "ve")
                    p.wait_mark(v, ("o", it))
                    if ci == 0 and h == 0 and bl >= 2:
                        p.wait_mark(v, ("blk_done", bl - 2))
                    i = v.tensor_copy(oacc_bl[bbuf][:, ci, h * 256:(h + 1) * 256], po[ib][:])
                    p.inc(i, "ve")
                    p.set_mark(("oev", it), "ve")

                if it % 16 == 15:
                    @block.sync
                    def _(s, bl=bl, bbuf=bbuf):
                        p.wait_mark(s, ("oev", bl * 16 + 15))
                        i = s.dma_start(
                            out=oacc_o[bl * 512:(bl + 1) * 512, :].rearrange("(c p) f -> p c f", p=128),
                            in_=oacc_bl[bbuf][:])
                        p.dma(i, "st_oacc%d" % bbuf)
                        p.set_mark(("blk_done", bl), "st_oacc%d" % bbuf)

            for step in range(NITER + 2):
                if step < NITER and step % 16 == 0:
                    emit_blk_load(step // 16)
                if step < NITER:
                    emit_A(step)
                if 1 <= step <= NITER:
                    emit_oT(step - 1)
                if step >= 2:
                    emit_U(step - 2)

            @block.sync
            def _(s):
                p.wait_mark(s, ("supd", NBL * 4 * H - 1))
                i = s.dma_start(out=sloc_o.rearrange("f p v -> p f v"), in_=S[:])
                p.dma(i, "st_fin")
                i = s.dma_start(out=dtot_o, in_=sb["Dtot"][:])
                p.dma(i, "st_fin")

            @block.sync
            def _(s):
                for nm in ("st_q0", "st_q1", "st_vg0", "st_vg1",
                           "st_oacc0", "st_oacc1", "st_fin"):
                    p.wait(s, nm)
            block.flush()
    return nc


def emit_setup_consts2(p, block, sb):
    @block.gpsimd
    def _(g):
        g.memset(sb["cb"][:, 0:1], EPS)
        g.memset(sb["cb"][:, 1:2], 1.0)
        g.memset(sb["cbm128"][:], -128.0)
        g.memset(sb["ones_f1"][:], 1.0)
        g.memset(sb["ones_bf1"][:], 1.0)
        g.memset(sb["ones_f"][:], 1.0)
        if "ones_b" in sb:
            g.memset(sb["ones_b"][:], 1.0)
        i = g.memset(sb["ident_f"][:], 0.0)
        p.wait(g, "ge", p.cnt["ge"])
        g.affine_select(out=sb["ident_f"][:], in_=sb["ident_f"][:],
                        compare_op=ALU.not_equal, fill=1.0, base=0,
                        pattern=[[-1, 128]], channel_multiplier=1)
        i = g.memset(sb["ident_b"][:], 0.0)
        p.wait(g, "ge", p.cnt["ge"])
        i = g.affine_select(out=sb["ident_b"][:], in_=sb["ident_b"][:],
                            compare_op=ALU.not_equal, fill=1.0, base=0,
                            pattern=[[-1, 128]], channel_multiplier=1)
        p.set_mark("consts", "ge")



def emit_bcast_row(p, block, sb, name, row_sb, n, psname, ld_mark=None):
    nt = (n + 511) // 512
    for j in range(nt):
        w = min(512, n - j * 512)
        @block.tensor
        def _(t, j=j, w=w):
            if j == 0:
                p.wait(t, "ge")
                p.wait(t, "ve")
                if ld_mark is not None:
                    p.wait_mark(t, ld_mark)
            else:
                p.wait_mark(t, (name, "ev", j - 1))
            i = t.matmul(sb[psname][:, 0:w], sb["ones_f1"][:],
                         row_sb[:, j * 512:j * 512 + w], start=True, stop=True)
            p.inc(i, "pe")
            p.set_mark((name, "mm", j), "pe")

        @block.vector
        def _(v, j=j, w=w):
            p.wait_mark(v, (name, "mm", j))
            v.tensor_copy(sb[name][:, j * 512:j * 512 + w], sb[psname][:, 0:w])
            p.set_mark((name, "ev", j), "ve")
    p.set_mark((name, "done"), "ve")


def emit_store_xres(p, block, sb, dst, tag):
    @block.sync
    def _(s):
        p.wait_mark(s, ("xres_done", tag))
        for tt in range(NT):
            i = s.dma_start(out=dst[tt * 128:(tt + 1) * 128, :],
                            in_=sb["xres"][:, tt, :])
            p.dma(i, "st_x" + tag)
    p.set_mark(("xstore", tag), "st_x" + tag)


def emit_ffn(p, block, nc, sb, xn_dram, x_dram, x_dst, xn_dst, wgT, wuT, wdT, tag):
    h_d = nc.dram_tensor("h_d" + tag, [FFD, TOK], BF16).ap()

    with contextlib.ExitStack() as gctx:
        G = gctx.enter_context
        xnT = G(nc.sbuf_tensor("ffn_xnT" + tag, [128, DC, TOK], BF16))
        w32 = [G(nc.sbuf_tensor(f"ffn_w32_{i}{tag}", [128, DC, 128], U8)) for i in range(2)]
        wu32 = [G(nc.sbuf_tensor(f"ffn_wu32_{i}{tag}", [128, DC, 128], U8)) for i in range(2)]
        wgb = [G(nc.sbuf_tensor(f"ffn_wgb{i}{tag}", [128, DC, 128], BF16)) for i in range(2)]
        wub = [G(nc.sbuf_tensor(f"ffn_wub{i}{tag}", [128, DC, 128], BF16)) for i in range(2)]
        gsil = [G(nc.sbuf_tensor(f"ffn_gs{i}{tag}", [128, TOK], BF16)) for i in range(2)]
        upd = [G(nc.sbuf_tensor(f"ffn_up{i}{tag}", [128, TOK], BF16)) for i in range(2)]
        ht = [G(nc.sbuf_tensor(f"ffn_ht{i}{tag}", [128, TOK], BF16)) for i in range(2)]
        psG = G(nc.psum_tensor("ffn_psG" + tag, [128, TOK], F32))
        psU = G(nc.psum_tensor("ffn_psU" + tag, [128, TOK], F32))

        @block.scalar
        def _(s):
            if ("lnst", NT // 2 - 2) in p.mark:
                p.wait_mark(s, ("lnst", NT // 2 - 2))
                p.wait_mark(s, ("lnst", NT // 2 - 1))
            for dc in range(DC):
                i = s.dma_start_transpose(
                    xnT[:, dc, 0:TOK // 2],
                    xn_dram[0:TOK // 2, dc * 128:(dc + 1) * 128])
                p.dma(i, "ds_ffnxnt" + tag)
            for nm in list(p.sem):
                if nm.startswith(("st_lnxn", "st_fxn")):
                    p.wait(s, nm)
            for dc in range(DC):
                i = s.dma_start_transpose(
                    xnT[:, dc, TOK // 2:TOK],
                    xn_dram[TOK // 2:TOK, dc * 128:(dc + 1) * 128])
                p.dma(i, "ds_ffnxnt" + tag)
        p.set_mark(("ffn_xnT", tag), "ds_ffnxnt" + tag)

        for fc in range(FC):
            buf = fc % 2
            @block.sync
            def _(s, fc=fc, buf=buf):
                if fc >= 2:
                    p.wait_mark(s, ("ffn_gmm", tag, fc - 2))
                i = s.dma_start(out=w32[buf][:], in_=wgT[fc])
                p.dma(i, "dg_ffnw%d%s" % (buf, tag))
                p.set_mark(("ffn_wg32", tag, fc), "dg_ffnw%d%s" % (buf, tag))

            @block.scalar
            def _(sc, fc=fc, buf=buf):
                p.wait_mark(sc, ("ffn_wg32", tag, fc))
                if fc >= 2:
                    p.wait_mark(sc, ("ffn_gmm", tag, fc - 2))
                sc.activation(wgb[buf][:], w32[buf][:], AF.Copy, bias=-128.0)
                p.set_mark(("ffn_wgb", tag, fc), "ac")

            @block.sync
            def _(s, fc=fc, buf=buf):
                if fc >= 2:
                    p.wait_mark(s, ("ffn_wub", tag, fc - 2))
                i = s.dma_start(out=wu32[buf][:], in_=wuT[fc])
                p.dma(i, "dg_ffnu%d%s" % (buf, tag))
                p.set_mark(("ffn_wu32", tag, fc), "dg_ffnu%d%s" % (buf, tag))

            @block.scalar
            def _(sc, fc=fc, buf=buf):
                p.wait_mark(sc, ("ffn_wu32", tag, fc))
                if fc >= 2:
                    p.wait_mark(sc, ("ffn_gmm", tag, fc - 2))
                sc.activation(wub[buf][:], wu32[buf][:], AF.Copy, bias=-128.0)
                p.set_mark(("ffn_wub", tag, fc), "ac")

            @block.tensor
            def _(t, fc=fc, buf=buf):
                p.wait_mark(t, ("ffn_wgb", tag, fc))
                p.wait_mark(t, ("ffn_xnT", tag))
                if fc >= 2:
                    p.wait_mark(t, ("ffn_upe", tag, fc - 2))
                for dc in range(DC):
                    for tt in range(4):
                        i = t.matmul(psG[:, tt * 512:(tt + 1) * 512],
                                     wgb[buf][:, dc, :],
                                     xnT[:, dc, tt * 512:(tt + 1) * 512],
                                     start=(dc == 0), stop=(dc == DC - 1))
                p.inc(i, "pe")
                p.set_mark(("ffn_gatemm", tag, fc), "pe")
                p.wait_mark(t, ("ffn_wub", tag, fc))
                for dc in range(DC):
                    for tt in range(4):
                        i = t.matmul(psU[:, tt * 512:(tt + 1) * 512],
                                     wub[buf][:, dc, :],
                                     xnT[:, dc, tt * 512:(tt + 1) * 512],
                                     start=(dc == 0), stop=(dc == DC - 1))
                p.inc(i, "pe")
                p.set_mark(("ffn_gmm", tag, fc), "pe")

            @block.scalar
            def _(sc, fc=fc, buf=buf):
                p.wait_mark(sc, ("ffn_gatemm", tag, fc))
                if fc >= 2:
                    p.wait_mark(sc, ("ffn_hmul", tag, fc - 2))
                sc.activation(gsil[buf][:], psG[:], AF.Silu,
                              scale=sb["sg_pp"][:, fc:fc + 1])
                p.set_mark(("ffn_sil", tag, fc), "ac")
                p.wait_mark(sc, ("ffn_gmm", tag, fc))
                sc.activation(upd[buf][:], psU[:], AF.Copy,
                              scale=sb["su_pp"][:, fc:fc + 1])
                p.set_mark(("ffn_upe", tag, fc), "ac")

            @block.vector
            def _(v, fc=fc, buf=buf):
                p.wait_mark(v, ("ffn_upe", tag, fc))
                if fc >= 2:
                    p.wait_mark(v, ("ffn_hst", tag, fc - 2))
                i = v.tensor_mul(ht[buf][:], gsil[buf][:], upd[buf][:])
                p.inc(i, "ve")
                p.set_mark(("ffn_hmul", tag, fc), "ve")

            @block.sync
            def _(s, fc=fc, buf=buf):
                p.wait_mark(s, ("ffn_hmul", tag, fc))
                i = s.dma_start(out=h_d[fc * 128:(fc + 1) * 128, :], in_=ht[buf][:])
                p.dma(i, "st_ffnh%d%s" % (buf, tag))
                p.set_mark(("ffn_hst", tag, fc), "st_ffnh%d%s" % (buf, tag))

        barrier(p, block)

    NH = NT // 2
    with contextlib.ExitStack() as dctx:
        G = dctx.enter_context
        xrh = G(nc.sbuf_tensor("ffn_xrh" + tag, [128, NH, D], F32))
        wd32 = [G(nc.sbuf_tensor(f"ffn_wd32_{i}{tag}", [128, D], U8)) for i in range(2)]
        wdbA = G(nc.sbuf_tensor("ffn_wdbA" + tag, [128, 6, D], BF16))
        wdbB = G(nc.sbuf_tensor("ffn_wdbB" + tag, [128, 5, D], BF16))
        hts = [G(nc.sbuf_tensor(f"ffn_hts{i}{tag}", [128, 11, 128], BF16)) for i in range(2)]
        tflu = G(nc.sbuf_tensor("ffn_tflu" + tag, [128, D], F32))
        psD = [G(nc.psum_tensor(f"ffn_psD{i}{tag}", [128, D], F32)) for i in range(2)]

        for hf in range(2):
            @block.sync
            def _(s, hf=hf):
                if hf == 0:
                    for nm in list(p.sem):
                        if nm.startswith(("st_x1", "st_x3")):
                            p.wait(s, nm)
                else:
                    p.wait(s, "st_xo%s0" % tag)
                    p.wait(s, "st_xo%s1" % tag)
                    if xn_dst is not None:
                        p.wait_mark(s, ("fl2xn", tag, NH - 1))
                for tt in range(NH):
                    r = (hf * NH + tt) * 128
                    i = s.dma_start(out=xrh[:, tt, :], in_=x_dram[r:r + 128, :])
                    p.dma(i, "ld_x" + tag)
                p.set_mark(("ffn_xld", tag, hf), "ld_x" + tag)

            for g_i in range(4):
                for j in range(11):
                    hc = g_i * 11 + j
                    gj = hf * 44 + g_i * 11 + j
                    @block.sync
                    def _(s, g_i=g_i, j=j, hc=hc, gj=gj, hf=hf):
                        if gj >= 2:
                            p.wait_mark(s, ("ffn_wdb", tag, gj - 2))
                        i = s.dma_start(out=wd32[gj % 2][:],
                                        in_=wdT.rearrange("(c p) f -> p c f", p=128)[:, hc, :])
                        p.dma(i, "dg_ffnwd%d%s" % (gj % 2, tag))
                        p.set_mark(("ffn_wd32", tag, gj), "dg_ffnwd%d%s" % (gj % 2, tag))

                    @block.scalar
                    def _(sc, gj=gj, j=j, hf=hf, g_i=g_i):
                        p.wait_mark(sc, ("ffn_wd32", tag, gj))
                        gg = hf * 4 + g_i
                        if gg > 0:
                            if j == 0:
                                p.wait_mark(sc, ("ffn_dmmA", tag, gg * NH - 1))
                            elif j == 6:
                                p.wait_mark(sc, ("ffn_dmm", tag, gg * NH - 1))
                        if j < 6:
                            sc.activation(wdbA[:, j, :], wd32[gj % 2][:],
                                          AF.Copy, bias=-128.0)
                        else:
                            sc.activation(wdbB[:, j - 6, :], wd32[gj % 2][:],
                                          AF.Copy, bias=-128.0)
                        p.set_mark(("ffn_wdb", tag, gj), "ac")
                        if j == 5:
                            p.set_mark(("ffn_wdbA", tag, gg), "ac")
                        elif j == 10:
                            p.set_mark(("ffn_wdbB", tag, gg), "ac")

                for tt in range(NH):
                    dbuf = tt % 2
                    gt = (hf * 4 + g_i) * NH + tt
                    @block.sync
                    def _(s, g_i=g_i, tt=tt, dbuf=dbuf, gt=gt, hf=hf):
                        if gt >= 2:
                            p.wait_mark(s, ("ffn_dmm", tag, gt - 2))
                        i = s.dma_start(
                            out=hts[dbuf][:],
                            in_=h_d.rearrange("(f p) t -> p f t", p=128)[
                                :, g_i * 11:(g_i + 1) * 11,
                                (hf * NH + tt) * 128:(hf * NH + tt + 1) * 128])
                        p.dma(i, "ld_ffnh%d%s" % (dbuf, tag))
                        p.set_mark(("ffn_htld", tag, gt), "ld_ffnh%d%s" % (dbuf, tag))

                    @block.tensor
                    def _(t, g_i=g_i, tt=tt, dbuf=dbuf, gt=gt, hf=hf):
                        gg = hf * 4 + g_i
                        p.wait_mark(t, ("ffn_htld", tag, gt))
                        p.wait_mark(t, ("ffn_wdbA", tag, gg))
                        if gt >= 2:
                            p.wait_mark(t, ("ffn_dfl", tag, gt - 2))
                        for j in range(6):
                            for ft in range(4):
                                i = t.matmul(psD[dbuf][:, ft * 512:(ft + 1) * 512],
                                             hts[dbuf][:, j, :],
                                             wdbA[:, j, ft * 512:(ft + 1) * 512],
                                             start=(j == 0), stop=False)
                        p.inc(i, "pe")
                        p.set_mark(("ffn_dmmA", tag, gt), "pe")
                        p.wait_mark(t, ("ffn_wdbB", tag, gg))
                        for j in range(5):
                            for ft in range(4):
                                i = t.matmul(psD[dbuf][:, ft * 512:(ft + 1) * 512],
                                             hts[dbuf][:, 6 + j, :],
                                             wdbB[:, j, ft * 512:(ft + 1) * 512],
                                             start=False, stop=(j == 4))
                        p.inc(i, "pe")
                        p.set_mark(("ffn_dmm", tag, gt), "pe")

                    @block.vector
                    def _(v, g_i=g_i, tt=tt, dbuf=dbuf, gt=gt, hf=hf):
                        p.wait_mark(v, ("ffn_dmm", tag, gt))
                        if g_i == 0:
                            p.wait_mark(v, ("ffn_xld", tag, hf))
                        v.tensor_mul(tflu[:], psD[dbuf][:], sb["sdb"][:])
                        v.tensor_add(xrh[:, tt, :], xrh[:, tt, :], tflu[:])
                        p.set_mark(("ffn_dfl", tag, gt), "ve")

            for tt in range(NH):
                buf = tt % 2
                r = (hf * NH + tt) * 128
                @block.sync
                def _(s, tt=tt, hf=hf, r=r):
                    p.wait_mark(s, ("ffn_dfl", tag, (hf * 4 + 3) * NH + tt))
                    i = s.dma_start(out=x_dst[r:r + 128, :], in_=xrh[:, tt, :])
                    p.dma(i, "st_xo%s%d" % (tag, tt % 2))
                    p.set_mark(("ffn_xst", tag, hf, tt), "st_xo%s%d" % (tag, tt % 2))

                if xn_dst is None:
                    continue

                @block.scalar
                def _(sc, tt=tt, buf=buf, hf=hf):
                    p.wait_mark(sc, ("ffn_dfl", tag, (hf * 4 + 3) * NH + tt))
                    gidx = hf * NH + tt
                    if gidx >= 2:
                        p.wait_mark(sc, ("fl2var", tag, gidx - 2))
                    sc.activation(sb["lnscr"][:], xrh[:, tt, :], AF.Square,
                                  accum_out=sb["lnss"][buf][:])
                    p.set_mark(("fl2sq", tag, gidx), "ac")

                @block.vector
                def _(v, tt=tt, buf=buf, hf=hf):
                    gidx = hf * NH + tt
                    v.reduce_sum(sb["lnsum"][buf][:], xrh[:, tt, :],
                                 axis=mybir.AxisListType.X)
                    v.tensor_scalar_mul(sb["lnm"][buf][:], sb["lnsum"][buf][:], 1.0 / D)
                    v.tensor_mul(sb["lnm2"][buf][:], sb["lnm"][buf][:], sb["lnm"][buf][:])
                    p.wait_mark(v, ("fl2sq", tag, gidx))
                    v.tensor_scalar(sb["lnvar"][buf][:], sb["lnss"][buf][:],
                                    1.0 / D, None, ALU.mult)
                    v.tensor_sub(sb["lnvar"][buf][:], sb["lnvar"][buf][:],
                                 sb["lnm2"][buf][:])
                    p.set_mark(("fl2var", tag, gidx), "ve")

                @block.scalar
                def _(sc, tt=tt, buf=buf, hf=hf):
                    gidx = hf * NH + tt
                    p.wait_mark(sc, ("fl2var", tag, gidx))
                    if gidx >= 2:
                        p.wait_mark(sc, ("fl2xn", tag, gidx - 2))
                    sc.activation(sb["lnrs"][buf][:], sb["lnvar"][buf][:], AF.Ln,
                                  bias=sb["cb"][:, 0:1])
                    sc.activation(sb["lnrs"][buf][:], sb["lnrs"][buf][:], AF.Exp,
                                  scale=-0.5)
                    p.set_mark(("fl2rs", tag, gidx), "ac")

                @block.vector
                def _(v, tt=tt, buf=buf, hf=hf):
                    gidx = hf * NH + tt
                    p.wait_mark(v, ("fl2rs", tag, gidx))
                    if gidx >= 2:
                        p.wait_mark(v, ("fl2st", tag, gidx - 2))
                    v.tensor_scalar(sb["lnxn"][buf][:], xrh[:, tt, :],
                                    sb["lnm"][buf][:], sb["lnrs"][buf][:],
                                    ALU.subtract, ALU.mult)
                    p.set_mark(("fl2xn", tag, gidx), "ve")

                @block.sync
                def _(s, tt=tt, buf=buf, hf=hf, r=r):
                    gidx = hf * NH + tt
                    p.wait_mark(s, ("fl2xn", tag, gidx))
                    i = s.dma_start(out=xn_dst[r:r + 128, :], in_=sb["lnxn"][buf][:])
                    p.dma(i, "st_fxn%s%d" % (tag, buf))
                    p.set_mark(("fl2st", tag, gidx), "st_fxn%s%d" % (tag, buf))

        barrier(p, block)


def emit_ln_sbuf(p, block, sb, xn_dst, tag):
    for t in range(NT):
        buf = t % 2
        @block.scalar
        def _(sc, t=t, buf=buf):
            if t == 0:
                p.wait_mark(sc, ("xres_done", tag))
            if t >= 2:
                p.wait_mark(sc, ("l2var", tag, t - 2))
            sc.activation(sb["lnscr"][:], sb["xres"][:, t, :], AF.Square,
                          accum_out=sb["lnss"][buf][:])
            p.set_mark(("l2sq", tag, t), "ac")

        @block.vector
        def _(v, t=t, buf=buf):
            v.reduce_sum(sb["lnsum"][buf][:], sb["xres"][:, t, :],
                         axis=mybir.AxisListType.X)
            v.tensor_scalar_mul(sb["lnm"][buf][:], sb["lnsum"][buf][:], 1.0 / D)
            v.tensor_mul(sb["lnm2"][buf][:], sb["lnm"][buf][:], sb["lnm"][buf][:])
            p.wait_mark(v, ("l2sq", tag, t))
            v.tensor_scalar(sb["lnvar"][buf][:], sb["lnss"][buf][:],
                            1.0 / D, None, ALU.mult)
            v.tensor_sub(sb["lnvar"][buf][:], sb["lnvar"][buf][:], sb["lnm2"][buf][:])
            p.set_mark(("l2var", tag, t), "ve")

        @block.scalar
        def _(sc, t=t, buf=buf):
            p.wait_mark(sc, ("l2var", tag, t))
            if t >= 2:
                p.wait_mark(sc, ("l2xn", tag, t - 2))
            sc.activation(sb["lnrs"][buf][:], sb["lnvar"][buf][:], AF.Ln,
                          bias=sb["cb"][:, 0:1])
            sc.activation(sb["lnrs"][buf][:], sb["lnrs"][buf][:], AF.Exp, scale=-0.5)
            p.set_mark(("l2rs", tag, t), "ac")

        @block.vector
        def _(v, t=t, buf=buf):
            p.wait_mark(v, ("l2rs", tag, t))
            if t >= 2:
                p.wait_mark(v, ("l2st", tag, t - 2))
            v.tensor_scalar(sb["lnxn"][buf][:], sb["xres"][:, t, :],
                            sb["lnm"][buf][:], sb["lnrs"][buf][:],
                            ALU.subtract, ALU.mult)
            p.set_mark(("l2xn", tag, t), "ve")

        @block.sync
        def _(s, t=t, buf=buf):
            p.wait_mark(s, ("l2xn", tag, t))
            i = s.dma_start(out=xn_dst[t * 128:(t + 1) * 128, :], in_=sb["lnxn"][buf][:])
            p.dma(i, "st_l2xn%d%s" % (buf, tag))
            p.set_mark(("l2st", tag, t), "st_l2xn%d%s" % (buf, tag))



def build_launch2():
    nc = bass.Bass("TRN2", target_bir_lowering=False)
    xsh = nc.dram_tensor("xsh", [TOK, D], F32, kind="ExternalInput").ap()
    oacc_i = nc.dram_tensor("oacc", [TOK, 1024], F32, kind="ExternalInput").ap()
    qgt_i = nc.dram_tensor("qgt", [NT, 128, 16, 128], BF16, kind="ExternalInput").ap()
    gtok_i = nc.dram_tensor("gtok", [TOK, 1024], BF16, kind="ExternalInput").ap()
    psl = [nc.dram_tensor(f"psl{j}", [16, 128, 256], F32, kind="ExternalInput").ap()
           for j in range(3)]
    pdt = [nc.dram_tensor(f"pdt{j}", [128, 16], F32, kind="ExternalInput").ap()
           for j in range(3)]
    wnr = nc.dram_tensor("wnr", [1, 1024], F32, kind="ExternalInput").ap()
    Wo = nc.dram_tensor("Wo", [128, 8, D], F32, kind="ExternalInput").ap()
    wgT = nc.dram_tensor("wgT", [FC, 128, 16, 128], U8, kind="ExternalInput").ap()
    wuT = nc.dram_tensor("wuT", [FC, 128, 16, 128], U8, kind="ExternalInput").ap()
    wdT = nc.dram_tensor("wdT", [FFD, D], U8, kind="ExternalInput").ap()
    sgr = nc.dram_tensor("sgr", [FC, 128], F32, kind="ExternalInput").ap()
    sur = nc.dram_tensor("sur", [FC, 128], F32, kind="ExternalInput").ap()
    sdr = nc.dram_tensor("sdr", [1, D], F32, kind="ExternalInput").ap()
    WqkvT = nc.dram_tensor("WqkvT", [48, 128, 16, 128], F32, kind="ExternalInput").ap()
    Wqv = nc.dram_tensor("Wqv", [2, 128, 16, 1024], F32, kind="ExternalInput").ap()
    qkvb = nc.dram_tensor("qkvb", [32, 128], F32, kind="ExternalInput").ap()
    qkvbv = nc.dram_tensor("qkvbv", [1, 2048], F32, kind="ExternalInput").ap()

    x2_o = nc.dram_tensor("x2", [TOK, D], F32, kind="ExternalOutput").ap()
    qt_o = nc.dram_tensor("qt", [2048, TOK], BF16, kind="ExternalOutput").ap()
    kc_o = nc.dram_tensor("kc", [2048, TOK], BF16, kind="ExternalOutput").ap()
    vc_o = nc.dram_tensor("vc", [TOK, 2048], BF16, kind="ExternalOutput").ap()

    x1_d = nc.dram_tensor("x1_d", [TOK, D], F32).ap()
    xn2_d = nc.dram_tensor("xn2_d", [TOK, D], BF16).ap()
    xn3_d = nc.dram_tensor("xn3_d", [TOK, D], BF16).ap()

    with contextlib.ExitStack() as ctx:
        E = ctx.enter_context
        p = Prog(nc, ctx, SEMS)
        sb = {}
        for nm, shp, dt in (
            ("cb", [128, 4], F32), ("cbm128", [128, 1], F32),
            ("ones_f1", [1, 128], F32), ("ones_bf1", [1, 128], BF16),
            ("ones_f", [128, 128], F32),
            ("ident_f", [128, 128], F32), ("ident_b", [128, 128], BF16),
            ("fvstage", [128, 128], F32),
            ("sg_pp", [128, FC], F32), ("su_pp", [128, FC], F32),
            ("qkvb_pp", [128, 32], F32),
            ("sdb", [128, D], F32), ("WB", [128, 1024], F32),
            ("lnscr", [128, D], F32),
        ):
            sb[nm] = E(nc.sbuf_tensor(nm, shp, dt))
        for nm in ("lnss", "lnsum", "lnm", "lnm2", "lnvar", "lnrs"):
            sb[nm] = [E(nc.sbuf_tensor(f"{nm}{i}", [128, 1], F32)) for i in range(2)]
        sb["lnxn"] = [E(nc.sbuf_tensor(f"lnxn{i}", [128, D], BF16)) for i in range(2)]
        sb["lnx"] = [E(nc.sbuf_tensor(f"lnx{i}", [128, D], F32)) for i in range(2)]
        sb["S_in_bf"] = E(nc.sbuf_tensor("S_in_bf", [128, 16, 256], BF16))
        S_in_bf = sb["S_in_bf"]

        block = WBlock(p, E(nc.Block()))
        emit_setup_consts2(p, block, sb)

        with contextlib.ExitStack() as p0:
            P0 = p0.enter_context
            sb["fvpsum"] = P0(nc.psum_tensor("fvpsum", [128, 128], F32))
            sb["bcps"] = P0(nc.psum_tensor("bcps", [128, 512], F32))
            sb["sd_row"] = P0(nc.sbuf_tensor("sd_row", [1, D], F32))
            sb["wn_row"] = P0(nc.sbuf_tensor("wn_row", [1, 1024], F32))
            S_in = P0(nc.sbuf_tensor("S_in", [128, 16, 256], F32))
            psl_s = [P0(nc.sbuf_tensor(f"psl_s{j}", [128, 16, 256], F32)) for j in range(3)]
            pdt_s = [P0(nc.sbuf_tensor(f"pdt_s{j}", [128, 16], F32)) for j in range(3)]

            emit_feature_vec(p, block, sb, "sg_pp", sgr, FC)
            emit_feature_vec(p, block, sb, "su_pp", sur, FC)
            emit_feature_vec(p, block, sb, "qkvb_pp", qkvb, 32)

            @block.sync
            def _(s):
                i = s.dma_start(out=sb["sd_row"][:], in_=sdr)
                p.dma(i, "ds_rows")
                i = s.dma_start(out=sb["wn_row"][:], in_=wnr)
                p.dma(i, "ds_rows")
                p.set_mark("rows_ld", "ds_rows")

            emit_bcast_row(p, block, sb, "sdb", sb["sd_row"], D, "bcps",
                           ld_mark="rows_ld")
            emit_bcast_row(p, block, sb, "WB", sb["wn_row"], 1024, "bcps",
                           ld_mark="rows_ld")

            @block.sync
            def _(s):
                for j in range(3):
                    i = s.dma_start(out=psl_s[j][:], in_=psl[j].rearrange("f p v -> p f v"))
                    p.dma(i, "ld_rel")
                    i = s.dma_start(out=pdt_s[j][:], in_=pdt[j])
                    p.dma(i, "ld_rel")
                p.set_mark("rel_ld", "ld_rel")

            @block.vector
            def _(v):
                p.wait_mark(v, "rel_ld")
                for fc in range(16):
                    v.scalar_tensor_tensor(
                        S_in[:, fc, :], psl_s[2][:, fc, :], pdt_s[1][:, fc:fc + 1],
                        psl_s[1][:, fc, :], ALU.mult, ALU.add)
                    v.scalar_tensor_tensor(
                        S_in[:, fc, :], S_in[:, fc, :], pdt_s[0][:, fc:fc + 1],
                        psl_s[0][:, fc, :], ALU.mult, ALU.add)
                v.tensor_copy(S_in_bf[:], S_in[:])
                p.set_mark("sin_done", "ve")
            barrier(p, block)

        with contextlib.ExitStack() as lctx:
            L = lctx.enter_context
            xoac = L(nc.sbuf_tensor("xoac", [128, NT, 1024], F32))
            wo_b = L(nc.sbuf_tensor("wo_b", [128, 8, D], BF16))
            qgt_t = [L(nc.sbuf_tensor(f"qgt_t{i}", [128, 16, 128], BF16)) for i in range(2)]
            gt_t = [L(nc.sbuf_tensor(f"gt_t{i}", [128, 1024], BF16)) for i in range(2)]
            sgt = [L(nc.sbuf_tensor(f"sgt{i}", [128, 1024], F32)) for i in range(2)]
            oft = [L(nc.sbuf_tensor(f"oft{i}", [128, 1024], BF16)) for i in range(2)]
            ofT = [L(nc.sbuf_tensor(f"ofT{i}", [128, 8, 128], BF16)) for i in range(2)]
            x_t = [L(nc.sbuf_tensor(f"x_t{i}", [128, D], F32)) for i in range(2)]
            rst = [L(nc.sbuf_tensor(f"rst{i}", [128, 4], F32)) for i in range(2)]
            psC = [L(nc.psum_tensor(f"psC{i}", [128, 256], F32)) for i in range(2)]
            psT = [L(nc.psum_tensor(f"psT{i}", [128, 8, 128], BF16)) for i in range(2)]
            psW = L(nc.psum_tensor("psW", [128, D], F32))

            @block.gpsimd
            def _(g):
                i = g.dma_start(out=wo_b[:], in_=Wo)
                p.dma(i, "dg_wo")
                p.set_mark("wo_ld", "dg_wo")

            @block.sync
            def _(s):
                for tt in range(NT):
                    i = s.dma_start(out=xoac[:, tt, :],
                                    in_=oacc_i[tt * 128:(tt + 1) * 128, :])
                    p.dma(i, "ld_oacc")
                p.set_mark("oacc_ld", "ld_oacc")

            for tt in range(NT):
                buf = tt % 2
                @block.sync
                def _(s, tt=tt, buf=buf):
                    if tt >= 2:
                        p.wait_mark(s, ("corr", tt - 2))
                    i = s.dma_start(out=qgt_t[buf][:], in_=qgt_i[tt])
                    p.dma(i, "ld_qgt%d" % buf)
                    p.set_mark(("qgt_ld", tt), "ld_qgt%d" % buf)

                for h in range(H):
                    @block.tensor
                    def _(t, tt=tt, buf=buf, h=h):
                        p.wait_mark(t, ("qgt_ld", tt))
                        if tt == 0 and h == 0:
                            p.wait_mark(t, "sin_done")
                        hb = (tt * H + h) % 2
                        if tt * H + h >= 2:
                            p.wait_mark(t, ("cadd", tt * H + h - 2))
                        for dkc in range(4):
                            i = t.matmul(psC[hb][:],
                                         qgt_t[buf][:, h * 4 + dkc, :],
                                         S_in_bf[:, h * 4 + dkc, :],
                                         start=(dkc == 0), stop=(dkc == 3))
                        p.inc(i, "pe")
                        p.set_mark(("cmm", tt * H + h), "pe")

                    @block.vector
                    def _(v, tt=tt, h=h):
                        p.wait_mark(v, ("cmm", tt * H + h))
                        if tt == 0 and h == 0:
                            p.wait_mark(v, "oacc_ld")
                        hb = (tt * H + h) % 2
                        v.tensor_add(xoac[:, tt, h * 256:(h + 1) * 256],
                                     xoac[:, tt, h * 256:(h + 1) * 256],
                                     psC[hb][:])
                        p.set_mark(("cadd", tt * H + h), "ve")
                        if h == H - 1:
                            p.set_mark(("corr", tt), "ve")

            for step in range(NT + 1):
                if step < NT:
                    tt = step
                    buf = tt % 2
                    @block.sync
                    def _(s, tt=tt, buf=buf):
                        if tt >= 2:
                            p.wait_mark(s, ("x1st", tt - 2))
                        i = s.dma_start(out=x_t[buf][:],
                                        in_=xsh[tt * 128:(tt + 1) * 128, :])
                        p.dma(i, "ld_xt%d" % buf)
                        p.set_mark(("x_ld", tt), "ld_xt%d" % buf)

                    @block.sync
                    def _(s, tt=tt, buf=buf):
                        if tt >= 2:
                            p.wait_mark(s, ("epi_g", tt - 2))
                        i = s.dma_start(out=gt_t[buf][:],
                                        in_=gtok_i[tt * 128:(tt + 1) * 128, :])
                        p.dma(i, "ld_gt%d" % buf)
                        p.set_mark(("gt_ld", tt), "ld_gt%d" % buf)

                    @block.scalar
                    def _(sc, tt=tt, buf=buf):
                        p.wait_mark(sc, ("gt_ld", tt))
                        if tt >= 2:
                            p.wait_mark(sc, ("epi_of", tt - 2))
                        sc.activation(sgt[buf][:], gt_t[buf][:], AF.Silu)
                        p.set_mark(("epi_g", tt), "ac")

                    @block.vector
                    def _(v, tt=tt, buf=buf):
                        p.wait_mark(v, ("corr", tt))
                        if tt >= 2:
                            p.wait_mark(v, ("epi_rs", tt - 2))
                        v.tensor_mul(sb["lnscr"][:, 0:1024], xoac[:, tt, :], xoac[:, tt, :])
                        for h in range(H):
                            v.reduce_sum(rst[buf][:, h:h + 1],
                                         sb["lnscr"][:, h * 256:(h + 1) * 256],
                                         mybir.AxisListType.X)
                        p.set_mark(("epi_sq", tt), "ve")

                    @block.scalar
                    def _(sc, tt=tt, buf=buf):
                        p.wait_mark(sc, ("epi_sq", tt))
                        for h in range(H):
                            sc.activation(rst[buf][:, h:h + 1], rst[buf][:, h:h + 1],
                                          AF.Ln, scale=1.0 / 256.0, bias=sb["cb"][:, 0:1])
                            sc.activation(rst[buf][:, h:h + 1], rst[buf][:, h:h + 1],
                                          AF.Exp, scale=-0.5)
                        p.set_mark(("epi_rs", tt), "ac")

                    @block.vector
                    def _(v, tt=tt, buf=buf):
                        p.wait_mark(v, ("epi_rs", tt))
                        p.wait_mark(v, ("epi_g", tt))
                        if tt >= 2:
                            p.wait_mark(v, ("ofT_mm", tt - 2))
                        for h in range(H):
                            v.scalar_tensor_tensor(
                                sgt[buf][:, h * 256:(h + 1) * 256],
                                xoac[:, tt, h * 256:(h + 1) * 256],
                                rst[buf][:, h:h + 1],
                                sgt[buf][:, h * 256:(h + 1) * 256], ALU.mult, ALU.mult)
                        i = v.tensor_mul(oft[buf][:], sgt[buf][:], sb["WB"][:])
                        p.inc(i, "ve")
                        p.set_mark(("epi_of", tt), "ve")

                    @block.tensor
                    def _(t, tt=tt, buf=buf):
                        p.wait_mark(t, ("epi_of", tt))
                        p.wait_mark(t, "consts")
                        if tt >= 2:
                            p.wait_mark(t, ("ofT_ev", tt - 2))
                        for pc in range(8):
                            i = t.transpose(psT[buf][:, pc, :],
                                            oft[buf][:, pc * 128:(pc + 1) * 128],
                                            sb["ident_b"][:])
                        p.inc(i, "pe")
                        p.set_mark(("ofT_mm", tt), "pe")

                    @block.vector
                    def _(v, tt=tt, buf=buf):
                        p.wait_mark(v, ("ofT_mm", tt))
                        if tt >= 2:
                            p.wait_mark(v, ("wo_mm", tt - 2, 3))
                        i = v.tensor_copy(ofT[buf][:], psT[buf][:])
                        p.inc(i, "ve")
                        p.set_mark(("ofT_ev", tt), "ve")

                if step >= 1:
                    tt = step - 1
                    buf = tt % 2
                    @block.tensor
                    def _(t, tt=tt, buf=buf):
                        p.wait_mark(t, ("ofT_ev", tt))
                        p.wait_mark(t, "wo_ld")
                        for ft in range(4):
                            if tt >= 1:
                                p.wait_mark(t, ("x1add", tt - 1, ft))
                            for pc in range(8):
                                i = t.matmul(psW[:, ft * 512:(ft + 1) * 512],
                                             ofT[buf][:, pc, :],
                                             wo_b[:, pc, ft * 512:(ft + 1) * 512],
                                             start=(pc == 0), stop=(pc == 7))
                            p.inc(i, "pe")
                            p.set_mark(("wo_mm", tt, ft), "pe")

                    @block.vector
                    def _(v, tt=tt, buf=buf):
                        p.wait_mark(v, ("x_ld", tt))
                        for ft in range(4):
                            p.wait_mark(v, ("wo_mm", tt, ft))
                            i = v.tensor_add(x_t[buf][:, ft * 512:(ft + 1) * 512],
                                             x_t[buf][:, ft * 512:(ft + 1) * 512],
                                             psW[:, ft * 512:(ft + 1) * 512])
                            p.inc(i, "ve")
                            p.set_mark(("x1add", tt, ft), "ve")
                        p.set_mark(("x1add", tt), "ve")

                    @block.sync
                    def _(s, tt=tt, buf=buf):
                        p.wait_mark(s, ("x1add", tt))
                        i = s.dma_start(out=x1_d[tt * 128:(tt + 1) * 128, :],
                                        in_=x_t[buf][:])
                        p.dma(i, "st_x1%d" % buf)
                        p.set_mark(("x1st", tt), "st_x1%d" % buf)

            emit_ln_phase(p, block, sb, x1_d, xn2_d, dep=lambda t: ("x1st", t), queue="gpsimd")
            barrier(p, block)



        emit_ffn(p, block, nc, sb, xn2_d, x1_d, x2_o, xn3_d, wgT, wuT, wdT, "f1")

        with contextlib.ExitStack() as lctx:
            L = lctx.enter_context
            xn3T = L(nc.sbuf_tensor("xn3T", [128, DC, TOK], BF16))
            wqk_t = [L(nc.sbuf_tensor(f"wqk_t{i}", [128, DC, 128], BF16)) for i in range(2)]
            qk_s = [L(nc.sbuf_tensor(f"qk_s{i}", [128, TOK], BF16)) for i in range(2)]
            psQ = [L(nc.psum_tensor(f"psQ{i}", [128, TOK], F32)) for i in range(2)]

            @block.sync
            def _(s):
                for nm in list(p.sem):
                    if nm.startswith(("st_l2xn", "st_xf1")):
                        p.wait(s, nm)
                for dc in range(DC):
                    i = s.dma_start_transpose(
                        xn3T[:, dc, :], xn3_d[:, dc * 128:(dc + 1) * 128])
                    p.dma(i, "ds_xn3t")
                p.set_mark("xn3T_ld", "ds_xn3t")

            for fc in range(32):
                buf = fc % 2
                @block.gpsimd
                def _(g, fc=fc, buf=buf):
                    if fc >= 2:
                        p.wait_mark(g, ("qkmm", fc - 2))
                    i = g.dma_start(out=wqk_t[buf][:], in_=WqkvT[fc])
                    p.dma(i, "dg_wqkv%d" % buf)
                    p.set_mark(("wqkv_ld", fc), "dg_wqkv%d" % buf)

                @block.tensor
                def _(t, fc=fc, buf=buf):
                    p.wait_mark(t, ("wqkv_ld", fc))
                    p.wait_mark(t, "xn3T_ld")
                    if fc >= 2:
                        p.wait_mark(t, ("qkev", fc - 2))
                    for dc in range(DC):
                        for tt in range(4):
                            i = t.matmul(psQ[buf][:, tt * 512:(tt + 1) * 512],
                                         wqk_t[buf][:, dc, :],
                                         xn3T[:, dc, tt * 512:(tt + 1) * 512],
                                         start=(dc == 0), stop=(dc == DC - 1))
                    p.inc(i, "pe")
                    p.set_mark(("qkmm", fc), "pe")

                @block.vector
                def _(v, fc=fc, buf=buf):
                    p.wait_mark(v, ("qkmm", fc))
                    if fc >= 2:
                        p.wait_mark(v, ("qkst", fc - 2))
                    v.tensor_scalar_add(qk_s[buf][:], psQ[buf][:],
                                        sb["qkvb_pp"][:, fc:fc + 1])
                    p.set_mark(("qkev", fc), "ve")

                @block.sync
                def _(s, fc=fc, buf=buf):
                    p.wait_mark(s, ("qkev", fc))
                    dst = qt_o if fc < 16 else kc_o
                    r = (fc % 16) * 128
                    i = s.dma_start(out=dst[r:r + 128, :], in_=qk_s[buf][:])
                    p.dma(i, "st_qk%d" % buf)
                    p.set_mark(("qkst", fc), "st_qk%d" % buf)

            wqv_b = L(nc.sbuf_tensor("wqv_b", [128, DC, 1024], BF16))
            vb_row = L(nc.sbuf_tensor("vb_row", [1, 2048], BF16))
            v_s = [L(nc.sbuf_tensor(f"v2_s{i}", [128, 1024], BF16)) for i in range(2)]

            for vh in range(2):
                @block.gpsimd
                def _(g, vh=vh):
                    if vh > 0:
                        for tt in range(NT - 2, NT):
                            p.wait_mark(g, ("vmm", vh - 1, tt))
                    i = g.dma_start(out=wqv_b[:], in_=Wqv[vh])
                    p.dma(i, "dg_wqv%d" % vh)
                    if vh == 0:
                        i = g.dma_start(out=vb_row[:], in_=qkvbv)
                        p.dma(i, "dg_wqv%d" % vh)
                    p.set_mark(("wqv_ld", vh), "dg_wqv%d" % vh)

                for tt in range(NT):
                    buf = tt % 2
                    gi = vh * NT + tt
                    @block.tensor
                    def _(t, tt=tt, buf=buf, vh=vh, gi=gi):
                        p.wait_mark(t, ("wqv_ld", vh))
                        p.wait_mark(t, "xn3T_ld")
                        if gi < 2:
                            p.wait_mark(t, ("qkev", 30 + gi))
                        elif tt >= 2:
                            p.wait_mark(t, ("vev", vh, tt - 2))
                        else:
                            p.wait_mark(t, ("vev", vh - 1, NT - 2 + tt))
                        for vt in range(2):
                            i = t.matmul(psQ[buf][:, vt * 512:(vt + 1) * 512],
                                         sb["ones_bf1"][:],
                                         vb_row[:, vh * 1024 + vt * 512:
                                                vh * 1024 + (vt + 1) * 512],
                                         start=True, stop=False)
                        for dc in range(DC):
                            for vt in range(2):
                                i = t.matmul(psQ[buf][:, vt * 512:(vt + 1) * 512],
                                             xn3T[:, dc, tt * 128:(tt + 1) * 128],
                                             wqv_b[:, dc, vt * 512:(vt + 1) * 512],
                                             start=False, stop=(dc == DC - 1))
                        p.inc(i, "pe")
                        p.set_mark(("vmm", vh, tt), "pe")

                    @block.vector
                    def _(v, tt=tt, buf=buf, vh=vh):
                        p.wait_mark(v, ("vmm", vh, tt))
                        gi = vh * NT + tt
                        if gi >= 2:
                            if tt >= 2:
                                p.wait_mark(v, ("vst", vh, tt - 2))
                            else:
                                p.wait_mark(v, ("vst", vh - 1, NT - 2 + tt))
                        i = v.tensor_copy(v_s[buf][:], psQ[buf][:, 0:1024])
                        p.inc(i, "ve")
                        p.set_mark(("vev", vh, tt), "ve")

                    @block.sync
                    def _(s, tt=tt, buf=buf, vh=vh):
                        p.wait_mark(s, ("vev", vh, tt))
                        i = s.dma_start(
                            out=vc_o[tt * 128:(tt + 1) * 128,
                                     vh * 1024:(vh + 1) * 1024],
                            in_=v_s[buf][:])
                        p.dma(i, "st_v%d" % buf)
                        p.set_mark(("vst", vh, tt), "st_v%d" % buf)

            @block.sync
            def _(s):
                for nm in list(p.sem):
                    if nm.startswith("st_"):
                        p.wait(s, nm)
            block.flush()
    return nc



def build_launch3():
    nc = bass.Bass("TRN2", target_bir_lowering=False)
    x2_i = nc.dram_tensor("x2", [TOK, D], F32, kind="ExternalInput").ap()
    qt_i = nc.dram_tensor("qt", [2048, TOK], BF16, kind="ExternalInput").ap()
    kgb = nc.dram_tensor("kgb", [HB, 128, 8192], BF16, kind="ExternalInput").ap()
    vgb = nc.dram_tensor("vgb", [HB, 128, 8192], BF16, kind="ExternalInput").ap()
    outwT = nc.dram_tensor("outwT", [128, 16, D], F32, kind="ExternalInput").ap()
    outbr = nc.dram_tensor("outbr", [1, D], F32, kind="ExternalInput").ap()
    wgT = nc.dram_tensor("wgT", [FC, 128, 16, 128], U8, kind="ExternalInput").ap()
    wuT = nc.dram_tensor("wuT", [FC, 128, 16, 128], U8, kind="ExternalInput").ap()
    wdT = nc.dram_tensor("wdT", [FFD, D], U8, kind="ExternalInput").ap()
    sgr = nc.dram_tensor("sgr", [FC, 128], F32, kind="ExternalInput").ap()
    sur = nc.dram_tensor("sur", [FC, 128], F32, kind="ExternalInput").ap()
    sdr = nc.dram_tensor("sdr", [1, D], F32, kind="ExternalInput").ap()

    xout_o = nc.dram_tensor("xout", [TOK, D], F32, kind="ExternalOutput").ap()

    x3_d = nc.dram_tensor("x3_d", [TOK, D], F32).ap()
    xn4_d = nc.dram_tensor("xn4_d", [TOK, D], BF16).ap()

    ATT_SCALE = 1.0 / math.sqrt(HD)

    with contextlib.ExitStack() as ctx:
        E = ctx.enter_context
        p = Prog(nc, ctx, SEMS)
        sb = {}
        for nm, shp, dt in (
            ("cb", [128, 4], F32), ("cbm128", [128, 1], F32),
            ("cbsh", [128, 1], F32),
            ("ones_f1", [1, 128], F32), ("ones_bf1", [1, 128], BF16),
            ("ones_f", [128, 128], F32), ("ones_b", [128, 128], BF16),
            ("ident_f", [128, 128], F32), ("ident_b", [128, 128], BF16),
            ("fvstage", [128, 128], F32),
            ("sg_pp", [128, FC], F32), ("su_pp", [128, FC], F32),
            ("sdb", [128, D], F32),
            ("sd_row", [1, D], F32),
            ("lnscr", [128, D], F32),
        ):
            sb[nm] = E(nc.sbuf_tensor(nm, shp, dt))
        for nm in ("lnss", "lnsum", "lnm", "lnm2", "lnvar", "lnrs"):
            sb[nm] = [E(nc.sbuf_tensor(f"{nm}{i}", [128, 1], F32)) for i in range(2)]
        sb["lnxn"] = [E(nc.sbuf_tensor(f"lnxn{i}", [128, D], BF16)) for i in range(2)]
        sb["lnx"] = [E(nc.sbuf_tensor(f"lnx{i}", [128, D], F32)) for i in range(2)]

        block = WBlock(p, E(nc.Block()))
        emit_setup_consts2(p, block, sb)

        @block.gpsimd
        def _(g):
            g.memset(sb["cbsh"][:], ATT_SHIFT)

        with nc.psum_tensor("fvpsum", [128, 128], F32) as fvps, \
             nc.psum_tensor("bcps", [128, 512], F32) as bcps:
            sb["fvpsum"] = fvps
            sb["bcps"] = bcps
            emit_feature_vec(p, block, sb, "sg_pp", sgr, FC)
            emit_feature_vec(p, block, sb, "su_pp", sur, FC)

            @block.sync
            def _(s):
                i = s.dma_start(out=sb["sd_row"][:], in_=sdr)
                p.dma(i, "ds_rows")
                p.set_mark("rows_ld", "ds_rows")

            emit_bcast_row(p, block, sb, "sdb", sb["sd_row"], D, "bcps",
                           ld_mark="rows_ld")
            barrier(p, block)

        ctxT_cm = nc.sbuf_tensor("ctxT", [128, HB, TOK], BF16)
        ctxT = ctxT_cm.__enter__()
        with contextlib.ExitStack() as lctx:
            L = lctx.enter_context
            kt_h = [L(nc.sbuf_tensor(f"kt_h{i}", [128, 8192], BF16)) for i in range(2)]
            v_h = [L(nc.sbuf_tensor(f"v_h{i}", [128, 64, 128], BF16)) for i in range(2)]
            qt_h = [L(nc.sbuf_tensor(f"qt_h{i}", [128, TOK], BF16)) for i in range(2)]
            ptl = [L(nc.sbuf_tensor(f"ptl{i}", [128, 1024], BF16)) for i in range(4)]
            recip = L(nc.sbuf_tensor("recip", [128, TOK], F32))
            dnm = recip
            dnmG = L(nc.sbuf_tensor("dnmG", [128, 1024], F32))
            psS = [L(nc.psum_tensor(f"psS{i}", [128, 1024], F32)) for i in range(2)]
            psO = L(nc.psum_tensor("psO", [128, 1024], F32))
            psD = L(nc.psum_tensor("psD", [128, 1024], F32))
            XP, XG = 128, 640

            def emit_bcast(hh, pr):
                @block.tensor
                def _(t, hh=hh, pr=pr):
                    gbase = (hh * 2 + pr) * 64
                    q0 = pr * 1024
                    g0 = pr * 512
                    p.wait_mark(t, ("dnmadd", gbase + 63))
                    p.wait_mark(t, ("dnmaddg", gbase + 62))
                    if hh * 2 + pr >= 1:
                        p.wait_mark(t, ("recipp", hh * 2 + pr - 1))
                    i = t.matmul(psD[:, XP:512], sb["ones_f"][:],
                                 dnm[:, q0 + XP:q0 + 512], start=True, stop=False)
                    i = t.matmul(psD[:, XP:512], sb["ones_f"][:],
                                 dnmG[:, g0:g0 + (512 - XP)], start=False, stop=True)
                    i = t.matmul(psD[:, 512:XG], sb["ones_f"][:],
                                 dnm[:, q0 + 512:q0 + XG], start=True, stop=False)
                    i = t.matmul(psD[:, 512:XG], sb["ones_f"][:],
                                 dnmG[:, g0 + (512 - XP):g0 + (XG - XP)],
                                 start=False, stop=True)
                    i = t.matmul(psD[:, XG:1024], sb["ones_f"][:],
                                 dnm[:, q0 + XG:q0 + 1024], start=True, stop=True)
                    p.inc(i, "pe")
                    p.set_mark(("dbc2", hh, pr), "pe")

            def emit_recipp(hh, pr):
                @block.scalar
                def _(sc, hh=hh, pr=pr):
                    q0 = pr * 1024
                    p.wait_mark(sc, ("dbc2", hh, pr))
                    sc.activation(recip[:, q0:q0 + 1024], psD[:], AF.Ln)
                    sc.activation(recip[:, q0:q0 + 1024], recip[:, q0:q0 + 1024],
                                  AF.Exp, scale=-1.0)
                    p.set_mark(("recipp", hh * 2 + pr), "ac")

            def emit_ctxnorm(hh, pr):
                @block.vector
                def _(v, hh=hh, pr=pr):
                    q0 = pr * 1024
                    p.wait_mark(v, ("recipp", hh * 2 + pr))
                    p.wait_mark(v, ("oev", hh * 2 + pr))
                    i = v.tensor_mul(ctxT[:, hh, q0:q0 + 1024],
                                     ctxT[:, hh, q0:q0 + 1024],
                                     recip[:, q0:q0 + 1024])
                    p.inc(i, "ve")
                    if pr == 1:
                        p.set_mark(("ctx_norm", hh), "ve")

            for h in range(HB):
                hb = h % 2
                @block.sync
                def _(s, h=h, hb=hb):
                    if h >= 2:
                        p.wait_mark(s, ("ctx_norm", h - 2))
                    i = s.dma_start(out=kt_h[hb][:], in_=kgb[h])
                    p.dma(i, "ld_kt%d" % hb)
                    i = s.dma_start(out=v_h[hb][:],
                                    in_=vgb[h].rearrange("p (kt f) -> p kt f", f=128))
                    p.dma(i, "ld_kt%d" % hb)
                    i = s.dma_start(out=qt_h[hb][:], in_=qt_i[h * 128:(h + 1) * 128, :])
                    p.dma(i, "ld_kt%d" % hb)
                    p.set_mark(("att_ld", h), "ld_kt%d" % hb)

                if h >= 1:
                    emit_bcast(h - 1, 1)

                for pair in range(2):
                    q0 = pair * 1024
                    for kt in range(66):
                        ib = kt % 2
                        gi = (h * 2 + pair) * 64 + kt
                        pb = gi % 4
                        if pair == 0 and kt == 2 and h >= 1:
                            emit_recipp(h - 1, 1)
                        if pair == 0 and kt == 3 and h >= 1:
                            emit_ctxnorm(h - 1, 1)
                        if pair == 1 and kt == 1:
                            emit_bcast(h, 0)
                        if pair == 1 and kt == 2:
                            emit_recipp(h, 0)
                        if pair == 1 and kt == 3:
                            emit_ctxnorm(h, 0)

                        if kt < 64:
                            @block.tensor
                            def _(t, h=h, hb=hb, pair=pair, kt=kt, ib=ib, gi=gi, q0=q0):
                                p.wait_mark(t, ("att_ld", h))
                                if gi >= 2:
                                    p.wait_mark(t, ("exp", gi - 2))
                                for qb in range(2):
                                    i = t.matmul(
                                        psS[ib][:, qb * 512:(qb + 1) * 512],
                                        kt_h[hb][:, kt * 128:(kt + 1) * 128],
                                        qt_h[hb][:, q0 + qb * 512:q0 + (qb + 1) * 512],
                                        start=True, stop=True)
                                p.inc(i, "pe")
                                p.set_mark(("smm", gi), "pe")

                        if 1 <= kt <= 64:
                            kt_o = kt - 1
                            gi_o = gi - 1
                            pb_o = gi_o % 4
                            @block.tensor
                            def _(t, h=h, hb=hb, pair=pair, kt_o=kt_o, gi_o=gi_o, pb_o=pb_o):
                                p.wait_mark(t, ("exp", gi_o))
                                if kt_o == 0 and (h > 0 or pair > 0):
                                    p.wait_mark(t, ("oev", h * 2 + pair - 1))
                                for qb in range(2):
                                    i = t.matmul(psO[:, qb * 512:(qb + 1) * 512],
                                                 v_h[hb][:, kt_o, :],
                                                 ptl[pb_o][:, qb * 512:(qb + 1) * 512],
                                                 start=(kt_o == 0), stop=(kt_o == 63))
                                p.inc(i, "pe")
                                p.set_mark(("omm", gi_o), "pe")

                        if kt >= 2:
                            kt_o2 = kt - 2
                            gi_o2 = gi - 2
                            pb_o2 = gi_o2 % 4
                            @block.tensor
                            def _(t, h=h, pair=pair, kt_o2=kt_o2, gi_o2=gi_o2, pb_o2=pb_o2):
                                if kt_o2 == 0 and (h > 0 or pair > 0):
                                    p.wait_mark(t, ("recipp", h * 2 + pair - 1))
                                i = t.matmul(psD[:, 0:XP], sb["ones_b"][:],
                                             ptl[pb_o2][:, 0:XP],
                                             start=(kt_o2 == 0), stop=(kt_o2 == 63))
                                p.inc(i, "pe")
                                p.set_mark(("dmm", gi_o2), "pe")

                        if kt < 64:
                            @block.scalar
                            def _(sc, gi=gi, ib=ib, pb=pb):
                                p.wait_mark(sc, ("smm", gi))
                                if gi >= 4:
                                    p.wait_mark(sc, ("dmm", gi - 4))
                                    p.wait_mark(sc, ("dnmadd", gi - 4))
                                    p.wait_mark(sc, ("dnmaddg", (gi - 4) // 2 * 2))
                                sc.activation(ptl[pb][:], psS[ib][:], AF.Exp,
                                              scale=ATT_SCALE, bias=sb["cbsh"][:, 0:1])
                                p.set_mark(("exp", gi), "ac")

                            @block.vector
                            def _(v, h=h, pair=pair, kt=kt, gi=gi, pb=pb, q0=q0):
                                p.wait_mark(v, ("exp", gi))
                                if kt == 0:
                                    i = v.tensor_copy(dnm[:, q0 + XG:q0 + 1024],
                                                      ptl[pb][:, XG:1024])
                                elif kt == 1:
                                    v.tensor_copy(dnm[:, q0 + XP:q0 + XG],
                                                  ptl[pb][:, XP:XG])
                                    i = v.tensor_add(dnm[:, q0 + XG:q0 + 1024],
                                                     dnm[:, q0 + XG:q0 + 1024],
                                                     ptl[pb][:, XG:1024])
                                else:
                                    c0 = XG if gi % 2 == 0 else XP
                                    i = v.tensor_add(dnm[:, q0 + c0:q0 + 1024],
                                                     dnm[:, q0 + c0:q0 + 1024],
                                                     ptl[pb][:, c0:1024])
                                p.inc(i, "ve")
                                p.set_mark(("dnmadd", gi), "ve")

                            if kt % 2 == 0:
                                @block.gpsimd
                                def _(g, h=h, pair=pair, kt=kt, gi=gi, pb=pb):
                                    p.wait_mark(g, ("exp", gi))
                                    g0 = pair * 512
                                    if kt == 0:
                                        if h >= 1:
                                            p.wait_mark(g, ("dbc2", h - 1, pair))
                                        i = g.tensor_copy(
                                            dnmG[:, g0:g0 + (XG - XP)],
                                            ptl[pb][:, XP:XG])
                                    else:
                                        i = g.tensor_add(
                                            dnmG[:, g0:g0 + (XG - XP)],
                                            dnmG[:, g0:g0 + (XG - XP)],
                                            ptl[pb][:, XP:XG])
                                    p.inc(i, "ge")
                                    p.set_mark(("dnmaddg", gi), "ge")

                    @block.vector
                    def _(v, h=h, pair=pair, q0=q0):
                        p.wait_mark(v, ("omm", (h * 2 + pair) * 64 + 63))
                        i = v.tensor_copy(ctxT[:, h, q0:q0 + 1024], psO[:])
                        p.inc(i, "ve")
                        p.set_mark(("oev", h * 2 + pair), "ve")

            emit_bcast(HB - 1, 1)
            emit_recipp(HB - 1, 1)
            emit_ctxnorm(HB - 1, 1)
            barrier(p, block)

        with contextlib.ExitStack() as lctx:
            L = lctx.enter_context
            outw_b = L(nc.sbuf_tensor("outw_b", [128, DC, D], BF16))
            ob_row = L(nc.sbuf_tensor("ob_row", [1, D], BF16))
            x_t = [L(nc.sbuf_tensor(f"x_t{i}", [128, D], F32)) for i in range(2)]
            psP = [L(nc.psum_tensor(f"psP{i}", [128, D], F32)) for i in range(2)]

            @block.gpsimd
            def _(g):
                i = g.dma_start(out=outw_b[:], in_=outwT)
                p.dma(i, "dg_outw")
                i = g.dma_start(out=ob_row[:], in_=outbr)
                p.dma(i, "dg_outw")
                p.set_mark("outw_ld", "dg_outw")

            for tt in range(NT):
                buf = tt % 2
                @block.sync
                def _(s, tt=tt, buf=buf):
                    if tt >= 2:
                        p.wait_mark(s, ("x3st", tt - 2))
                    i = s.dma_start(out=x_t[buf][:],
                                    in_=x2_i[tt * 128:(tt + 1) * 128, :])
                    p.dma(i, "ld_x2%d" % buf)
                    p.set_mark(("x2_ld", tt), "ld_x2%d" % buf)

                @block.tensor
                def _(t, tt=tt, buf=buf):
                    p.wait_mark(t, "outw_ld")
                    if tt >= 2:
                        p.wait_mark(t, ("x3add", tt - 2))
                    for ft in range(4):
                        i = t.matmul(psP[buf][:, ft * 512:(ft + 1) * 512],
                                     sb["ones_bf1"][:],
                                     ob_row[:, ft * 512:(ft + 1) * 512],
                                     start=True, stop=False)
                    for pc in range(DC):
                        for ft in range(4):
                            i = t.matmul(psP[buf][:, ft * 512:(ft + 1) * 512],
                                         ctxT[:, pc, tt * 128:(tt + 1) * 128],
                                         outw_b[:, pc, ft * 512:(ft + 1) * 512],
                                         start=False, stop=(pc == DC - 1))
                    p.inc(i, "pe")
                    p.set_mark(("op_mm", tt), "pe")

                @block.vector
                def _(v, tt=tt, buf=buf):
                    p.wait_mark(v, ("op_mm", tt))
                    p.wait_mark(v, ("x2_ld", tt))
                    i = v.tensor_add(x_t[buf][:], x_t[buf][:], psP[buf][:])
                    p.inc(i, "ve")
                    p.set_mark(("x3add", tt), "ve")

                @block.sync
                def _(s, tt=tt, buf=buf):
                    p.wait_mark(s, ("x3add", tt))
                    i = s.dma_start(out=x3_d[tt * 128:(tt + 1) * 128, :], in_=x_t[buf][:])
                    p.dma(i, "st_x3%d" % buf)
                    p.set_mark(("x3st", tt), "st_x3%d" % buf)
            emit_ln_phase(p, block, sb, x3_d, xn4_d, dep=lambda t: ("x3st", t), queue="gpsimd")
            barrier(p, block)

        ctxT_cm.__exit__(None, None, None)

        emit_ffn(p, block, nc, sb, xn4_d, x3_d, xout_o, None, wgT, wuT, wdT, "f2")

        @block.sync
        def _(s):
            for nm in list(p.sem):
                if nm.startswith("st_"):
                    p.wait(s, nm)
        block.flush()
    return nc



_cache = {}


def _get(name, builder):
    if name not in _cache:
        _cache[name] = builder()
    return _cache[name]


def _l1w4(w):
    w = np.asarray(w, np.float32)
    return _c(w.reshape(16, 128, 16, 128).transpose(2, 1, 0, 3))


def _l1w3(w):
    w = np.asarray(w, np.float32)
    return _c(w.reshape(16, 128, w.shape[1]).transpose(1, 0, 2))


def run_launch1(inputs):
    nc = _get("l1", build_launch1)
    x = inputs["L0_attn_a_x"]
    in_maps = []
    for core in range(8):
        b, pos = core // 4, core % 4
        xs = np.ascontiguousarray(x[b, pos * TOK:(pos + 1) * TOK, :], dtype=np.float32)
        in_maps.append({
            "xsh": xs,
            "Wq": _l1w4(inputs["L0_attn_a_Wq"]),
            "Wk": _l1w4(inputs["L0_attn_a_Wk"]),
            "Wv": _l1w3(inputs["L0_attn_a_Wv"]),
            "Wg": _l1w3(inputs["L0_attn_a_Wg"]),
            "Wgk1": _c(np.asarray(inputs["L0_attn_a_Wgk1"])
                       .reshape(16, 128, 16).transpose(1, 0, 2), np.float32),
            "Wgk2": inputs["L0_attn_a_Wgk2"],
            "bgk2r": np.ascontiguousarray(
                inputs["L0_attn_a_bgk2"].reshape(16, 128), dtype=np.float32),
        })
    res = run_bass_kernel_spmd(nc, in_maps, core_ids=list(range(8)))
    return res.results


def _c(a, dt=None):
    a = np.asarray(a)
    if dt is not None:
        a = a.astype(dt, copy=False) if a.dtype != dt else a
    return np.ascontiguousarray(a)


def kernel(**inputs):
    f32 = np.float32
    x = _c(inputs["L0_attn_a_x"], f32)

    def _u8(a):
        a = np.ascontiguousarray(np.asarray(a))
        return a.view(np.uint8).reshape(a.shape + (4,))[..., 0]

    wgT = _c(_u8(inputs["L0_ffn_wq_gate"]).reshape(FC, 128, 16, 128)
             .transpose(0, 3, 2, 1))
    wuT = _c(_u8(inputs["L0_ffn_wq_up"]).reshape(FC, 128, 16, 128)
             .transpose(0, 3, 2, 1))
    wdT = _c(_u8(inputs["L0_ffn_wq_down"]).T)
    sgr = _c(inputs["L0_ffn_s_gate"].reshape(FC, 128), f32)
    sur = _c(inputs["L0_ffn_s_up"].reshape(FC, 128), f32)
    sdr = _c(inputs["L0_ffn_s_down"].reshape(1, D), f32)
    Wqkv_w = np.asarray(inputs["L0_attn_b_Wqkv_w"])
    WqkvT = _c(Wqkv_w.reshape(48, 128, 16, 128).transpose(0, 3, 2, 1), f32)
    Wqv = _c(Wqkv_w[4096:].reshape(2, 1024, 16, 128).transpose(0, 3, 2, 1), f32)
    qkvb = _c(inputs["L0_attn_b_Wqkv_b"][:4096].reshape(32, 128), f32)
    qkvbv = _c(inputs["L0_attn_b_Wqkv_b"][4096:].reshape(1, 2048), f32)
    outwT = _c(np.asarray(inputs["L0_attn_b_out_w"]).T.reshape(16, 128, D)
               .transpose(1, 0, 2), f32)
    outbr = _c(inputs["L0_attn_b_out_b"].reshape(1, D), f32)
    wnr = _c(np.tile(np.asarray(inputs["L0_attn_a_Wnorm"]), 4).reshape(1, 1024), f32)

    r1 = run_launch1(inputs)

    nc2 = _get("l2", build_launch2)
    zsl = np.zeros((16, 128, 256), f32)
    odt = np.ones((128, 16), f32)
    in_maps = []
    for core in range(8):
        b, pos = core // 4, core % 4
        m = {
            "xsh": _c(x[b, pos * TOK:(pos + 1) * TOK, :], f32),
            "oacc": _c(r1[core]["oacc"], f32),
            "qgt": _c(np.asarray(r1[core]["qgt"]).reshape(16, 128, 16, 128)
                      .transpose(2, 1, 0, 3)),
            "gtok": _c(r1[core]["gtok"]),
            "wnr": wnr,
            "Wo": _c(np.asarray(inputs["L0_attn_a_Wo"]).reshape(8, 128, D)
                     .transpose(1, 0, 2), f32),
            "wgT": wgT, "wuT": wuT, "wdT": wdT,
            "sgr": sgr, "sur": sur, "sdr": sdr,
            "WqkvT": WqkvT, "Wqv": Wqv, "qkvb": qkvb, "qkvbv": qkvbv,
        }
        for j in range(3):
            pp = pos - 1 - j
            if pp >= 0:
                m[f"psl{j}"] = _c(r1[b * 4 + pp]["sloc"], f32)
                m[f"pdt{j}"] = _c(r1[b * 4 + pp]["dtot"], f32)
            else:
                m[f"psl{j}"] = zsl
                m[f"pdt{j}"] = odt
        in_maps.append(m)
    res2 = run_bass_kernel_spmd(nc2, in_maps, core_ids=list(range(8)))
    r2 = res2.results

    nc3 = _get("l3", build_launch3)
    kgb_b, vgb_b = [], []
    for b in range(2):
        kfull = np.concatenate([np.asarray(r2[b * 4 + r]["kc"]) for r in range(4)], axis=1)
        kgb_b.append(_c(kfull.reshape(HB, 128, 8192)))
        vfull = np.concatenate([np.asarray(r2[b * 4 + r]["vc"]) for r in range(4)], axis=0)
        vgb_b.append(_c(vfull.reshape(64, 128, HB, 128)
                        .transpose(2, 1, 0, 3).reshape(HB, 128, 8192)))
    in_maps = []
    for core in range(8):
        b = core // 4
        in_maps.append({
            "x2": _c(r2[core]["x2"], f32),
            "qt": _c(r2[core]["qt"]),
            "kgb": kgb_b[b], "vgb": vgb_b[b],
            "outwT": outwT, "outbr": outbr,
            "wgT": wgT, "wuT": wuT, "wdT": wdT,
            "sgr": sgr, "sur": sur, "sdr": sdr,
        })
    res3 = run_bass_kernel_spmd(nc3, in_maps, core_ids=list(range(8)))
    r3 = res3.results

    out = np.empty((2, 8192, D), f32)
    for core in range(8):
        b, pos = core // 4, core % 4
        out[b, pos * TOK:(pos + 1) * TOK, :] = np.asarray(r3[core]["xout"], f32)
    return out


# version 21 (final)
# speedup vs baseline: 1.0204x; 1.0204x over previous
import contextlib
import math

import numpy as np
import ml_dtypes

import concourse.bass as bass
import concourse.mybir as mybir
from concourse.bass_utils import run_bass_kernel_spmd

F32 = mybir.dt.float32
BF16 = mybir.dt.bfloat16
I32 = mybir.dt.int32
U8 = mybir.dt.uint8
AF = mybir.ActivationFunctionType
ALU = mybir.AluOpType

TOK = 2048
D = 2048
NT = TOK // 128
DC = D // 128
H = 4
DK, DV = 512, 256
CH = 128
NCH = TOK // CH
SCALE_GLA = float(DK ** -0.5)
HB = 16
HD = 128
FFD = 5632
FC = FFD // 128
ATT_SHIFT = -20.0
EPS = 1e-5


_CHAINED_OPS = {
    "memset", "memzero", "tensor_copy", "tensor_tensor", "tensor_add",
    "tensor_sub", "tensor_mul", "tensor_max", "tensor_scalar",
    "tensor_single_scalar", "tensor_scalar_add", "tensor_scalar_sub",
    "tensor_scalar_mul", "tensor_scalar_max", "tensor_scalar_min",
    "tensor_relu", "tensor_reduce", "reduce_max", "reduce_sum",
    "scalar_tensor_tensor", "tensor_tensor_scan", "reciprocal", "select",
    "copy_predicated", "transpose", "bn_stats", "bn_aggr", "activation",
    "square", "sqrt", "mul", "copy", "add", "sign", "stream_shuffle",
    "pool", "pool_avg", "pool_max", "affine_then_add", "affine_select",
    "tensor_tensor_reduce", "affine_mul_reduce",
}


class _Chained:
    def __init__(self, p, eng, sem):
        self._p, self._eng, self._sem = p, eng, sem

    def __getattr__(self, name):
        f = getattr(self._eng, name)
        if name not in _CHAINED_OPS:
            return f
        p, eng, sem = self._p, self._eng, self._sem

        def wrap(*a, **k):
            p.wait(eng, sem)
            i = f(*a, **k)
            p.inc(i, sem)
            p._chain_last = (i, sem)
            return i
        return wrap


class RInstr:
    __slots__ = ("post",)

    def __init__(self):
        self.post = []

    def then_inc(self, sem, k=1):
        self.post.append((sem, k))
        return self


class REng:

    def __init__(self):
        self.ops = []

    def __getattr__(self, name):
        def call(*a, **k):
            tok = RInstr()
            self.ops.append((name, a, k, tok))
            return tok
        return call


class WBlock:

    _ENGS = ("sync", "tensor", "vector", "scalar", "gpsimd")

    def __init__(self, p, block):
        self._p, self._b = p, block
        self._rec = {e: REng() for e in self._ENGS}

    def _section(self, ename, f, chain_sem):
        rec = self._rec[ename]
        if chain_sem is None:
            f(rec)
        else:
            f(_Chained(self._p, rec, chain_sem))

    def sync(self, f):
        self._section("sync", f, None)

    def tensor(self, f):
        self._section("tensor", f, None)

    def vector(self, f):
        self._section("vector", f, "ve")

    def scalar(self, f):
        self._section("scalar", f, "ac")

    def gpsimd(self, f):
        self._section("gpsimd", f, "ge")

    def flush(self):
        for ename in self._ENGS:
            rec = self._rec[ename]
            if not rec.ops:
                continue
            ops, rec.ops = rec.ops, []
            dec = getattr(self._b, ename)

            def body(eng, ops=ops):
                for name, a, k, tok in ops:
                    i = getattr(eng, name)(*a, **k)
                    for sem, kk in tok.post:
                        i.then_inc(sem, kk)
            dec(body)


class Prog:

    def __init__(self, nc, ctx, sem_names):
        self.nc = nc
        self.ctx = ctx
        self.sem = {}
        self.cnt = {}
        self.mark = {}
        self._chain_last = None
        for n in sem_names:
            self.sem[n] = ctx.enter_context(nc.semaphore(n))
            self.cnt[n] = 0

    def _ensure(self, name):
        if name not in self.sem:
            self.sem[name] = self.ctx.enter_context(self.nc.semaphore(name))
            self.cnt[name] = 0

    def inc(self, instr, name, k=1):
        if (self._chain_last is not None and self._chain_last[0] is instr
                and self._chain_last[1] == name):
            return self.cnt[name]
        self._ensure(name)
        instr.then_inc(self.sem[name], k)
        self.cnt[name] += k
        return self.cnt[name]

    def dma(self, instr, name):
        return self.inc(instr, name, 16)

    def set_mark(self, key, name=None):
        self.mark[key] = (name, self.cnt[name])

    def wait(self, eng, name, val=None):
        self._ensure(name)
        if val is None:
            val = self.cnt[name]
        if val > 0:
            eng.wait_ge(self.sem[name], val)

    def wait_mark(self, eng, key):
        name, val = self.mark[key]
        if val > 0:
            eng.wait_ge(self.sem[name], val)


def barrier(p, block, engs=("sync", "tensor", "vector", "scalar", "gpsimd")):
    full = len(engs) == 5
    dma_streams = [n for n in p.sem
                   if n.split("_")[0] in ("ld", "st", "dg", "ds")] if full else []
    snap = {}
    for e in engs:
        sname = "ph_" + e
        snap[e] = p.cnt[sname] + 1
    own = {"vector": "ve", "scalar": "ac", "gpsimd": "ge", "tensor": "pe"}
    for e in engs:
        dec = getattr(block, e)

        def body(eng, e=e):
            if e == "sync":
                for n in dma_streams:
                    p.wait(eng, n)
            if e in own:
                p.wait(eng, own[e])
            eng.nop().then_inc(p.sem["ph_" + e], 1)
            for o in engs:
                if o != e:
                    eng.wait_ge(p.sem["ph_" + o], snap[o])
        dec(body)
    for e in engs:
        p.cnt["ph_" + e] = snap[e]
    block.flush()


SEMS = [
    "dg", "ds", "pe", "ve", "ac", "ge", "os",
    "ph_sync", "ph_tensor", "ph_vector", "ph_scalar", "ph_gpsimd",
]



def emit_setup_consts(p, block, sb):
    @block.gpsimd
    def _(g):
        i = g.memset(sb["cb"][:, 0:1], EPS)
        p.inc(i, "ge")
        i = g.memset(sb["cb"][:, 1:2], 1.0)
        p.inc(i, "ge")
        i = g.memset(sb["ident_f"][:], 0.0)
        p.inc(i, "ge")
        p.wait(g, "ge")
        i = g.affine_select(out=sb["ident_f"][:], in_=sb["ident_f"][:],
                            compare_op=ALU.not_equal, fill=1.0, base=0,
                            pattern=[[-1, 128]], channel_multiplier=1)
        p.inc(i, "ge")
        i = g.memset(sb["ident_b"][:], 0.0)
        p.inc(i, "ge")
        p.wait(g, "ge")
        i = g.affine_select(out=sb["ident_b"][:], in_=sb["ident_b"][:],
                            compare_op=ALU.not_equal, fill=1.0, base=0,
                            pattern=[[-1, 128]], channel_multiplier=1)
        p.inc(i, "ge")
        i = g.memset(sb["mask"][:], 0.0)
        p.inc(i, "ge")
        p.wait(g, "ge")
        i = g.affine_select(out=sb["mask"][:], in_=sb["mask"][:],
                            compare_op=ALU.is_gt, fill=1.0, base=0,
                            pattern=[[-1, 128]], channel_multiplier=1)
        p.inc(i, "ge")


def emit_feature_vec(p, block, sb, name, src_ap, cols, scale=1.0):
    @block.sync
    def _(s):
        p.wait(s, "ac")
        i = s.dma_start(out=sb["fvstage"][0:cols, :], in_=src_ap)
        p.dma(i, "ds_fv")

    @block.tensor
    def _(t):
        p.wait(t, "ds_fv")
        p.wait(t, "ge")
        p.wait(t, "ac")
        i = t.transpose(sb["fvpsum"][:, 0:cols], sb["fvstage"][0:cols, :],
                        sb["ident_f"][0:cols, 0:cols])
        p.inc(i, "pe")

    @block.scalar
    def _(sc):
        p.wait(sc, "pe")
        i = sc.activation(sb[name][:], sb["fvpsum"][:, 0:cols], AF.Copy,
                          scale=float(scale))
        p.inc(i, "ac")


def emit_ln_phase(p, block, sb, x_src, xn_dst, ntiles=NT, dep=None, queue="sync"):
    qdec = getattr(block, queue)
    for t in range(ntiles):
        buf = t % 2
        @qdec
        def _(s, t=t, buf=buf):
            if dep is not None:
                p.wait_mark(s, dep(t))
            if t >= 2:
                p.wait_mark(s, ("lnxn", t - 2))
            i = s.dma_start(out=sb["lnx"][buf][:], in_=x_src[t * 128:(t + 1) * 128, :])
            p.dma(i, "ld_lnx%d" % buf)
            p.set_mark(("lnld", t), "ld_lnx%d" % buf)

        @block.scalar
        def _(sc, t=t, buf=buf):
            p.wait_mark(sc, ("lnld", t))
            if t >= 2:
                p.wait_mark(sc, ("lnvar", t - 2))
            i = sc.activation(sb["lnscr"][:], sb["lnx"][buf][:], AF.Square,
                              accum_out=sb["lnss"][buf][:])
            p.inc(i, "ac")
            p.set_mark(("lnsq", t), "ac")

        @block.vector
        def _(v, t=t, buf=buf):
            p.wait_mark(v, ("lnld", t))
            v.reduce_sum(sb["lnsum"][buf][:], sb["lnx"][buf][:],
                         axis=mybir.AxisListType.X)
            v.tensor_scalar_mul(sb["lnm"][buf][:], sb["lnsum"][buf][:], 1.0 / D)
            v.tensor_mul(sb["lnm2"][buf][:], sb["lnm"][buf][:], sb["lnm"][buf][:])
            p.wait_mark(v, ("lnsq", t))
            i = v.tensor_scalar(sb["lnvar"][buf][:], sb["lnss"][buf][:],
                                1.0 / D, None, ALU.mult)
            i = v.tensor_sub(sb["lnvar"][buf][:], sb["lnvar"][buf][:], sb["lnm2"][buf][:])
            p.inc(i, "ve")
            p.set_mark(("lnvar", t), "ve")

        @block.scalar
        def _(sc, t=t, buf=buf):
            p.wait_mark(sc, ("lnvar", t))
            if t >= 2:
                p.wait_mark(sc, ("lnxn", t - 2))
            sc.activation(sb["lnrs"][buf][:], sb["lnvar"][buf][:], AF.Ln, bias=sb["cb"][:, 0:1])
            i = sc.activation(sb["lnrs"][buf][:], sb["lnrs"][buf][:], AF.Exp, scale=-0.5)
            p.inc(i, "ac")
            p.set_mark(("lnrs", t), "ac")

        @block.vector
        def _(v, t=t, buf=buf):
            p.wait_mark(v, ("lnrs", t))
            if t >= 2:
                p.wait_mark(v, ("lnst", t - 2))
            i = v.tensor_scalar(sb["lnxn"][buf][:], sb["lnx"][buf][:],
                                sb["lnm"][buf][:], sb["lnrs"][buf][:],
                                ALU.subtract, ALU.mult)
            p.inc(i, "ve")
            p.set_mark(("lnxn", t), "ve")

        @qdec
        def _(s, t=t, buf=buf):
            p.wait_mark(s, ("lnxn", t))
            i = s.dma_start(out=xn_dst[t * 128:(t + 1) * 128, :], in_=sb["lnxn"][buf][:])
            p.dma(i, "st_lnxn%d" % buf)
            p.set_mark(("lnst", t), "st_lnxn%d" % buf)


def emit_load_xnT(p, block, sb, xn_dram, dst_key="xnT"):
    @block.scalar
    def _(s):
        if ("lnst", NT // 2 - 2) in p.mark:
            p.wait_mark(s, ("lnst", NT // 2 - 2))
            p.wait_mark(s, ("lnst", NT // 2 - 1))
        for dc in range(DC):
            i = s.dma_start_transpose(
                sb[dst_key][:, dc, 0:TOK // 2],
                xn_dram[0:TOK // 2, dc * 128:(dc + 1) * 128])
            p.dma(i, "ds_xnt")
        p.wait(s, "st_lnxn0")
        p.wait(s, "st_lnxn1")
        for dc in range(DC):
            i = s.dma_start_transpose(
                sb[dst_key][:, dc, TOK // 2:TOK],
                xn_dram[TOK // 2:TOK, dc * 128:(dc + 1) * 128])
            p.dma(i, "ds_xnt")
    p.set_mark("xnT_ready", "ds_xnt")



def build_launch1():
    nc = bass.Bass("TRN2", target_bir_lowering=False)
    xsh = nc.dram_tensor("xsh", [TOK, D], F32, kind="ExternalInput").ap()
    Wq = nc.dram_tensor("Wq", [16, 128, 16, 128], F32, kind="ExternalInput").ap()
    Wk = nc.dram_tensor("Wk", [16, 128, 16, 128], F32, kind="ExternalInput").ap()
    Wv = nc.dram_tensor("Wv", [128, 16, 1024], F32, kind="ExternalInput").ap()
    Wg = nc.dram_tensor("Wg", [128, 16, 1024], F32, kind="ExternalInput").ap()
    Wgk1 = nc.dram_tensor("Wgk1", [128, 16, 16], F32, kind="ExternalInput").ap()
    Wgk2 = nc.dram_tensor("Wgk2", [16, 2048], F32, kind="ExternalInput").ap()
    bgk2r = nc.dram_tensor("bgk2r", [16, 128], F32, kind="ExternalInput").ap()

    oacc_o = nc.dram_tensor("oacc", [TOK, 1024], F32, kind="ExternalOutput").ap()
    qgt_o = nc.dram_tensor("qgt", [2048, TOK], BF16, kind="ExternalOutput").ap()
    gtok_o = nc.dram_tensor("gtok", [TOK, 1024], BF16, kind="ExternalOutput").ap()
    sloc_o = nc.dram_tensor("sloc", [16, 128, 256], F32, kind="ExternalOutput").ap()
    dtot_o = nc.dram_tensor("dtot", [128, 16], F32, kind="ExternalOutput").ap()

    xn_d = nc.dram_tensor("xn_d", [TOK, D], BF16).ap()
    qet_d = nc.dram_tensor("qet_d", [2048, TOK], BF16).ap()
    kdt_d = nc.dram_tensor("kdt_d", [2048, TOK], BF16).ap()
    kbt_d = nc.dram_tensor("kbt_d", [2048, TOK], BF16).ap()
    v_d = nc.dram_tensor("v_d", [TOK, 1024], BF16).ap()

    with contextlib.ExitStack() as ctx:
        E = ctx.enter_context
        p = Prog(nc, ctx, SEMS)
        sb = {}
        sb["cb"] = E(nc.sbuf_tensor("cb", [128, 2], F32))
        sb["ident_f"] = E(nc.sbuf_tensor("ident_f", [128, 128], F32))
        sb["ident_b"] = E(nc.sbuf_tensor("ident_b", [128, 128], BF16))
        sb["mask"] = E(nc.sbuf_tensor("mask", [128, 128], BF16))
        sb["fvstage"] = E(nc.sbuf_tensor("fvstage", [128, 128], F32))
        sb["negb"] = E(nc.sbuf_tensor("negb", [128, 16], F32))
        sb["DecL"] = E(nc.sbuf_tensor("DecL", [128, 16, NCH], F32))
        sb["Dtot"] = E(nc.sbuf_tensor("Dtot", [128, 16], F32))

        block = WBlock(p, E(nc.Block()))

        emit_setup_consts(p, block, sb)
        with nc.psum_tensor("fvpsum", [128, 128], F32) as fvps:
            sb["fvpsum"] = fvps
            emit_feature_vec(p, block, sb, "negb", bgk2r, 16, scale=-1.0)
            barrier(p, block)

            with contextlib.ExitStack() as lctx:
                L = lctx.enter_context
                sb["lnx"] = [L(nc.sbuf_tensor(f"lnx{i}", [128, D], F32)) for i in range(2)]
                sb["lnxn"] = [L(nc.sbuf_tensor(f"lnxn{i}", [128, D], BF16)) for i in range(2)]
                sb["lnscr"] = L(nc.sbuf_tensor("lnscr", [128, D], F32))
                for nm in ("lnss", "lnsum", "lnm", "lnm2", "lnvar", "lnrs"):
                    sb[nm] = [L(nc.sbuf_tensor(f"{nm}{i}", [128, 1], F32)) for i in range(2)]
                emit_ln_phase(p, block, sb, xsh, xn_d)
                barrier(p, block)

            sb["xnT"] = ctx2 = nc.sbuf_tensor("xnT", [128, DC, TOK], BF16)
            sb["xnT"] = sb["xnT"].__enter__()
            emit_load_xnT(p, block, sb, xn_d)
            barrier(p, block)

        with contextlib.ExitStack() as lctx:
            L = lctx.enter_context
            wgk1b = L(nc.sbuf_tensor("wgk1b", [128, DC, 16], BF16))
            wgk2b = L(nc.sbuf_tensor("wgk2b", [16, 2048], BF16))
            gk1t = L(nc.sbuf_tensor("gk1t", [16, TOK], BF16))
            wq_t = [L(nc.sbuf_tensor(f"wq_t{i}", [128, DC, 128], BF16)) for i in range(2)]
            wk_t = [L(nc.sbuf_tensor(f"wk_t{i}", [128, DC, 128], BF16)) for i in range(2)]
            tA = L(nc.sbuf_tensor("tA", [128, TOK], F32))
            Pb = L(nc.sbuf_tensor("Pb", [128, TOK], F32))
            Plast = L(nc.sbuf_tensor("Plast", [128, NCH], F32))
            PlastS = L(nc.sbuf_tensor("PlastS", [128, NCH], F32))
            Gx = L(nc.sbuf_tensor("Gx", [128, NCH + 1], F32))
            EG = L(nc.sbuf_tensor("EG", [128, NCH], F32))
            Ptot = L(nc.sbuf_tensor("Ptot", [128, 1], F32))
            E1 = L(nc.sbuf_tensor("E1", [128, TOK], BF16))
            E2 = L(nc.sbuf_tensor("E2", [128, TOK], BF16))
            E3 = L(nc.sbuf_tensor("E3", [128, TOK], BF16))
            qet_s = [L(nc.sbuf_tensor(f"qet_s{i}", [128, TOK], BF16)) for i in range(2)]
            qgt_s = [L(nc.sbuf_tensor(f"qgt_s{i}", [128, TOK], BF16)) for i in range(2)]
            kdt_s = [L(nc.sbuf_tensor(f"kdt_s{i}", [128, TOK], BF16)) for i in range(2)]
            kbt_s = [L(nc.sbuf_tensor(f"kbt_s{i}", [128, TOK], BF16)) for i in range(2)]
            with nc.psum_tensor("psGK", [16, TOK], F32) as psGK:
                @block.gpsimd
                def _(g):
                    i = g.dma_start(out=wgk1b[:], in_=Wgk1)
                    p.dma(i, "dg_gkw")
                    i = g.dma_start(out=wgk2b[:], in_=Wgk2)
                    p.dma(i, "dg_gkw")
                    p.set_mark("gkw", "dg_gkw")

                @block.tensor
                def _(t):
                    p.wait_mark(t, "gkw")
                    p.wait_mark(t, "xnT_ready")
                    for tt in range(4):
                        for dc in range(DC):
                            i = t.matmul(psGK[:, tt * 512:(tt + 1) * 512],
                                         wgk1b[:, dc, :],
                                         sb["xnT"][:, dc, tt * 512:(tt + 1) * 512],
                                         start=(dc == 0), stop=(dc == DC - 1))
                    p.inc(i, "pe")
                    p.set_mark("gk1t_mm", "pe")

                @block.vector
                def _(v):
                    p.wait_mark(v, "gk1t_mm")
                    i = v.tensor_copy(gk1t[:], psGK[:])
                    p.inc(i, "ve")
                    p.set_mark("gk1t", "ve")
                barrier(p, block, engs=("tensor", "vector"))

            psA = L(nc.psum_tensor("psA", [128, TOK], F32))
            psB = L(nc.psum_tensor("psB", [128, TOK], F32))

            for fc in range(16):
                buf = fc % 2
                @block.gpsimd
                def _(g, fc=fc, buf=buf):
                    if fc >= 2:
                        p.wait_mark(g, ("qk_mm", fc - 2))
                    i = g.dma_start(out=wq_t[buf][:], in_=Wq[fc])
                    p.dma(i, "dg_wqk%d" % buf)
                    i = g.dma_start(out=wk_t[buf][:], in_=Wk[fc])
                    p.dma(i, "dg_wqk%d" % buf)
                    p.set_mark(("wqk", fc), "dg_wqk%d" % buf)

                @block.tensor
                def _(t, fc=fc):
                    p.wait_mark(t, "gk1t")
                    if fc >= 1:
                        p.wait_mark(t, ("kproj_done_v", fc - 1))
                    for tt in range(4):
                        i = t.matmul(psA[:, tt * 512:(tt + 1) * 512],
                                     wgk2b[:, fc * 128:(fc + 1) * 128],
                                     gk1t[:, tt * 512:(tt + 1) * 512],
                                     start=True, stop=True)
                    p.inc(i, "pe")
                    p.set_mark(("gkpre", fc), "pe")

                @block.scalar
                def _(sc, fc=fc):
                    p.wait_mark(sc, ("gkpre", fc))
                    if fc >= 1:
                        p.wait_mark(sc, ("cum", fc - 1))
                    sc.activation(tA[:], psA[:], AF.Exp, scale=-1.0,
                                  bias=sb["negb"][:, fc:fc + 1])
                    i = sc.activation(tA[:], tA[:], AF.Ln, bias=sb["cb"][:, 1:2])
                    p.inc(i, "ac")
                    p.set_mark(("sp", fc), "ac")

                @block.vector
                def _(v, fc=fc):
                    p.wait_mark(v, ("sp", fc))
                    if fc >= 1:
                        p.wait_mark(v, ("emul", fc - 1))
                    for c in range(NCH):
                        v.tensor_tensor_scan(Pb[:, c * CH:(c + 1) * CH],
                                             tA[:, c * CH:(c + 1) * CH],
                                             tA[:, c * CH:(c + 1) * CH],
                                             0.0, ALU.add, ALU.bypass)
                    v.tensor_copy(Plast[:], Pb[:, CH - 1::CH])
                    v.tensor_scalar_mul(PlastS[:], Plast[:], -1.0 / 16.0)
                    v.memset(Gx[:, 0:1], 0.0)
                    v.tensor_tensor_scan(Gx[:, 1:NCH + 1], Plast[:], Plast[:],
                                         0.0, ALU.add, ALU.bypass)
                    i = v.reduce_sum(Ptot[:], Plast[:], axis=mybir.AxisListType.X)
                    p.inc(i, "ve")
                    p.set_mark(("cum", fc), "ve")

                @block.scalar
                def _(sc, fc=fc):
                    p.wait_mark(sc, ("cum", fc))
                    if fc >= 1:
                        p.wait_mark(sc, ("qgt_done", fc - 1))
                        p.wait_mark(sc, ("kbt_done", fc - 1))
                    sc.activation(E1[:], Pb[:], AF.Exp, scale=-1.0 / 16.0)
                    sc.activation(E2[:], Pb[:], AF.Exp, scale=1.0 / 16.0)
                    for c in range(NCH):
                        sc.activation(E3[:, c * CH:(c + 1) * CH],
                                      Pb[:, c * CH:(c + 1) * CH], AF.Exp,
                                      scale=1.0 / 16.0, bias=PlastS[:, c:c + 1])
                    sc.activation(EG[:], Gx[:, 0:NCH], AF.Exp, scale=-1.0 / 16.0)
                    sc.activation(sb["DecL"][:, fc, :], Plast[:], AF.Exp, scale=-1.0 / 16.0)
                    i = sc.activation(sb["Dtot"][:, fc:fc + 1], Ptot[:], AF.Exp, scale=-1.0 / 16.0)
                    p.inc(i, "ac")
                    p.set_mark(("emul", fc), "ac")

                @block.tensor
                def _(t, fc=fc, buf=buf):
                    p.wait_mark(t, ("wqk", fc))
                    if fc >= 1:
                        p.wait_mark(t, ("qet_done", fc - 1))
                    for dc in range(DC):
                        for tt in range(4):
                            i = t.matmul(psB[:, tt * 512:(tt + 1) * 512],
                                         wq_t[buf][:, dc, :],
                                         sb["xnT"][:, dc, tt * 512:(tt + 1) * 512],
                                         start=(dc == 0), stop=(dc == DC - 1))
                    p.inc(i, "pe")
                    p.set_mark(("qproj", fc), "pe")

                @block.vector
                def _(v, fc=fc, buf=buf):
                    p.wait_mark(v, ("emul", fc))
                    p.wait_mark(v, ("qproj", fc))
                    if fc >= 2:
                        p.wait_mark(v, ("qst", fc - 2))
                    for tt in range(4):
                        i = v.scalar_tensor_tensor(
                            qet_s[buf][:, tt * 512:(tt + 1) * 512],
                            psB[:, tt * 512:(tt + 1) * 512], SCALE_GLA,
                            E1[:, tt * 512:(tt + 1) * 512], ALU.mult, ALU.mult)
                    p.inc(i, "ve")
                    p.set_mark(("qet_done", fc), "ve")
                    for c in range(NCH):
                        i = v.tensor_scalar_mul(
                            qgt_s[buf][:, c * CH:(c + 1) * CH],
                            qet_s[buf][:, c * CH:(c + 1) * CH],
                            EG[:, c:c + 1])
                    p.inc(i, "ve")
                    p.set_mark(("qgt_done", fc), "ve")

                @block.tensor
                def _(t, fc=fc, buf=buf):
                    p.wait_mark(t, ("sp", fc))
                    p.wait_mark(t, ("wqk", fc))
                    if fc >= 1:
                        p.wait_mark(t, ("kbt_done", fc - 1))
                    for dc in range(DC):
                        for tt in range(4):
                            i = t.matmul(psA[:, tt * 512:(tt + 1) * 512],
                                         wk_t[buf][:, dc, :],
                                         sb["xnT"][:, dc, tt * 512:(tt + 1) * 512],
                                         start=(dc == 0), stop=(dc == DC - 1))
                    p.inc(i, "pe")
                    p.set_mark(("kproj", fc), "pe")

                @block.vector
                def _(v, fc=fc, buf=buf):
                    p.wait_mark(v, ("kproj", fc))
                    if fc >= 2:
                        p.wait_mark(v, ("qst", fc - 2))
                    for tt in range(4):
                        v.tensor_mul(kdt_s[buf][:, tt * 512:(tt + 1) * 512],
                                     psA[:, tt * 512:(tt + 1) * 512],
                                     E2[:, tt * 512:(tt + 1) * 512])
                    for tt in range(4):
                        i = v.tensor_mul(kbt_s[buf][:, tt * 512:(tt + 1) * 512],
                                         psA[:, tt * 512:(tt + 1) * 512],
                                         E3[:, tt * 512:(tt + 1) * 512])
                    p.inc(i, "ve")
                    p.set_mark(("kbt_done", fc), "ve")
                    p.set_mark(("kproj_done_v", fc), "ve")

                @block.sync
                def _(s, fc=fc, buf=buf):
                    p.wait_mark(s, ("qgt_done", fc))
                    i = s.dma_start(out=qet_d[fc * 128:(fc + 1) * 128, :], in_=qet_s[buf][:])
                    p.dma(i, "st_q%d" % buf)
                    i = s.dma_start(out=qgt_o[fc * 128:(fc + 1) * 128, :], in_=qgt_s[buf][:])
                    p.dma(i, "st_q%d" % buf)
                    p.wait_mark(s, ("kbt_done", fc))
                    i = s.dma_start(out=kdt_d[fc * 128:(fc + 1) * 128, :], in_=kdt_s[buf][:])
                    p.dma(i, "st_q%d" % buf)
                    i = s.dma_start(out=kbt_d[fc * 128:(fc + 1) * 128, :], in_=kbt_s[buf][:])
                    p.dma(i, "st_q%d" % buf)
                    p.set_mark(("qst", fc), "st_q%d" % buf)
                    p.set_mark(("qk_mm", fc), "st_q%d" % buf)

            barrier(p, block)

        with contextlib.ExitStack() as lctx:
            L = lctx.enter_context
            wv_b = L(nc.sbuf_tensor("wv_b", [128, DC, 1024], BF16))
            wg_b = L(nc.sbuf_tensor("wg_b", [128, DC, 1024], BF16))
            v_s = [L(nc.sbuf_tensor(f"v_s{i}", [128, 1024], BF16)) for i in range(2)]
            g_s = [L(nc.sbuf_tensor(f"g_s{i}", [128, 1024], BF16)) for i in range(2)]
            psV = [L(nc.psum_tensor(f"psV{i}", [128, 1024], F32)) for i in range(2)]
            psG = [L(nc.psum_tensor(f"psG{i}", [128, 1024], F32)) for i in range(2)]

            @block.gpsimd
            def _(g):
                i = g.dma_start(out=wv_b[:], in_=Wv)
                p.dma(i, "dg_wvg")
                i = g.dma_start(out=wg_b[:], in_=Wg)
                p.dma(i, "dg_wvg")
                p.set_mark("wvg", "dg_wvg")

            for tt in range(NT):
                buf = tt % 2
                @block.tensor
                def _(t, tt=tt, buf=buf):
                    p.wait_mark(t, "wvg")
                    if tt >= 2:
                        p.wait_mark(t, ("vg_evac", tt - 2))
                    for half in range(2):
                        for dc in range(DC):
                            t.matmul(psV[buf][:, half * 512:(half + 1) * 512],
                                     sb["xnT"][:, dc, tt * 128:(tt + 1) * 128],
                                     wv_b[:, dc, half * 512:(half + 1) * 512],
                                     start=(dc == 0), stop=(dc == DC - 1))
                    for half in range(2):
                        for dc in range(DC):
                            i = t.matmul(psG[buf][:, half * 512:(half + 1) * 512],
                                         sb["xnT"][:, dc, tt * 128:(tt + 1) * 128],
                                         wg_b[:, dc, half * 512:(half + 1) * 512],
                                         start=(dc == 0), stop=(dc == DC - 1))
                    p.inc(i, "pe")
                    p.set_mark(("vg_mm", tt), "pe")

                @block.vector
                def _(v, tt=tt, buf=buf):
                    p.wait_mark(v, ("vg_mm", tt))
                    if tt >= 2:
                        p.wait_mark(v, ("vg_st", tt - 2))
                    v.tensor_copy(v_s[buf][:], psV[buf][:])
                    i = v.tensor_copy(g_s[buf][:], psG[buf][:])
                    p.inc(i, "ve")
                    p.set_mark(("vg_evac", tt), "ve")

                @block.sync
                def _(s, tt=tt, buf=buf):
                    p.wait_mark(s, ("vg_evac", tt))
                    i = s.dma_start(out=v_d[tt * 128:(tt + 1) * 128, :], in_=v_s[buf][:])
                    p.dma(i, "st_vg%d" % buf)
                    i = s.dma_start(out=gtok_o[tt * 128:(tt + 1) * 128, :], in_=g_s[buf][:])
                    p.dma(i, "st_vg%d" % buf)
                    p.set_mark(("vg_st", tt), "st_vg%d" % buf)
            barrier(p, block)

        ctx2.__exit__(None, None, None)

        with contextlib.ExitStack() as lctx:
            L = lctx.enter_context
            S = L(nc.sbuf_tensor("S", [128, 16, 256], F32))
            S_bf = L(nc.sbuf_tensor("S_bf", [128, 16, 256], BF16))
            qet_bl = [L(nc.sbuf_tensor(f"qet_bl{i}", [128, 16, 512], BF16)) for i in range(2)]
            kdt_bl = [L(nc.sbuf_tensor(f"kdt_bl{i}", [128, 16, 512], BF16)) for i in range(2)]
            kbt_bl = [L(nc.sbuf_tensor(f"kbt_bl{i}", [128, 16, 512], BF16)) for i in range(2)]
            v_bl = [L(nc.sbuf_tensor(f"v_bl{i}", [128, 4, 1024], BF16)) for i in range(2)]
            oacc_bl = [L(nc.sbuf_tensor(f"oacc_bl{i}", [128, 4, 1024], F32)) for i in range(2)]
            Am = [L(nc.sbuf_tensor(f"Am{i}", [128, 128], BF16)) for i in range(2)]
            kbts = [L(nc.sbuf_tensor(f"kbts{i}", [128, 4, 128], BF16)) for i in range(2)]
            pa = [L(nc.psum_tensor(f"pa{i}", [128, 128], F32)) for i in range(2)]
            po = [L(nc.psum_tensor(f"po{i}", [128, 256], F32)) for i in range(2)]
            pt = [L(nc.psum_tensor(f"pt{i}", [128, 4, 128], BF16)) for i in range(2)]
            pu = L(nc.psum_tensor("pu", [128, 4, 256], F32))

            @block.vector
            def _(v):
                v.memset(S[:], 0.0)
                i = v.memset(S_bf[:], 0.0)
                p.inc(i, "ve")
                p.set_mark("S0", "ve")

            NBL = 4
            NITER = NBL * 4 * H

            def emit_blk_load(bl):
                bbuf = bl % 2
                @block.sync
                def _(s, bl=bl, bbuf=bbuf):
                    if bl < 2:
                        for nm in ("st_q0", "st_q1", "st_vg0", "st_vg1"):
                            p.wait(s, nm)
                    else:
                        p.wait_mark(s, ("blk_done", bl - 2))
                        p.wait_mark(s, ("supd", (bl - 2) * 4 * H + 4 * H - 1))
                    sl = slice(bl * 512, (bl + 1) * 512)
                    i = s.dma_start(out=qet_bl[bbuf][:],
                                    in_=qet_d.rearrange("(f p) t -> p f t", p=128)[:, :, sl])
                    p.dma(i, "ld_blk%d" % bbuf)
                    i = s.dma_start(out=kdt_bl[bbuf][:],
                                    in_=kdt_d.rearrange("(f p) t -> p f t", p=128)[:, :, sl])
                    p.dma(i, "ld_blk%d" % bbuf)
                    i = s.dma_start(out=kbt_bl[bbuf][:],
                                    in_=kbt_d.rearrange("(f p) t -> p f t", p=128)[:, :, sl])
                    p.dma(i, "ld_blk%d" % bbuf)
                    i = s.dma_start(out=v_bl[bbuf][:],
                                    in_=v_d[bl * 512:(bl + 1) * 512, :].rearrange("(c p) f -> p c f", p=128))
                    p.dma(i, "ld_blk%d" % bbuf)
                    p.set_mark(("blk_ld", bl), "ld_blk%d" % bbuf)

            def emit_A(it):
                bl, ci, h = it // 16, (it // H) % 4, it % H
                bbuf, ib = bl % 2, it % 2
                @block.tensor
                def _(t, bl=bl, bbuf=bbuf, ci=ci, h=h, it=it, ib=ib):
                    p.wait_mark(t, ("blk_ld", bl))
                    if it == 0:
                        p.wait_mark(t, "S0")
                        p.wait(t, "ge")
                    if it >= 2:
                        p.wait_mark(t, ("mask", it - 2))
                    for dkc in range(4):
                        i = t.matmul(pa[ib][:],
                                     kdt_bl[bbuf][:, h * 4 + dkc, ci * 128:(ci + 1) * 128],
                                     qet_bl[bbuf][:, h * 4 + dkc, ci * 128:(ci + 1) * 128],
                                     start=(dkc == 0), stop=(dkc == 3))
                    p.inc(i, "pe")
                    p.set_mark(("A", it), "pe")

                @block.vector
                def _(v, it=it, ib=ib):
                    p.wait_mark(v, ("A", it))
                    i = v.tensor_mul(Am[ib][:], pa[ib][:], sb["mask"][:])
                    p.inc(i, "ve")
                    p.set_mark(("mask", it), "ve")

            def emit_oT(it):
                bl, ci, h = it // 16, (it // H) % 4, it % H
                bbuf, ib = bl % 2, it % 2
                @block.tensor
                def _(t, bl=bl, bbuf=bbuf, ci=ci, h=h, it=it, ib=ib):
                    p.wait_mark(t, ("mask", it))
                    if it >= H:
                        p.wait_mark(t, ("supd", it - H))
                    if it >= 2:
                        p.wait_mark(t, ("oev", it - 2))
                    i = t.matmul(po[ib][:], Am[ib][:],
                                 v_bl[bbuf][:, ci, h * 256:(h + 1) * 256],
                                 start=True, stop=False)
                    for dkc in range(4):
                        i = t.matmul(po[ib][:],
                                     qet_bl[bbuf][:, h * 4 + dkc, ci * 128:(ci + 1) * 128],
                                     S_bf[:, h * 4 + dkc, :],
                                     start=False, stop=(dkc == 3))
                    p.inc(i, "pe")
                    p.set_mark(("o", it), "pe")
                    if it >= 2:
                        p.wait_mark(t, ("tev", it - 2))
                    for dkc in range(4):
                        i = t.transpose(pt[ib][:, dkc, :],
                                        kbt_bl[bbuf][:, h * 4 + dkc, ci * 128:(ci + 1) * 128],
                                        sb["ident_b"][:])
                    p.inc(i, "pe")
                    p.set_mark(("T", it), "pe")

                @block.vector
                def _(v, it=it, ib=ib):
                    p.wait_mark(v, ("T", it))
                    i = v.tensor_copy(kbts[ib][:], pt[ib][:])
                    p.inc(i, "ve")
                    p.set_mark(("tev", it), "ve")

            def emit_U(it):
                bl, ci, h = it // 16, (it // H) % 4, it % H
                bbuf, ib = bl % 2, it % 2
                c_glob = it // H
                @block.tensor
                def _(t, bl=bl, bbuf=bbuf, ci=ci, h=h, it=it, ib=ib):
                    p.wait_mark(t, ("tev", it))
                    if it >= 1:
                        p.wait_mark(t, ("supd", it - 1))
                    for dkc in range(4):
                        i = t.matmul(pu[:, dkc, :], kbts[ib][:, dkc, :],
                                     v_bl[bbuf][:, ci, h * 256:(h + 1) * 256],
                                     start=True, stop=True)
                    p.inc(i, "pe")
                    p.set_mark(("U", it), "pe")

                @block.vector
                def _(v, bl=bl, ci=ci, h=h, it=it, ib=ib, c_glob=c_glob, bbuf=bbuf):
                    p.wait_mark(v, ("U", it))
                    for dkc in range(4):
                        v.scalar_tensor_tensor(
                            S[:, h * 4 + dkc, :], S[:, h * 4 + dkc, :],
                            sb["DecL"][:, h * 4 + dkc, c_glob:c_glob + 1],
                            pu[:, dkc, :], ALU.mult, ALU.add)
                    i = v.tensor_copy(S_bf[:, h * 4:h * 4 + 4, :], S[:, h * 4:h * 4 + 4, :])
                    p.inc(i, "ve")
                    p.set_mark(("supd", it), "ve")
                    p.wait_mark(v, ("o", it))
                    if ci == 0 and h == 0 and bl >= 2:
                        p.wait_mark(v, ("blk_done", bl - 2))
                    i = v.tensor_copy(oacc_bl[bbuf][:, ci, h * 256:(h + 1) * 256], po[ib][:])
                    p.inc(i, "ve")
                    p.set_mark(("oev", it), "ve")

                if it % 16 == 15:
                    @block.sync
                    def _(s, bl=bl, bbuf=bbuf):
                        p.wait_mark(s, ("oev", bl * 16 + 15))
                        i = s.dma_start(
                            out=oacc_o[bl * 512:(bl + 1) * 512, :].rearrange("(c p) f -> p c f", p=128),
                            in_=oacc_bl[bbuf][:])
                        p.dma(i, "st_oacc%d" % bbuf)
                        p.set_mark(("blk_done", bl), "st_oacc%d" % bbuf)

            for step in range(NITER + 2):
                if step < NITER and step % 16 == 0:
                    emit_blk_load(step // 16)
                if step < NITER:
                    emit_A(step)
                if 1 <= step <= NITER:
                    emit_oT(step - 1)
                if step >= 2:
                    emit_U(step - 2)

            @block.sync
            def _(s):
                p.wait_mark(s, ("supd", NBL * 4 * H - 1))
                i = s.dma_start(out=sloc_o.rearrange("f p v -> p f v"), in_=S[:])
                p.dma(i, "st_fin")
                i = s.dma_start(out=dtot_o, in_=sb["Dtot"][:])
                p.dma(i, "st_fin")

            @block.sync
            def _(s):
                for nm in ("st_q0", "st_q1", "st_vg0", "st_vg1",
                           "st_oacc0", "st_oacc1", "st_fin"):
                    p.wait(s, nm)
            block.flush()
    return nc


def emit_setup_consts2(p, block, sb):
    @block.gpsimd
    def _(g):
        g.memset(sb["cb"][:, 0:1], EPS)
        g.memset(sb["cb"][:, 1:2], 1.0)
        g.memset(sb["cbm128"][:], -128.0)
        g.memset(sb["ones_f1"][:], 1.0)
        g.memset(sb["ones_bf1"][:], 1.0)
        g.memset(sb["ones_f"][:], 1.0)
        if "ones_b" in sb:
            g.memset(sb["ones_b"][:], 1.0)
        i = g.memset(sb["ident_f"][:], 0.0)
        p.wait(g, "ge", p.cnt["ge"])
        g.affine_select(out=sb["ident_f"][:], in_=sb["ident_f"][:],
                        compare_op=ALU.not_equal, fill=1.0, base=0,
                        pattern=[[-1, 128]], channel_multiplier=1)
        i = g.memset(sb["ident_b"][:], 0.0)
        p.wait(g, "ge", p.cnt["ge"])
        i = g.affine_select(out=sb["ident_b"][:], in_=sb["ident_b"][:],
                            compare_op=ALU.not_equal, fill=1.0, base=0,
                            pattern=[[-1, 128]], channel_multiplier=1)
        p.set_mark("consts", "ge")



def emit_bcast_row(p, block, sb, name, row_sb, n, psname, ld_mark=None):
    nt = (n + 511) // 512
    for j in range(nt):
        w = min(512, n - j * 512)
        @block.tensor
        def _(t, j=j, w=w):
            if j == 0:
                p.wait(t, "ge")
                p.wait(t, "ve")
                if ld_mark is not None:
                    p.wait_mark(t, ld_mark)
            else:
                p.wait_mark(t, (name, "ev", j - 1))
            i = t.matmul(sb[psname][:, 0:w], sb["ones_f1"][:],
                         row_sb[:, j * 512:j * 512 + w], start=True, stop=True)
            p.inc(i, "pe")
            p.set_mark((name, "mm", j), "pe")

        @block.vector
        def _(v, j=j, w=w):
            p.wait_mark(v, (name, "mm", j))
            v.tensor_copy(sb[name][:, j * 512:j * 512 + w], sb[psname][:, 0:w])
            p.set_mark((name, "ev", j), "ve")
    p.set_mark((name, "done"), "ve")


def emit_store_xres(p, block, sb, dst, tag):
    @block.sync
    def _(s):
        p.wait_mark(s, ("xres_done", tag))
        for tt in range(NT):
            i = s.dma_start(out=dst[tt * 128:(tt + 1) * 128, :],
                            in_=sb["xres"][:, tt, :])
            p.dma(i, "st_x" + tag)
    p.set_mark(("xstore", tag), "st_x" + tag)


def emit_ffn(p, block, nc, sb, xn_dram, x_dram, x_dst, xn_dst, wgT, wuT, wdT, tag):
    h_d = nc.dram_tensor("h_d" + tag, [FFD, TOK], BF16).ap()

    with contextlib.ExitStack() as gctx:
        G = gctx.enter_context
        xnT = G(nc.sbuf_tensor("ffn_xnT" + tag, [128, DC, TOK], BF16))
        w32 = [G(nc.sbuf_tensor(f"ffn_w32_{i}{tag}", [128, DC, 128], U8)) for i in range(2)]
        wu32 = [G(nc.sbuf_tensor(f"ffn_wu32_{i}{tag}", [128, DC, 128], U8)) for i in range(2)]
        wgb = [G(nc.sbuf_tensor(f"ffn_wgb{i}{tag}", [128, DC, 128], BF16)) for i in range(2)]
        wub = [G(nc.sbuf_tensor(f"ffn_wub{i}{tag}", [128, DC, 128], BF16)) for i in range(2)]
        gsil = [G(nc.sbuf_tensor(f"ffn_gs{i}{tag}", [128, TOK], BF16)) for i in range(2)]
        upd = [G(nc.sbuf_tensor(f"ffn_up{i}{tag}", [128, TOK], BF16)) for i in range(2)]
        ht = [G(nc.sbuf_tensor(f"ffn_ht{i}{tag}", [128, TOK], BF16)) for i in range(2)]
        psG = G(nc.psum_tensor("ffn_psG" + tag, [128, TOK], F32))
        psU = G(nc.psum_tensor("ffn_psU" + tag, [128, TOK], F32))

        @block.scalar
        def _(s):
            if ("lnst", NT // 2 - 2) in p.mark:
                p.wait_mark(s, ("lnst", NT // 2 - 2))
                p.wait_mark(s, ("lnst", NT // 2 - 1))
            for dc in range(DC):
                i = s.dma_start_transpose(
                    xnT[:, dc, 0:TOK // 2],
                    xn_dram[0:TOK // 2, dc * 128:(dc + 1) * 128])
                p.dma(i, "ds_ffnxnt" + tag)
            for nm in list(p.sem):
                if nm.startswith(("st_lnxn", "st_fxn")):
                    p.wait(s, nm)
            for dc in range(DC):
                i = s.dma_start_transpose(
                    xnT[:, dc, TOK // 2:TOK],
                    xn_dram[TOK // 2:TOK, dc * 128:(dc + 1) * 128])
                p.dma(i, "ds_ffnxnt" + tag)
        p.set_mark(("ffn_xnT", tag), "ds_ffnxnt" + tag)

        for fc in range(FC):
            buf = fc % 2
            @block.sync
            def _(s, fc=fc, buf=buf):
                if fc >= 2:
                    p.wait_mark(s, ("ffn_gmm", tag, fc - 2))
                i = s.dma_start(out=w32[buf][:], in_=wgT[fc])
                p.dma(i, "dg_ffnw%d%s" % (fc % 4, tag))
                p.set_mark(("ffn_wg32", tag, fc), "dg_ffnw%d%s" % (fc % 4, tag))

            @block.scalar
            def _(sc, fc=fc, buf=buf):
                p.wait_mark(sc, ("ffn_wg32", tag, fc))
                if fc >= 2:
                    p.wait_mark(sc, ("ffn_gmm", tag, fc - 2))
                sc.activation(wgb[buf][:], w32[buf][:], AF.Copy, bias=-128.0)
                p.set_mark(("ffn_wgb", tag, fc), "ac")

            @block.sync
            def _(s, fc=fc, buf=buf):
                if fc >= 2:
                    p.wait_mark(s, ("ffn_wub", tag, fc - 2))
                i = s.dma_start(out=wu32[buf][:], in_=wuT[fc])
                p.dma(i, "dg_ffnu%d%s" % (fc % 4, tag))
                p.set_mark(("ffn_wu32", tag, fc), "dg_ffnu%d%s" % (fc % 4, tag))

            @block.scalar
            def _(sc, fc=fc, buf=buf):
                p.wait_mark(sc, ("ffn_wu32", tag, fc))
                if fc >= 2:
                    p.wait_mark(sc, ("ffn_gmm", tag, fc - 2))
                sc.activation(wub[buf][:], wu32[buf][:], AF.Copy, bias=-128.0)
                p.set_mark(("ffn_wub", tag, fc), "ac")

            @block.tensor
            def _(t, fc=fc, buf=buf):
                p.wait_mark(t, ("ffn_wgb", tag, fc))
                p.wait_mark(t, ("ffn_xnT", tag))
                if fc >= 2:
                    p.wait_mark(t, ("ffn_upe", tag, fc - 2))
                for dc in range(DC):
                    for tt in range(4):
                        i = t.matmul(psG[:, tt * 512:(tt + 1) * 512],
                                     wgb[buf][:, dc, :],
                                     xnT[:, dc, tt * 512:(tt + 1) * 512],
                                     start=(dc == 0), stop=(dc == DC - 1))
                p.inc(i, "pe")
                p.set_mark(("ffn_gatemm", tag, fc), "pe")
                p.wait_mark(t, ("ffn_wub", tag, fc))
                for dc in range(DC):
                    for tt in range(4):
                        i = t.matmul(psU[:, tt * 512:(tt + 1) * 512],
                                     wub[buf][:, dc, :],
                                     xnT[:, dc, tt * 512:(tt + 1) * 512],
                                     start=(dc == 0), stop=(dc == DC - 1))
                p.inc(i, "pe")
                p.set_mark(("ffn_gmm", tag, fc), "pe")

            @block.scalar
            def _(sc, fc=fc, buf=buf):
                p.wait_mark(sc, ("ffn_gatemm", tag, fc))
                if fc >= 2:
                    p.wait_mark(sc, ("ffn_hmul", tag, fc - 2))
                sc.activation(gsil[buf][:], psG[:], AF.Silu,
                              scale=sb["sg_pp"][:, fc:fc + 1])
                p.set_mark(("ffn_sil", tag, fc), "ac")
                p.wait_mark(sc, ("ffn_gmm", tag, fc))
                sc.activation(upd[buf][:], psU[:], AF.Copy,
                              scale=sb["su_pp"][:, fc:fc + 1])
                p.set_mark(("ffn_upe", tag, fc), "ac")

            @block.vector
            def _(v, fc=fc, buf=buf):
                p.wait_mark(v, ("ffn_upe", tag, fc))
                if fc >= 2:
                    p.wait_mark(v, ("ffn_hst", tag, fc - 2))
                i = v.tensor_mul(ht[buf][:], gsil[buf][:], upd[buf][:])
                p.inc(i, "ve")
                p.set_mark(("ffn_hmul", tag, fc), "ve")

            @block.sync
            def _(s, fc=fc, buf=buf):
                p.wait_mark(s, ("ffn_hmul", tag, fc))
                i = s.dma_start(out=h_d[fc * 128:(fc + 1) * 128, :], in_=ht[buf][:])
                p.dma(i, "st_ffnh%d%s" % (buf, tag))
                p.set_mark(("ffn_hst", tag, fc), "st_ffnh%d%s" % (buf, tag))

        barrier(p, block)

    NH = NT // 2
    with contextlib.ExitStack() as dctx:
        G = dctx.enter_context
        xrh = G(nc.sbuf_tensor("ffn_xrh" + tag, [128, NH, D], F32))
        wd32 = [G(nc.sbuf_tensor(f"ffn_wd32_{i}{tag}", [128, D], U8)) for i in range(2)]
        wdbA = G(nc.sbuf_tensor("ffn_wdbA" + tag, [128, 6, D], BF16))
        wdbB = G(nc.sbuf_tensor("ffn_wdbB" + tag, [128, 5, D], BF16))
        hts = [G(nc.sbuf_tensor(f"ffn_hts{i}{tag}", [128, 11, 128], BF16)) for i in range(2)]
        tflu = G(nc.sbuf_tensor("ffn_tflu" + tag, [128, D], F32))
        psD = [G(nc.psum_tensor(f"ffn_psD{i}{tag}", [128, D], F32)) for i in range(2)]

        for hf in range(2):
            @block.sync
            def _(s, hf=hf):
                if hf == 0:
                    for nm in list(p.sem):
                        if nm.startswith(("st_x1", "st_x3")):
                            p.wait(s, nm)
                else:
                    p.wait(s, "st_xo%s0" % tag)
                    p.wait(s, "st_xo%s1" % tag)
                    if xn_dst is not None:
                        p.wait_mark(s, ("fl2xn", tag, NH - 1))
                for tt in range(NH):
                    r = (hf * NH + tt) * 128
                    i = s.dma_start(out=xrh[:, tt, :], in_=x_dram[r:r + 128, :])
                    p.dma(i, "ld_x" + tag)
                p.set_mark(("ffn_xld", tag, hf), "ld_x" + tag)

            for g_i in range(4):
                for j in range(11):
                    hc = g_i * 11 + j
                    gj = hf * 44 + g_i * 11 + j
                    @block.sync
                    def _(s, g_i=g_i, j=j, hc=hc, gj=gj, hf=hf):
                        if gj >= 2:
                            p.wait_mark(s, ("ffn_wdb", tag, gj - 2))
                        i = s.dma_start(out=wd32[gj % 2][:],
                                        in_=wdT.rearrange("(c p) f -> p c f", p=128)[:, hc, :])
                        p.dma(i, "dg_ffnwd%d%s" % (gj % 4, tag))
                        p.set_mark(("ffn_wd32", tag, gj), "dg_ffnwd%d%s" % (gj % 4, tag))

                    @block.scalar
                    def _(sc, gj=gj, j=j, hf=hf, g_i=g_i):
                        p.wait_mark(sc, ("ffn_wd32", tag, gj))
                        gg = hf * 4 + g_i
                        if gg > 0:
                            if j == 0:
                                p.wait_mark(sc, ("ffn_dmmA", tag, gg * NH - 1))
                            elif j == 6:
                                p.wait_mark(sc, ("ffn_dmm", tag, gg * NH - 1))
                        if j < 6:
                            sc.activation(wdbA[:, j, :], wd32[gj % 2][:],
                                          AF.Copy, bias=-128.0)
                        else:
                            sc.activation(wdbB[:, j - 6, :], wd32[gj % 2][:],
                                          AF.Copy, bias=-128.0)
                        p.set_mark(("ffn_wdb", tag, gj), "ac")
                        if j == 5:
                            p.set_mark(("ffn_wdbA", tag, gg), "ac")
                        elif j == 10:
                            p.set_mark(("ffn_wdbB", tag, gg), "ac")

                for tt in range(NH):
                    dbuf = tt % 2
                    gt = (hf * 4 + g_i) * NH + tt
                    @block.sync
                    def _(s, g_i=g_i, tt=tt, dbuf=dbuf, gt=gt, hf=hf):
                        if gt >= 2:
                            p.wait_mark(s, ("ffn_dmm", tag, gt - 2))
                        i = s.dma_start(
                            out=hts[dbuf][:],
                            in_=h_d.rearrange("(f p) t -> p f t", p=128)[
                                :, g_i * 11:(g_i + 1) * 11,
                                (hf * NH + tt) * 128:(hf * NH + tt + 1) * 128])
                        p.dma(i, "ld_ffnh%d%s" % (gt % 4, tag))
                        p.set_mark(("ffn_htld", tag, gt), "ld_ffnh%d%s" % (gt % 4, tag))

                    @block.tensor
                    def _(t, g_i=g_i, tt=tt, dbuf=dbuf, gt=gt, hf=hf):
                        gg = hf * 4 + g_i
                        p.wait_mark(t, ("ffn_htld", tag, gt))
                        p.wait_mark(t, ("ffn_wdbA", tag, gg))
                        if gt >= 2:
                            p.wait_mark(t, ("ffn_dfl", tag, gt - 2))
                        for j in range(6):
                            for ft in range(4):
                                i = t.matmul(psD[dbuf][:, ft * 512:(ft + 1) * 512],
                                             hts[dbuf][:, j, :],
                                             wdbA[:, j, ft * 512:(ft + 1) * 512],
                                             start=(j == 0), stop=False)
                        p.inc(i, "pe")
                        p.set_mark(("ffn_dmmA", tag, gt), "pe")
                        p.wait_mark(t, ("ffn_wdbB", tag, gg))
                        for j in range(5):
                            for ft in range(4):
                                i = t.matmul(psD[dbuf][:, ft * 512:(ft + 1) * 512],
                                             hts[dbuf][:, 6 + j, :],
                                             wdbB[:, j, ft * 512:(ft + 1) * 512],
                                             start=False, stop=(j == 4))
                        p.inc(i, "pe")
                        p.set_mark(("ffn_dmm", tag, gt), "pe")

                    @block.vector
                    def _(v, g_i=g_i, tt=tt, dbuf=dbuf, gt=gt, hf=hf):
                        p.wait_mark(v, ("ffn_dmm", tag, gt))
                        if g_i == 0:
                            p.wait_mark(v, ("ffn_xld", tag, hf))
                        v.tensor_mul(tflu[:], psD[dbuf][:], sb["sdb"][:])
                        v.tensor_add(xrh[:, tt, :], xrh[:, tt, :], tflu[:])
                        p.set_mark(("ffn_dfl", tag, gt), "ve")

            for tt in range(NH):
                buf = tt % 2
                r = (hf * NH + tt) * 128
                @block.sync
                def _(s, tt=tt, hf=hf, r=r):
                    p.wait_mark(s, ("ffn_dfl", tag, (hf * 4 + 3) * NH + tt))
                    i = s.dma_start(out=x_dst[r:r + 128, :], in_=xrh[:, tt, :])
                    p.dma(i, "st_xo%s%d" % (tag, tt % 2))
                    p.set_mark(("ffn_xst", tag, hf, tt), "st_xo%s%d" % (tag, tt % 2))

                if xn_dst is None:
                    continue

                @block.scalar
                def _(sc, tt=tt, buf=buf, hf=hf):
                    p.wait_mark(sc, ("ffn_dfl", tag, (hf * 4 + 3) * NH + tt))
                    gidx = hf * NH + tt
                    if gidx >= 2:
                        p.wait_mark(sc, ("fl2var", tag, gidx - 2))
                    sc.activation(sb["lnscr"][:], xrh[:, tt, :], AF.Square,
                                  accum_out=sb["lnss"][buf][:])
                    p.set_mark(("fl2sq", tag, gidx), "ac")

                @block.vector
                def _(v, tt=tt, buf=buf, hf=hf):
                    gidx = hf * NH + tt
                    v.reduce_sum(sb["lnsum"][buf][:], xrh[:, tt, :],
                                 axis=mybir.AxisListType.X)
                    v.tensor_scalar_mul(sb["lnm"][buf][:], sb["lnsum"][buf][:], 1.0 / D)
                    v.tensor_mul(sb["lnm2"][buf][:], sb["lnm"][buf][:], sb["lnm"][buf][:])
                    p.wait_mark(v, ("fl2sq", tag, gidx))
                    v.tensor_scalar(sb["lnvar"][buf][:], sb["lnss"][buf][:],
                                    1.0 / D, None, ALU.mult)
                    v.tensor_sub(sb["lnvar"][buf][:], sb["lnvar"][buf][:],
                                 sb["lnm2"][buf][:])
                    p.set_mark(("fl2var", tag, gidx), "ve")

                @block.scalar
                def _(sc, tt=tt, buf=buf, hf=hf):
                    gidx = hf * NH + tt
                    p.wait_mark(sc, ("fl2var", tag, gidx))
                    if gidx >= 2:
                        p.wait_mark(sc, ("fl2xn", tag, gidx - 2))
                    sc.activation(sb["lnrs"][buf][:], sb["lnvar"][buf][:], AF.Ln,
                                  bias=sb["cb"][:, 0:1])
                    sc.activation(sb["lnrs"][buf][:], sb["lnrs"][buf][:], AF.Exp,
                                  scale=-0.5)
                    p.set_mark(("fl2rs", tag, gidx), "ac")

                @block.vector
                def _(v, tt=tt, buf=buf, hf=hf):
                    gidx = hf * NH + tt
                    p.wait_mark(v, ("fl2rs", tag, gidx))
                    if gidx >= 2:
                        p.wait_mark(v, ("fl2st", tag, gidx - 2))
                    v.tensor_scalar(sb["lnxn"][buf][:], xrh[:, tt, :],
                                    sb["lnm"][buf][:], sb["lnrs"][buf][:],
                                    ALU.subtract, ALU.mult)
                    p.set_mark(("fl2xn", tag, gidx), "ve")

                @block.sync
                def _(s, tt=tt, buf=buf, hf=hf, r=r):
                    gidx = hf * NH + tt
                    p.wait_mark(s, ("fl2xn", tag, gidx))
                    i = s.dma_start(out=xn_dst[r:r + 128, :], in_=sb["lnxn"][buf][:])
                    p.dma(i, "st_fxn%s%d" % (tag, buf))
                    p.set_mark(("fl2st", tag, gidx), "st_fxn%s%d" % (tag, buf))

        barrier(p, block)


def emit_ln_sbuf(p, block, sb, xn_dst, tag):
    for t in range(NT):
        buf = t % 2
        @block.scalar
        def _(sc, t=t, buf=buf):
            if t == 0:
                p.wait_mark(sc, ("xres_done", tag))
            if t >= 2:
                p.wait_mark(sc, ("l2var", tag, t - 2))
            sc.activation(sb["lnscr"][:], sb["xres"][:, t, :], AF.Square,
                          accum_out=sb["lnss"][buf][:])
            p.set_mark(("l2sq", tag, t), "ac")

        @block.vector
        def _(v, t=t, buf=buf):
            v.reduce_sum(sb["lnsum"][buf][:], sb["xres"][:, t, :],
                         axis=mybir.AxisListType.X)
            v.tensor_scalar_mul(sb["lnm"][buf][:], sb["lnsum"][buf][:], 1.0 / D)
            v.tensor_mul(sb["lnm2"][buf][:], sb["lnm"][buf][:], sb["lnm"][buf][:])
            p.wait_mark(v, ("l2sq", tag, t))
            v.tensor_scalar(sb["lnvar"][buf][:], sb["lnss"][buf][:],
                            1.0 / D, None, ALU.mult)
            v.tensor_sub(sb["lnvar"][buf][:], sb["lnvar"][buf][:], sb["lnm2"][buf][:])
            p.set_mark(("l2var", tag, t), "ve")

        @block.scalar
        def _(sc, t=t, buf=buf):
            p.wait_mark(sc, ("l2var", tag, t))
            if t >= 2:
                p.wait_mark(sc, ("l2xn", tag, t - 2))
            sc.activation(sb["lnrs"][buf][:], sb["lnvar"][buf][:], AF.Ln,
                          bias=sb["cb"][:, 0:1])
            sc.activation(sb["lnrs"][buf][:], sb["lnrs"][buf][:], AF.Exp, scale=-0.5)
            p.set_mark(("l2rs", tag, t), "ac")

        @block.vector
        def _(v, t=t, buf=buf):
            p.wait_mark(v, ("l2rs", tag, t))
            if t >= 2:
                p.wait_mark(v, ("l2st", tag, t - 2))
            v.tensor_scalar(sb["lnxn"][buf][:], sb["xres"][:, t, :],
                            sb["lnm"][buf][:], sb["lnrs"][buf][:],
                            ALU.subtract, ALU.mult)
            p.set_mark(("l2xn", tag, t), "ve")

        @block.sync
        def _(s, t=t, buf=buf):
            p.wait_mark(s, ("l2xn", tag, t))
            i = s.dma_start(out=xn_dst[t * 128:(t + 1) * 128, :], in_=sb["lnxn"][buf][:])
            p.dma(i, "st_l2xn%d%s" % (buf, tag))
            p.set_mark(("l2st", tag, t), "st_l2xn%d%s" % (buf, tag))



def build_launch2():
    nc = bass.Bass("TRN2", target_bir_lowering=False)
    xsh = nc.dram_tensor("xsh", [TOK, D], F32, kind="ExternalInput").ap()
    oacc_i = nc.dram_tensor("oacc", [TOK, 1024], F32, kind="ExternalInput").ap()
    qgt_i = nc.dram_tensor("qgt", [NT, 128, 16, 128], BF16, kind="ExternalInput").ap()
    gtok_i = nc.dram_tensor("gtok", [TOK, 1024], BF16, kind="ExternalInput").ap()
    psl = [nc.dram_tensor(f"psl{j}", [16, 128, 256], F32, kind="ExternalInput").ap()
           for j in range(3)]
    pdt = [nc.dram_tensor(f"pdt{j}", [128, 16], F32, kind="ExternalInput").ap()
           for j in range(3)]
    wnr = nc.dram_tensor("wnr", [1, 1024], F32, kind="ExternalInput").ap()
    Wo = nc.dram_tensor("Wo", [128, 8, D], F32, kind="ExternalInput").ap()
    wgT = nc.dram_tensor("wgT", [FC, 128, 16, 128], U8, kind="ExternalInput").ap()
    wuT = nc.dram_tensor("wuT", [FC, 128, 16, 128], U8, kind="ExternalInput").ap()
    wdT = nc.dram_tensor("wdT", [FFD, D], U8, kind="ExternalInput").ap()
    sgr = nc.dram_tensor("sgr", [FC, 128], F32, kind="ExternalInput").ap()
    sur = nc.dram_tensor("sur", [FC, 128], F32, kind="ExternalInput").ap()
    sdr = nc.dram_tensor("sdr", [1, D], F32, kind="ExternalInput").ap()
    WqkvT = nc.dram_tensor("WqkvT", [48, 128, 16, 128], F32, kind="ExternalInput").ap()
    Wqv = nc.dram_tensor("Wqv", [2, 128, 16, 1024], F32, kind="ExternalInput").ap()
    qkvb = nc.dram_tensor("qkvb", [32, 128], F32, kind="ExternalInput").ap()
    qkvbv = nc.dram_tensor("qkvbv", [1, 2048], F32, kind="ExternalInput").ap()

    x2_o = nc.dram_tensor("x2", [TOK, D], F32, kind="ExternalOutput").ap()
    qt_o = nc.dram_tensor("qt", [2048, TOK], BF16, kind="ExternalOutput").ap()
    kc_o = nc.dram_tensor("kc", [2048, TOK], BF16, kind="ExternalOutput").ap()
    vc_o = nc.dram_tensor("vc", [TOK, 2048], BF16, kind="ExternalOutput").ap()

    x1_d = nc.dram_tensor("x1_d", [TOK, D], F32).ap()
    xn2_d = nc.dram_tensor("xn2_d", [TOK, D], BF16).ap()
    xn3_d = nc.dram_tensor("xn3_d", [TOK, D], BF16).ap()

    with contextlib.ExitStack() as ctx:
        E = ctx.enter_context
        p = Prog(nc, ctx, SEMS)
        sb = {}
        for nm, shp, dt in (
            ("cb", [128, 4], F32), ("cbm128", [128, 1], F32),
            ("ones_f1", [1, 128], F32), ("ones_bf1", [1, 128], BF16),
            ("ones_f", [128, 128], F32),
            ("ident_f", [128, 128], F32), ("ident_b", [128, 128], BF16),
            ("fvstage", [128, 128], F32),
            ("sg_pp", [128, FC], F32), ("su_pp", [128, FC], F32),
            ("qkvb_pp", [128, 32], F32),
            ("sdb", [128, D], F32), ("WB", [128, 1024], F32),
            ("lnscr", [128, D], F32),
        ):
            sb[nm] = E(nc.sbuf_tensor(nm, shp, dt))
        for nm in ("lnss", "lnsum", "lnm", "lnm2", "lnvar", "lnrs"):
            sb[nm] = [E(nc.sbuf_tensor(f"{nm}{i}", [128, 1], F32)) for i in range(2)]
        sb["lnxn"] = [E(nc.sbuf_tensor(f"lnxn{i}", [128, D], BF16)) for i in range(2)]
        sb["lnx"] = [E(nc.sbuf_tensor(f"lnx{i}", [128, D], F32)) for i in range(2)]
        sb["S_in_bf"] = E(nc.sbuf_tensor("S_in_bf", [128, 16, 256], BF16))
        S_in_bf = sb["S_in_bf"]

        block = WBlock(p, E(nc.Block()))
        emit_setup_consts2(p, block, sb)

        with contextlib.ExitStack() as p0:
            P0 = p0.enter_context
            sb["fvpsum"] = P0(nc.psum_tensor("fvpsum", [128, 128], F32))
            sb["bcps"] = P0(nc.psum_tensor("bcps", [128, 512], F32))
            sb["sd_row"] = P0(nc.sbuf_tensor("sd_row", [1, D], F32))
            sb["wn_row"] = P0(nc.sbuf_tensor("wn_row", [1, 1024], F32))
            S_in = P0(nc.sbuf_tensor("S_in", [128, 16, 256], F32))
            psl_s = [P0(nc.sbuf_tensor(f"psl_s{j}", [128, 16, 256], F32)) for j in range(3)]
            pdt_s = [P0(nc.sbuf_tensor(f"pdt_s{j}", [128, 16], F32)) for j in range(3)]

            emit_feature_vec(p, block, sb, "sg_pp", sgr, FC)
            emit_feature_vec(p, block, sb, "su_pp", sur, FC)
            emit_feature_vec(p, block, sb, "qkvb_pp", qkvb, 32)

            @block.sync
            def _(s):
                i = s.dma_start(out=sb["sd_row"][:], in_=sdr)
                p.dma(i, "ds_rows")
                i = s.dma_start(out=sb["wn_row"][:], in_=wnr)
                p.dma(i, "ds_rows")
                p.set_mark("rows_ld", "ds_rows")

            emit_bcast_row(p, block, sb, "sdb", sb["sd_row"], D, "bcps",
                           ld_mark="rows_ld")
            emit_bcast_row(p, block, sb, "WB", sb["wn_row"], 1024, "bcps",
                           ld_mark="rows_ld")

            @block.sync
            def _(s):
                for j in range(3):
                    i = s.dma_start(out=psl_s[j][:], in_=psl[j].rearrange("f p v -> p f v"))
                    p.dma(i, "ld_rel")
                    i = s.dma_start(out=pdt_s[j][:], in_=pdt[j])
                    p.dma(i, "ld_rel")
                p.set_mark("rel_ld", "ld_rel")

            @block.vector
            def _(v):
                p.wait_mark(v, "rel_ld")
                for fc in range(16):
                    v.scalar_tensor_tensor(
                        S_in[:, fc, :], psl_s[2][:, fc, :], pdt_s[1][:, fc:fc + 1],
                        psl_s[1][:, fc, :], ALU.mult, ALU.add)
                    v.scalar_tensor_tensor(
                        S_in[:, fc, :], S_in[:, fc, :], pdt_s[0][:, fc:fc + 1],
                        psl_s[0][:, fc, :], ALU.mult, ALU.add)
                v.tensor_copy(S_in_bf[:], S_in[:])
                p.set_mark("sin_done", "ve")
            barrier(p, block)

        with contextlib.ExitStack() as lctx:
            L = lctx.enter_context
            xoac = L(nc.sbuf_tensor("xoac", [128, NT, 1024], F32))
            wo_b = L(nc.sbuf_tensor("wo_b", [128, 8, D], BF16))
            qgt_t = [L(nc.sbuf_tensor(f"qgt_t{i}", [128, 16, 128], BF16)) for i in range(2)]
            gt_t = [L(nc.sbuf_tensor(f"gt_t{i}", [128, 1024], BF16)) for i in range(2)]
            sgt = [L(nc.sbuf_tensor(f"sgt{i}", [128, 1024], F32)) for i in range(2)]
            oft = [L(nc.sbuf_tensor(f"oft{i}", [128, 1024], BF16)) for i in range(2)]
            ofT = [L(nc.sbuf_tensor(f"ofT{i}", [128, 8, 128], BF16)) for i in range(2)]
            x_t = [L(nc.sbuf_tensor(f"x_t{i}", [128, D], F32)) for i in range(2)]
            rst = [L(nc.sbuf_tensor(f"rst{i}", [128, 4], F32)) for i in range(2)]
            psC = [L(nc.psum_tensor(f"psC{i}", [128, 256], F32)) for i in range(2)]
            psT = [L(nc.psum_tensor(f"psT{i}", [128, 8, 128], BF16)) for i in range(2)]
            psW = L(nc.psum_tensor("psW", [128, D], F32))

            @block.gpsimd
            def _(g):
                i = g.dma_start(out=wo_b[:], in_=Wo)
                p.dma(i, "dg_wo")
                p.set_mark("wo_ld", "dg_wo")

            @block.sync
            def _(s):
                for tt in range(NT):
                    i = s.dma_start(out=xoac[:, tt, :],
                                    in_=oacc_i[tt * 128:(tt + 1) * 128, :])
                    p.dma(i, "ld_oacc")
                p.set_mark("oacc_ld", "ld_oacc")

            for tt in range(NT):
                buf = tt % 2
                @block.sync
                def _(s, tt=tt, buf=buf):
                    if tt >= 2:
                        p.wait_mark(s, ("corr", tt - 2))
                    i = s.dma_start(out=qgt_t[buf][:], in_=qgt_i[tt])
                    p.dma(i, "ld_qgt%d" % buf)
                    p.set_mark(("qgt_ld", tt), "ld_qgt%d" % buf)

                for h in range(H):
                    @block.tensor
                    def _(t, tt=tt, buf=buf, h=h):
                        p.wait_mark(t, ("qgt_ld", tt))
                        if tt == 0 and h == 0:
                            p.wait_mark(t, "sin_done")
                        hb = (tt * H + h) % 2
                        if tt * H + h >= 2:
                            p.wait_mark(t, ("cadd", tt * H + h - 2))
                        for dkc in range(4):
                            i = t.matmul(psC[hb][:],
                                         qgt_t[buf][:, h * 4 + dkc, :],
                                         S_in_bf[:, h * 4 + dkc, :],
                                         start=(dkc == 0), stop=(dkc == 3))
                        p.inc(i, "pe")
                        p.set_mark(("cmm", tt * H + h), "pe")

                    @block.vector
                    def _(v, tt=tt, h=h):
                        p.wait_mark(v, ("cmm", tt * H + h))
                        if tt == 0 and h == 0:
                            p.wait_mark(v, "oacc_ld")
                        hb = (tt * H + h) % 2
                        v.tensor_add(xoac[:, tt, h * 256:(h + 1) * 256],
                                     xoac[:, tt, h * 256:(h + 1) * 256],
                                     psC[hb][:])
                        p.set_mark(("cadd", tt * H + h), "ve")
                        if h == H - 1:
                            p.set_mark(("corr", tt), "ve")

            for step in range(NT + 1):
                if step < NT:
                    tt = step
                    buf = tt % 2
                    @block.sync
                    def _(s, tt=tt, buf=buf):
                        if tt >= 2:
                            p.wait_mark(s, ("x1st", tt - 2))
                        i = s.dma_start(out=x_t[buf][:],
                                        in_=xsh[tt * 128:(tt + 1) * 128, :])
                        p.dma(i, "ld_xt%d" % buf)
                        p.set_mark(("x_ld", tt), "ld_xt%d" % buf)

                    @block.sync
                    def _(s, tt=tt, buf=buf):
                        if tt >= 2:
                            p.wait_mark(s, ("epi_g", tt - 2))
                        i = s.dma_start(out=gt_t[buf][:],
                                        in_=gtok_i[tt * 128:(tt + 1) * 128, :])
                        p.dma(i, "ld_gt%d" % buf)
                        p.set_mark(("gt_ld", tt), "ld_gt%d" % buf)

                    @block.scalar
                    def _(sc, tt=tt, buf=buf):
                        p.wait_mark(sc, ("gt_ld", tt))
                        if tt >= 2:
                            p.wait_mark(sc, ("epi_of", tt - 2))
                        sc.activation(sgt[buf][:], gt_t[buf][:], AF.Silu)
                        p.set_mark(("epi_g", tt), "ac")

                    @block.vector
                    def _(v, tt=tt, buf=buf):
                        p.wait_mark(v, ("corr", tt))
                        if tt >= 2:
                            p.wait_mark(v, ("epi_rs", tt - 2))
                        v.tensor_mul(sb["lnscr"][:, 0:1024], xoac[:, tt, :], xoac[:, tt, :])
                        for h in range(H):
                            v.reduce_sum(rst[buf][:, h:h + 1],
                                         sb["lnscr"][:, h * 256:(h + 1) * 256],
                                         mybir.AxisListType.X)
                        p.set_mark(("epi_sq", tt), "ve")

                    @block.scalar
                    def _(sc, tt=tt, buf=buf):
                        p.wait_mark(sc, ("epi_sq", tt))
                        for h in range(H):
                            sc.activation(rst[buf][:, h:h + 1], rst[buf][:, h:h + 1],
                                          AF.Ln, scale=1.0 / 256.0, bias=sb["cb"][:, 0:1])
                            sc.activation(rst[buf][:, h:h + 1], rst[buf][:, h:h + 1],
                                          AF.Exp, scale=-0.5)
                        p.set_mark(("epi_rs", tt), "ac")

                    @block.vector
                    def _(v, tt=tt, buf=buf):
                        p.wait_mark(v, ("epi_rs", tt))
                        p.wait_mark(v, ("epi_g", tt))
                        if tt >= 2:
                            p.wait_mark(v, ("ofT_mm", tt - 2))
                        for h in range(H):
                            v.scalar_tensor_tensor(
                                sgt[buf][:, h * 256:(h + 1) * 256],
                                xoac[:, tt, h * 256:(h + 1) * 256],
                                rst[buf][:, h:h + 1],
                                sgt[buf][:, h * 256:(h + 1) * 256], ALU.mult, ALU.mult)
                        i = v.tensor_mul(oft[buf][:], sgt[buf][:], sb["WB"][:])
                        p.inc(i, "ve")
                        p.set_mark(("epi_of", tt), "ve")

                    @block.tensor
                    def _(t, tt=tt, buf=buf):
                        p.wait_mark(t, ("epi_of", tt))
                        p.wait_mark(t, "consts")
                        if tt >= 2:
                            p.wait_mark(t, ("ofT_ev", tt - 2))
                        for pc in range(8):
                            i = t.transpose(psT[buf][:, pc, :],
                                            oft[buf][:, pc * 128:(pc + 1) * 128],
                                            sb["ident_b"][:])
                        p.inc(i, "pe")
                        p.set_mark(("ofT_mm", tt), "pe")

                    @block.vector
                    def _(v, tt=tt, buf=buf):
                        p.wait_mark(v, ("ofT_mm", tt))
                        if tt >= 2:
                            p.wait_mark(v, ("wo_mm", tt - 2, 3))
                        i = v.tensor_copy(ofT[buf][:], psT[buf][:])
                        p.inc(i, "ve")
                        p.set_mark(("ofT_ev", tt), "ve")

                if step >= 1:
                    tt = step - 1
                    buf = tt % 2
                    @block.tensor
                    def _(t, tt=tt, buf=buf):
                        p.wait_mark(t, ("ofT_ev", tt))
                        p.wait_mark(t, "wo_ld")
                        for ft in range(4):
                            if tt >= 1:
                                p.wait_mark(t, ("x1add", tt - 1, ft))
                            for pc in range(8):
                                i = t.matmul(psW[:, ft * 512:(ft + 1) * 512],
                                             ofT[buf][:, pc, :],
                                             wo_b[:, pc, ft * 512:(ft + 1) * 512],
                                             start=(pc == 0), stop=(pc == 7))
                            p.inc(i, "pe")
                            p.set_mark(("wo_mm", tt, ft), "pe")

                    @block.vector
                    def _(v, tt=tt, buf=buf):
                        p.wait_mark(v, ("x_ld", tt))
                        for ft in range(4):
                            p.wait_mark(v, ("wo_mm", tt, ft))
                            i = v.tensor_add(x_t[buf][:, ft * 512:(ft + 1) * 512],
                                             x_t[buf][:, ft * 512:(ft + 1) * 512],
                                             psW[:, ft * 512:(ft + 1) * 512])
                            p.inc(i, "ve")
                            p.set_mark(("x1add", tt, ft), "ve")
                        p.set_mark(("x1add", tt), "ve")

                    @block.sync
                    def _(s, tt=tt, buf=buf):
                        p.wait_mark(s, ("x1add", tt))
                        i = s.dma_start(out=x1_d[tt * 128:(tt + 1) * 128, :],
                                        in_=x_t[buf][:])
                        p.dma(i, "st_x1%d" % buf)
                        p.set_mark(("x1st", tt), "st_x1%d" % buf)

            emit_ln_phase(p, block, sb, x1_d, xn2_d, dep=lambda t: ("x1st", t), queue="gpsimd")
            barrier(p, block)



        emit_ffn(p, block, nc, sb, xn2_d, x1_d, x2_o, xn3_d, wgT, wuT, wdT, "f1")

        with contextlib.ExitStack() as lctx:
            L = lctx.enter_context
            xn3T = L(nc.sbuf_tensor("xn3T", [128, DC, TOK], BF16))
            wqk_t = [L(nc.sbuf_tensor(f"wqk_t{i}", [128, DC, 128], BF16)) for i in range(2)]
            qk_s = [L(nc.sbuf_tensor(f"qk_s{i}", [128, TOK], BF16)) for i in range(2)]
            psQ = [L(nc.psum_tensor(f"psQ{i}", [128, TOK], F32)) for i in range(2)]

            @block.sync
            def _(s):
                for nm in list(p.sem):
                    if nm.startswith(("st_l2xn", "st_xf1")):
                        p.wait(s, nm)
                for dc in range(DC):
                    i = s.dma_start_transpose(
                        xn3T[:, dc, :], xn3_d[:, dc * 128:(dc + 1) * 128])
                    p.dma(i, "ds_xn3t")
                p.set_mark("xn3T_ld", "ds_xn3t")

            for fc in range(32):
                buf = fc % 2
                @block.gpsimd
                def _(g, fc=fc, buf=buf):
                    if fc >= 2:
                        p.wait_mark(g, ("qkmm", fc - 2))
                    i = g.dma_start(out=wqk_t[buf][:], in_=WqkvT[fc])
                    p.dma(i, "dg_wqkv%d" % buf)
                    p.set_mark(("wqkv_ld", fc), "dg_wqkv%d" % buf)

                @block.tensor
                def _(t, fc=fc, buf=buf):
                    p.wait_mark(t, ("wqkv_ld", fc))
                    p.wait_mark(t, "xn3T_ld")
                    if fc >= 2:
                        p.wait_mark(t, ("qkev", fc - 2))
                    for dc in range(DC):
                        for tt in range(4):
                            i = t.matmul(psQ[buf][:, tt * 512:(tt + 1) * 512],
                                         wqk_t[buf][:, dc, :],
                                         xn3T[:, dc, tt * 512:(tt + 1) * 512],
                                         start=(dc == 0), stop=(dc == DC - 1))
                    p.inc(i, "pe")
                    p.set_mark(("qkmm", fc), "pe")

                @block.vector
                def _(v, fc=fc, buf=buf):
                    p.wait_mark(v, ("qkmm", fc))
                    if fc >= 2:
                        p.wait_mark(v, ("qkst", fc - 2))
                    v.tensor_scalar_add(qk_s[buf][:], psQ[buf][:],
                                        sb["qkvb_pp"][:, fc:fc + 1])
                    p.set_mark(("qkev", fc), "ve")

                @block.sync
                def _(s, fc=fc, buf=buf):
                    p.wait_mark(s, ("qkev", fc))
                    dst = qt_o if fc < 16 else kc_o
                    r = (fc % 16) * 128
                    i = s.dma_start(out=dst[r:r + 128, :], in_=qk_s[buf][:])
                    p.dma(i, "st_qk%d" % buf)
                    p.set_mark(("qkst", fc), "st_qk%d" % buf)

            wqv_b = L(nc.sbuf_tensor("wqv_b", [128, DC, 1024], BF16))
            vb_row = L(nc.sbuf_tensor("vb_row", [1, 2048], BF16))
            v_s = [L(nc.sbuf_tensor(f"v2_s{i}", [128, 1024], BF16)) for i in range(2)]

            for vh in range(2):
                @block.gpsimd
                def _(g, vh=vh):
                    if vh > 0:
                        for tt in range(NT - 2, NT):
                            p.wait_mark(g, ("vmm", vh - 1, tt))
                    i = g.dma_start(out=wqv_b[:], in_=Wqv[vh])
                    p.dma(i, "dg_wqv%d" % vh)
                    if vh == 0:
                        i = g.dma_start(out=vb_row[:], in_=qkvbv)
                        p.dma(i, "dg_wqv%d" % vh)
                    p.set_mark(("wqv_ld", vh), "dg_wqv%d" % vh)

                for tt in range(NT):
                    buf = tt % 2
                    gi = vh * NT + tt
                    @block.tensor
                    def _(t, tt=tt, buf=buf, vh=vh, gi=gi):
                        p.wait_mark(t, ("wqv_ld", vh))
                        p.wait_mark(t, "xn3T_ld")
                        if gi < 2:
                            p.wait_mark(t, ("qkev", 30 + gi))
                        elif tt >= 2:
                            p.wait_mark(t, ("vev", vh, tt - 2))
                        else:
                            p.wait_mark(t, ("vev", vh - 1, NT - 2 + tt))
                        for vt in range(2):
                            i = t.matmul(psQ[buf][:, vt * 512:(vt + 1) * 512],
                                         sb["ones_bf1"][:],
                                         vb_row[:, vh * 1024 + vt * 512:
                                                vh * 1024 + (vt + 1) * 512],
                                         start=True, stop=False)
                        for dc in range(DC):
                            for vt in range(2):
                                i = t.matmul(psQ[buf][:, vt * 512:(vt + 1) * 512],
                                             xn3T[:, dc, tt * 128:(tt + 1) * 128],
                                             wqv_b[:, dc, vt * 512:(vt + 1) * 512],
                                             start=False, stop=(dc == DC - 1))
                        p.inc(i, "pe")
                        p.set_mark(("vmm", vh, tt), "pe")

                    @block.vector
                    def _(v, tt=tt, buf=buf, vh=vh):
                        p.wait_mark(v, ("vmm", vh, tt))
                        gi = vh * NT + tt
                        if gi >= 2:
                            if tt >= 2:
                                p.wait_mark(v, ("vst", vh, tt - 2))
                            else:
                                p.wait_mark(v, ("vst", vh - 1, NT - 2 + tt))
                        i = v.tensor_copy(v_s[buf][:], psQ[buf][:, 0:1024])
                        p.inc(i, "ve")
                        p.set_mark(("vev", vh, tt), "ve")

                    @block.sync
                    def _(s, tt=tt, buf=buf, vh=vh):
                        p.wait_mark(s, ("vev", vh, tt))
                        i = s.dma_start(
                            out=vc_o[tt * 128:(tt + 1) * 128,
                                     vh * 1024:(vh + 1) * 1024],
                            in_=v_s[buf][:])
                        p.dma(i, "st_v%d" % buf)
                        p.set_mark(("vst", vh, tt), "st_v%d" % buf)

            @block.sync
            def _(s):
                for nm in list(p.sem):
                    if nm.startswith("st_"):
                        p.wait(s, nm)
            block.flush()
    return nc



def build_launch3():
    nc = bass.Bass("TRN2", target_bir_lowering=False)
    x2_i = nc.dram_tensor("x2", [TOK, D], F32, kind="ExternalInput").ap()
    qt_i = nc.dram_tensor("qt", [2048, TOK], BF16, kind="ExternalInput").ap()
    kgb = nc.dram_tensor("kgb", [HB, 128, 8192], BF16, kind="ExternalInput").ap()
    vgb = nc.dram_tensor("vgb", [HB, 128, 8192], BF16, kind="ExternalInput").ap()
    outwT = nc.dram_tensor("outwT", [128, 16, D], F32, kind="ExternalInput").ap()
    outbr = nc.dram_tensor("outbr", [1, D], F32, kind="ExternalInput").ap()
    wgT = nc.dram_tensor("wgT", [FC, 128, 16, 128], U8, kind="ExternalInput").ap()
    wuT = nc.dram_tensor("wuT", [FC, 128, 16, 128], U8, kind="ExternalInput").ap()
    wdT = nc.dram_tensor("wdT", [FFD, D], U8, kind="ExternalInput").ap()
    sgr = nc.dram_tensor("sgr", [FC, 128], F32, kind="ExternalInput").ap()
    sur = nc.dram_tensor("sur", [FC, 128], F32, kind="ExternalInput").ap()
    sdr = nc.dram_tensor("sdr", [1, D], F32, kind="ExternalInput").ap()

    xout_o = nc.dram_tensor("xout", [TOK, D], F32, kind="ExternalOutput").ap()

    x3_d = nc.dram_tensor("x3_d", [TOK, D], F32).ap()
    xn4_d = nc.dram_tensor("xn4_d", [TOK, D], BF16).ap()

    ATT_SCALE = 1.0 / math.sqrt(HD)

    with contextlib.ExitStack() as ctx:
        E = ctx.enter_context
        p = Prog(nc, ctx, SEMS)
        sb = {}
        for nm, shp, dt in (
            ("cb", [128, 4], F32), ("cbm128", [128, 1], F32),
            ("cbsh", [128, 1], F32),
            ("ones_f1", [1, 128], F32), ("ones_bf1", [1, 128], BF16),
            ("ones_f", [128, 128], F32), ("ones_b", [128, 128], BF16),
            ("ident_f", [128, 128], F32), ("ident_b", [128, 128], BF16),
            ("fvstage", [128, 128], F32),
            ("sg_pp", [128, FC], F32), ("su_pp", [128, FC], F32),
            ("sdb", [128, D], F32),
            ("sd_row", [1, D], F32),
            ("lnscr", [128, D], F32),
        ):
            sb[nm] = E(nc.sbuf_tensor(nm, shp, dt))
        for nm in ("lnss", "lnsum", "lnm", "lnm2", "lnvar", "lnrs"):
            sb[nm] = [E(nc.sbuf_tensor(f"{nm}{i}", [128, 1], F32)) for i in range(2)]
        sb["lnxn"] = [E(nc.sbuf_tensor(f"lnxn{i}", [128, D], BF16)) for i in range(2)]
        sb["lnx"] = [E(nc.sbuf_tensor(f"lnx{i}", [128, D], F32)) for i in range(2)]

        block = WBlock(p, E(nc.Block()))
        emit_setup_consts2(p, block, sb)

        @block.gpsimd
        def _(g):
            g.memset(sb["cbsh"][:], ATT_SHIFT)

        with nc.psum_tensor("fvpsum", [128, 128], F32) as fvps, \
             nc.psum_tensor("bcps", [128, 512], F32) as bcps:
            sb["fvpsum"] = fvps
            sb["bcps"] = bcps
            emit_feature_vec(p, block, sb, "sg_pp", sgr, FC)
            emit_feature_vec(p, block, sb, "su_pp", sur, FC)

            @block.sync
            def _(s):
                i = s.dma_start(out=sb["sd_row"][:], in_=sdr)
                p.dma(i, "ds_rows")
                p.set_mark("rows_ld", "ds_rows")

            emit_bcast_row(p, block, sb, "sdb", sb["sd_row"], D, "bcps",
                           ld_mark="rows_ld")
            barrier(p, block)

        ctxT_cm = nc.sbuf_tensor("ctxT", [128, HB, TOK], BF16)
        ctxT = ctxT_cm.__enter__()
        with contextlib.ExitStack() as lctx:
            L = lctx.enter_context
            kt_h = [L(nc.sbuf_tensor(f"kt_h{i}", [128, 8192], BF16)) for i in range(2)]
            v_h = [L(nc.sbuf_tensor(f"v_h{i}", [128, 64, 128], BF16)) for i in range(2)]
            qt_h = [L(nc.sbuf_tensor(f"qt_h{i}", [128, TOK], BF16)) for i in range(2)]
            ptl = [L(nc.sbuf_tensor(f"ptl{i}", [128, 1024], BF16)) for i in range(4)]
            recip = L(nc.sbuf_tensor("recip", [128, TOK], F32))
            dnm = recip
            dnmG = L(nc.sbuf_tensor("dnmG", [128, 1024], F32))
            psS = [L(nc.psum_tensor(f"psS{i}", [128, 1024], F32)) for i in range(2)]
            psO = L(nc.psum_tensor("psO", [128, 1024], F32))
            psD = L(nc.psum_tensor("psD", [128, 1024], F32))
            XP, XG = 128, 640

            def emit_bcast(hh, pr):
                @block.tensor
                def _(t, hh=hh, pr=pr):
                    gbase = (hh * 2 + pr) * 64
                    q0 = pr * 1024
                    g0 = pr * 512
                    p.wait_mark(t, ("dnmadd", gbase + 63))
                    p.wait_mark(t, ("dnmaddg", gbase + 62))
                    if hh * 2 + pr >= 1:
                        p.wait_mark(t, ("recipp", hh * 2 + pr - 1))
                    i = t.matmul(psD[:, XP:512], sb["ones_f"][:],
                                 dnm[:, q0 + XP:q0 + 512], start=True, stop=False)
                    i = t.matmul(psD[:, XP:512], sb["ones_f"][:],
                                 dnmG[:, g0:g0 + (512 - XP)], start=False, stop=True)
                    i = t.matmul(psD[:, 512:XG], sb["ones_f"][:],
                                 dnm[:, q0 + 512:q0 + XG], start=True, stop=False)
                    i = t.matmul(psD[:, 512:XG], sb["ones_f"][:],
                                 dnmG[:, g0 + (512 - XP):g0 + (XG - XP)],
                                 start=False, stop=True)
                    i = t.matmul(psD[:, XG:1024], sb["ones_f"][:],
                                 dnm[:, q0 + XG:q0 + 1024], start=True, stop=True)
                    p.inc(i, "pe")
                    p.set_mark(("dbc2", hh, pr), "pe")

            def emit_recipp(hh, pr):
                @block.scalar
                def _(sc, hh=hh, pr=pr):
                    q0 = pr * 1024
                    p.wait_mark(sc, ("dbc2", hh, pr))
                    sc.activation(recip[:, q0:q0 + 1024], psD[:], AF.Ln)
                    sc.activation(recip[:, q0:q0 + 1024], recip[:, q0:q0 + 1024],
                                  AF.Exp, scale=-1.0)
                    p.set_mark(("recipp", hh * 2 + pr), "ac")

            def emit_ctxnorm(hh, pr):
                @block.vector
                def _(v, hh=hh, pr=pr):
                    q0 = pr * 1024
                    p.wait_mark(v, ("recipp", hh * 2 + pr))
                    p.wait_mark(v, ("oev", hh * 2 + pr))
                    i = v.tensor_mul(ctxT[:, hh, q0:q0 + 1024],
                                     ctxT[:, hh, q0:q0 + 1024],
                                     recip[:, q0:q0 + 1024])
                    p.inc(i, "ve")
                    if pr == 1:
                        p.set_mark(("ctx_norm", hh), "ve")

            for h in range(HB):
                hb = h % 2
                @block.sync
                def _(s, h=h, hb=hb):
                    if h >= 2:
                        p.wait_mark(s, ("ctx_norm", h - 2))
                    i = s.dma_start(out=kt_h[hb][:], in_=kgb[h])
                    p.dma(i, "ld_kt%d" % hb)
                    i = s.dma_start(out=v_h[hb][:],
                                    in_=vgb[h].rearrange("p (kt f) -> p kt f", f=128))
                    p.dma(i, "ld_kt%d" % hb)
                    i = s.dma_start(out=qt_h[hb][:], in_=qt_i[h * 128:(h + 1) * 128, :])
                    p.dma(i, "ld_kt%d" % hb)
                    p.set_mark(("att_ld", h), "ld_kt%d" % hb)

                if h >= 1:
                    emit_bcast(h - 1, 1)

                for pair in range(2):
                    q0 = pair * 1024
                    for kt in range(66):
                        ib = kt % 2
                        gi = (h * 2 + pair) * 64 + kt
                        pb = gi % 4
                        if pair == 0 and kt == 2 and h >= 1:
                            emit_recipp(h - 1, 1)
                        if pair == 0 and kt == 3 and h >= 1:
                            emit_ctxnorm(h - 1, 1)
                        if pair == 1 and kt == 1:
                            emit_bcast(h, 0)
                        if pair == 1 and kt == 2:
                            emit_recipp(h, 0)
                        if pair == 1 and kt == 3:
                            emit_ctxnorm(h, 0)

                        if kt < 64:
                            @block.tensor
                            def _(t, h=h, hb=hb, pair=pair, kt=kt, ib=ib, gi=gi, q0=q0):
                                p.wait_mark(t, ("att_ld", h))
                                if gi >= 2:
                                    p.wait_mark(t, ("exp", gi - 2))
                                for qb in range(2):
                                    i = t.matmul(
                                        psS[ib][:, qb * 512:(qb + 1) * 512],
                                        kt_h[hb][:, kt * 128:(kt + 1) * 128],
                                        qt_h[hb][:, q0 + qb * 512:q0 + (qb + 1) * 512],
                                        start=True, stop=True)
                                p.inc(i, "pe")
                                p.set_mark(("smm", gi), "pe")

                        if 1 <= kt <= 64:
                            kt_o = kt - 1
                            gi_o = gi - 1
                            pb_o = gi_o % 4
                            @block.tensor
                            def _(t, h=h, hb=hb, pair=pair, kt_o=kt_o, gi_o=gi_o, pb_o=pb_o):
                                p.wait_mark(t, ("exp", gi_o))
                                if kt_o == 0 and (h > 0 or pair > 0):
                                    p.wait_mark(t, ("oev", h * 2 + pair - 1))
                                for qb in range(2):
                                    i = t.matmul(psO[:, qb * 512:(qb + 1) * 512],
                                                 v_h[hb][:, kt_o, :],
                                                 ptl[pb_o][:, qb * 512:(qb + 1) * 512],
                                                 start=(kt_o == 0), stop=(kt_o == 63))
                                p.inc(i, "pe")
                                p.set_mark(("omm", gi_o), "pe")

                        if kt >= 2:
                            kt_o2 = kt - 2
                            gi_o2 = gi - 2
                            pb_o2 = gi_o2 % 4
                            @block.tensor
                            def _(t, h=h, pair=pair, kt_o2=kt_o2, gi_o2=gi_o2, pb_o2=pb_o2):
                                if kt_o2 == 0 and (h > 0 or pair > 0):
                                    p.wait_mark(t, ("recipp", h * 2 + pair - 1))
                                i = t.matmul(psD[:, 0:XP], sb["ones_b"][:],
                                             ptl[pb_o2][:, 0:XP],
                                             start=(kt_o2 == 0), stop=(kt_o2 == 63))
                                p.inc(i, "pe")
                                p.set_mark(("dmm", gi_o2), "pe")

                        if kt < 64:
                            @block.scalar
                            def _(sc, gi=gi, ib=ib, pb=pb):
                                p.wait_mark(sc, ("smm", gi))
                                if gi >= 4:
                                    p.wait_mark(sc, ("dmm", gi - 4))
                                    p.wait_mark(sc, ("dnmadd", gi - 4))
                                    p.wait_mark(sc, ("dnmaddg", (gi - 4) // 2 * 2))
                                sc.activation(ptl[pb][:], psS[ib][:], AF.Exp,
                                              scale=ATT_SCALE, bias=sb["cbsh"][:, 0:1])
                                p.set_mark(("exp", gi), "ac")

                            @block.vector
                            def _(v, h=h, pair=pair, kt=kt, gi=gi, pb=pb, q0=q0):
                                p.wait_mark(v, ("exp", gi))
                                if kt == 0:
                                    i = v.tensor_copy(dnm[:, q0 + XG:q0 + 1024],
                                                      ptl[pb][:, XG:1024])
                                elif kt == 1:
                                    v.tensor_copy(dnm[:, q0 + XP:q0 + XG],
                                                  ptl[pb][:, XP:XG])
                                    i = v.tensor_add(dnm[:, q0 + XG:q0 + 1024],
                                                     dnm[:, q0 + XG:q0 + 1024],
                                                     ptl[pb][:, XG:1024])
                                else:
                                    c0 = XG if gi % 2 == 0 else XP
                                    i = v.tensor_add(dnm[:, q0 + c0:q0 + 1024],
                                                     dnm[:, q0 + c0:q0 + 1024],
                                                     ptl[pb][:, c0:1024])
                                p.inc(i, "ve")
                                p.set_mark(("dnmadd", gi), "ve")

                            if kt % 2 == 0:
                                @block.gpsimd
                                def _(g, h=h, pair=pair, kt=kt, gi=gi, pb=pb):
                                    p.wait_mark(g, ("exp", gi))
                                    g0 = pair * 512
                                    if kt == 0:
                                        if h >= 1:
                                            p.wait_mark(g, ("dbc2", h - 1, pair))
                                        i = g.tensor_copy(
                                            dnmG[:, g0:g0 + (XG - XP)],
                                            ptl[pb][:, XP:XG])
                                    else:
                                        i = g.tensor_add(
                                            dnmG[:, g0:g0 + (XG - XP)],
                                            dnmG[:, g0:g0 + (XG - XP)],
                                            ptl[pb][:, XP:XG])
                                    p.inc(i, "ge")
                                    p.set_mark(("dnmaddg", gi), "ge")

                    @block.vector
                    def _(v, h=h, pair=pair, q0=q0):
                        p.wait_mark(v, ("omm", (h * 2 + pair) * 64 + 63))
                        i = v.tensor_copy(ctxT[:, h, q0:q0 + 1024], psO[:])
                        p.inc(i, "ve")
                        p.set_mark(("oev", h * 2 + pair), "ve")

            emit_bcast(HB - 1, 1)
            emit_recipp(HB - 1, 1)
            emit_ctxnorm(HB - 1, 1)
            barrier(p, block)

        with contextlib.ExitStack() as lctx:
            L = lctx.enter_context
            outw_b = L(nc.sbuf_tensor("outw_b", [128, DC, D], BF16))
            ob_row = L(nc.sbuf_tensor("ob_row", [1, D], BF16))
            x_t = [L(nc.sbuf_tensor(f"x_t{i}", [128, D], F32)) for i in range(2)]
            psP = [L(nc.psum_tensor(f"psP{i}", [128, D], F32)) for i in range(2)]

            @block.gpsimd
            def _(g):
                i = g.dma_start(out=outw_b[:], in_=outwT)
                p.dma(i, "dg_outw")
                i = g.dma_start(out=ob_row[:], in_=outbr)
                p.dma(i, "dg_outw")
                p.set_mark("outw_ld", "dg_outw")

            for tt in range(NT):
                buf = tt % 2
                @block.sync
                def _(s, tt=tt, buf=buf):
                    if tt >= 2:
                        p.wait_mark(s, ("x3st", tt - 2))
                    i = s.dma_start(out=x_t[buf][:],
                                    in_=x2_i[tt * 128:(tt + 1) * 128, :])
                    p.dma(i, "ld_x2%d" % buf)
                    p.set_mark(("x2_ld", tt), "ld_x2%d" % buf)

                @block.tensor
                def _(t, tt=tt, buf=buf):
                    p.wait_mark(t, "outw_ld")
                    if tt >= 2:
                        p.wait_mark(t, ("x3add", tt - 2))
                    for ft in range(4):
                        i = t.matmul(psP[buf][:, ft * 512:(ft + 1) * 512],
                                     sb["ones_bf1"][:],
                                     ob_row[:, ft * 512:(ft + 1) * 512],
                                     start=True, stop=False)
                    for pc in range(DC):
                        for ft in range(4):
                            i = t.matmul(psP[buf][:, ft * 512:(ft + 1) * 512],
                                         ctxT[:, pc, tt * 128:(tt + 1) * 128],
                                         outw_b[:, pc, ft * 512:(ft + 1) * 512],
                                         start=False, stop=(pc == DC - 1))
                    p.inc(i, "pe")
                    p.set_mark(("op_mm", tt), "pe")

                @block.vector
                def _(v, tt=tt, buf=buf):
                    p.wait_mark(v, ("op_mm", tt))
                    p.wait_mark(v, ("x2_ld", tt))
                    i = v.tensor_add(x_t[buf][:], x_t[buf][:], psP[buf][:])
                    p.inc(i, "ve")
                    p.set_mark(("x3add", tt), "ve")

                @block.sync
                def _(s, tt=tt, buf=buf):
                    p.wait_mark(s, ("x3add", tt))
                    i = s.dma_start(out=x3_d[tt * 128:(tt + 1) * 128, :], in_=x_t[buf][:])
                    p.dma(i, "st_x3%d" % buf)
                    p.set_mark(("x3st", tt), "st_x3%d" % buf)
            emit_ln_phase(p, block, sb, x3_d, xn4_d, dep=lambda t: ("x3st", t), queue="gpsimd")
            barrier(p, block)

        ctxT_cm.__exit__(None, None, None)

        emit_ffn(p, block, nc, sb, xn4_d, x3_d, xout_o, None, wgT, wuT, wdT, "f2")

        @block.sync
        def _(s):
            for nm in list(p.sem):
                if nm.startswith("st_"):
                    p.wait(s, nm)
        block.flush()
    return nc



_cache = {}


def _get(name, builder):
    if name not in _cache:
        _cache[name] = builder()
    return _cache[name]


def _l1w4(w):
    w = np.asarray(w, np.float32)
    return _c(w.reshape(16, 128, 16, 128).transpose(2, 1, 0, 3))


def _l1w3(w):
    w = np.asarray(w, np.float32)
    return _c(w.reshape(16, 128, w.shape[1]).transpose(1, 0, 2))


def run_launch1(inputs):
    nc = _get("l1", build_launch1)
    x = inputs["L0_attn_a_x"]
    in_maps = []
    for core in range(8):
        b, pos = core // 4, core % 4
        xs = np.ascontiguousarray(x[b, pos * TOK:(pos + 1) * TOK, :], dtype=np.float32)
        in_maps.append({
            "xsh": xs,
            "Wq": _l1w4(inputs["L0_attn_a_Wq"]),
            "Wk": _l1w4(inputs["L0_attn_a_Wk"]),
            "Wv": _l1w3(inputs["L0_attn_a_Wv"]),
            "Wg": _l1w3(inputs["L0_attn_a_Wg"]),
            "Wgk1": _c(np.asarray(inputs["L0_attn_a_Wgk1"])
                       .reshape(16, 128, 16).transpose(1, 0, 2), np.float32),
            "Wgk2": inputs["L0_attn_a_Wgk2"],
            "bgk2r": np.ascontiguousarray(
                inputs["L0_attn_a_bgk2"].reshape(16, 128), dtype=np.float32),
        })
    res = run_bass_kernel_spmd(nc, in_maps, core_ids=list(range(8)))
    return res.results


def _c(a, dt=None):
    a = np.asarray(a)
    if dt is not None:
        a = a.astype(dt, copy=False) if a.dtype != dt else a
    return np.ascontiguousarray(a)


def kernel(**inputs):
    f32 = np.float32
    x = _c(inputs["L0_attn_a_x"], f32)

    def _u8(a):
        a = np.ascontiguousarray(np.asarray(a))
        return a.view(np.uint8).reshape(a.shape + (4,))[..., 0]

    wgT = _c(_u8(inputs["L0_ffn_wq_gate"]).reshape(FC, 128, 16, 128)
             .transpose(0, 3, 2, 1))
    wuT = _c(_u8(inputs["L0_ffn_wq_up"]).reshape(FC, 128, 16, 128)
             .transpose(0, 3, 2, 1))
    wdT = _c(_u8(inputs["L0_ffn_wq_down"]).T)
    sgr = _c(inputs["L0_ffn_s_gate"].reshape(FC, 128), f32)
    sur = _c(inputs["L0_ffn_s_up"].reshape(FC, 128), f32)
    sdr = _c(inputs["L0_ffn_s_down"].reshape(1, D), f32)
    Wqkv_w = np.asarray(inputs["L0_attn_b_Wqkv_w"])
    WqkvT = _c(Wqkv_w.reshape(48, 128, 16, 128).transpose(0, 3, 2, 1), f32)
    Wqv = _c(Wqkv_w[4096:].reshape(2, 1024, 16, 128).transpose(0, 3, 2, 1), f32)
    qkvb = _c(inputs["L0_attn_b_Wqkv_b"][:4096].reshape(32, 128), f32)
    qkvbv = _c(inputs["L0_attn_b_Wqkv_b"][4096:].reshape(1, 2048), f32)
    outwT = _c(np.asarray(inputs["L0_attn_b_out_w"]).T.reshape(16, 128, D)
               .transpose(1, 0, 2), f32)
    outbr = _c(inputs["L0_attn_b_out_b"].reshape(1, D), f32)
    wnr = _c(np.tile(np.asarray(inputs["L0_attn_a_Wnorm"]), 4).reshape(1, 1024), f32)

    r1 = run_launch1(inputs)

    nc2 = _get("l2", build_launch2)
    zsl = np.zeros((16, 128, 256), f32)
    odt = np.ones((128, 16), f32)
    in_maps = []
    for core in range(8):
        b, pos = core // 4, core % 4
        m = {
            "xsh": _c(x[b, pos * TOK:(pos + 1) * TOK, :], f32),
            "oacc": _c(r1[core]["oacc"], f32),
            "qgt": _c(np.asarray(r1[core]["qgt"]).reshape(16, 128, 16, 128)
                      .transpose(2, 1, 0, 3)),
            "gtok": _c(r1[core]["gtok"]),
            "wnr": wnr,
            "Wo": _c(np.asarray(inputs["L0_attn_a_Wo"]).reshape(8, 128, D)
                     .transpose(1, 0, 2), f32),
            "wgT": wgT, "wuT": wuT, "wdT": wdT,
            "sgr": sgr, "sur": sur, "sdr": sdr,
            "WqkvT": WqkvT, "Wqv": Wqv, "qkvb": qkvb, "qkvbv": qkvbv,
        }
        for j in range(3):
            pp = pos - 1 - j
            if pp >= 0:
                m[f"psl{j}"] = _c(r1[b * 4 + pp]["sloc"], f32)
                m[f"pdt{j}"] = _c(r1[b * 4 + pp]["dtot"], f32)
            else:
                m[f"psl{j}"] = zsl
                m[f"pdt{j}"] = odt
        in_maps.append(m)
    res2 = run_bass_kernel_spmd(nc2, in_maps, core_ids=list(range(8)))
    r2 = res2.results

    nc3 = _get("l3", build_launch3)
    kgb_b, vgb_b = [], []
    for b in range(2):
        kfull = np.concatenate([np.asarray(r2[b * 4 + r]["kc"]) for r in range(4)], axis=1)
        kgb_b.append(_c(kfull.reshape(HB, 128, 8192)))
        vfull = np.concatenate([np.asarray(r2[b * 4 + r]["vc"]) for r in range(4)], axis=0)
        vgb_b.append(_c(vfull.reshape(64, 128, HB, 128)
                        .transpose(2, 1, 0, 3).reshape(HB, 128, 8192)))
    in_maps = []
    for core in range(8):
        b = core // 4
        in_maps.append({
            "x2": _c(r2[core]["x2"], f32),
            "qt": _c(r2[core]["qt"]),
            "kgb": kgb_b[b], "vgb": vgb_b[b],
            "outwT": outwT, "outbr": outbr,
            "wgT": wgT, "wuT": wuT, "wdT": wdT,
            "sgr": sgr, "sur": sur, "sdr": sdr,
        })
    res3 = run_bass_kernel_spmd(nc3, in_maps, core_ids=list(range(8)))
    r3 = res3.results

    out = np.empty((2, 8192, D), f32)
    for core in range(8):
        b, pos = core // 4, core % 4
        out[b, pos * TOK:(pos + 1) * TOK, :] = np.asarray(r3[core]["xout"], f32)
    return out
